# version 01
# baseline (speedup 1.0000x reference)
from contextlib import ExitStack

import numpy as np
import ml_dtypes
import concourse.bass as bass
import concourse.mybir as mybir
from concourse.bass_utils import run_bass_kernel_spmd

F32 = mybir.dt.float32
BF16 = mybir.dt.bfloat16
AX = mybir.AxisListType.X
AF = mybir.ActivationFunctionType
OP = mybir.AluOpType
BF = ml_dtypes.bfloat16

NCORES = 8
B, S, D = 4, 2048, 4096
T = B * S
TC = T // NCORES
NT = TC // 128
NH0, HD0 = 32, 128
NH1, HD1 = 64, 64
HPC = NH1 // NCORES
FPC = HPC * HD1
LN_EPS = 1e-5
RMS_EPS = 1.1920929e-07
QSCALE = float(HD0) ** -0.5
NU = 32
CH = 32
NCH = S // CH


def _alibi_slopes(n):
    start = 2.0 ** (-(2.0 ** (-(np.log2(n) - 3))))
    return np.array([start ** (i + 1) for i in range(n)], dtype=np.float64)


class Ser:

    def __init__(self):
        self.ops = []

    def op(self, eng, fn):
        self.ops.append((eng, fn, 1))

    def dma(self, fn):
        self.ops.append(("sync", fn, 16))

    def finish(self, block, chain):
        total = sum(i for _, _, i in self.ops)

        def runner_for(name):
            def runner(eng):
                cnt = 0
                for e, fn, inc in self.ops:
                    if e == name:
                        eng.wait_ge(chain, cnt)
                        fn(eng).then_inc(chain, inc)
                    cnt += inc
                if name == "gpsimd":
                    eng.wait_ge(chain, total)
            return runner

        block.sync(runner_for("sync"))
        block.vector(runner_for("vector"))
        block.scalar(runner_for("scalar"))
        block.tensor(runner_for("tensor"))
        block.gpsimd(runner_for("gpsimd"))


def _ln_ops(ser, x, xn, sq, cols, n, eps_col):
    mean, msq, negv, lnv, rstd = (cols[:, i:i + 1] for i in range(5))
    ser.op("vector", lambda e: e.reduce_sum(mean, x, AX))
    ser.op("vector", lambda e: e.tensor_scalar(mean, mean, 1.0 / n, None, OP.mult))
    ser.op("scalar", lambda e: e.activation(sq, x, AF.Square))
    ser.op("vector", lambda e: e.reduce_sum(msq, sq, AX))
    ser.op("vector", lambda e: e.tensor_scalar(msq, msq, 1.0 / n, None, OP.mult))
    ser.op("vector", lambda e: e.scalar_tensor_tensor(negv, mean, mean, msq, OP.mult, OP.subtract))
    ser.op("scalar", lambda e: e.activation(lnv, negv, AF.Ln, bias=eps_col, scale=-1.0))
    ser.op("scalar", lambda e: e.activation(rstd, lnv, AF.Exp, scale=-0.5))
    ser.op("vector", lambda e: e.tensor_scalar(xn, x, mean, rstd, OP.subtract, OP.mult))


def build_p1():
    nc = bass.Bass("TRN2", target_bir_lowering=False)
    x_in = nc.dram_tensor("x", [TC, D], F32, kind="ExternalInput").ap()
    wq_in = nc.dram_tensor("wq", [D, D], F32, kind="ExternalInput").ap()
    wkv_in = nc.dram_tensor("wkv", [D, 256], F32, kind="ExternalInput").ap()
    idf_in = nc.dram_tensor("idf", [128, 128], F32, kind="ExternalInput").ap()
    qfm_out = nc.dram_tensor("qfm", [D, TC], F32, kind="ExternalOutput").ap()
    kv_out = nc.dram_tensor("kv", [TC, 256], F32, kind="ExternalOutput").ap()

    with ExitStack() as st:
        sb = lambda name, shp, dt: st.enter_context(nc.sbuf_tensor("t_" + name, shp, dt))
        ps = lambda name, shp, dt: st.enter_context(nc.psum_tensor("p_" + name, shp, dt))
        sx = sb("sx", [128, D], F32)
        sq = sb("sq", [128, D], F32)
        cols = sb("cols", [128, 8], F32)
        ceps = sb("ceps", [128, 1], F32)
        xn = sb("xn", [128, D], F32)
        xnT = sb("xnT", [128, 32 * TC], BF16)
        idf = sb("idf", [128, 128], F32)
        wtf = sb("wtf", [128, 128], F32)
        wtb = sb("wtb", [128, 128], BF16)
        wkvf = sb("wkvf", [128, 256], F32)
        wkvb = sb("wkvb", [128, 32 * 256], BF16)
        evac = sb("evac", [128, 512], F32)
        pf0 = ps("pf0", [128, 512], F32)
        pf1 = ps("pf1", [128, 512], F32)
        pf2 = ps("pf2", [128, 512], F32)
        ptrf = pf2[:, 0:128]
        pkv = ps("pkv", [128, 256], F32)
        chain = st.enter_context(nc.semaphore("chain"))
        block = st.enter_context(nc.Block())

        ser = Ser()
        ser.dma(lambda e: e.dma_start(out=idf[:], in_=idf_in))
        ser.op("vector", lambda e: e.memset(ceps[:], LN_EPS))
        for tt in range(NT):
            ser.dma(lambda e, tt=tt: e.dma_start(out=sx[:], in_=x_in[tt * 128:(tt + 1) * 128, :]))
            _ln_ops(ser, sx[:], xn[:], sq[:], cols, D, ceps[:, 0:1])
            for fi in range(32):
                ser.op("tensor", lambda e, fi=fi: e.transpose(ptrf, xn[:, fi * 128:(fi + 1) * 128], idf[:]))
                ser.op("vector", lambda e, fi=fi, tt=tt:
                       e.tensor_copy(xnT[:, fi * TC + tt * 128: fi * TC + (tt + 1) * 128], ptrf))
        for fo in range(32):
            for fi in range(32):
                ser.dma(lambda e, fo=fo, fi=fi: e.dma_start(
                    out=wtf[:], in_=wq_in[fi * 128:(fi + 1) * 128, fo * 128:(fo + 1) * 128]))
                ser.op("vector", lambda e: e.tensor_copy(wtb[:], wtf[:]))
                for th in range(2):
                    pq = pf0 if th == 0 else pf1
                    ser.op("tensor", lambda e, fi=fi, th=th, pq=pq: e.matmul(
                        pq[:], wtb[:], xnT[:, fi * TC + th * 512: fi * TC + (th + 1) * 512],
                        start=(fi == 0), stop=(fi == 31), skip_group_check=True))
            for th in range(2):
                pq = pf0 if th == 0 else pf1
                ser.op("vector", lambda e, pq=pq: e.tensor_copy(evac[:], pq[:]))
                ser.dma(lambda e, fo=fo, th=th: e.dma_start(
                    out=qfm_out[fo * 128:(fo + 1) * 128, th * 512:(th + 1) * 512], in_=evac[:]))
        for fi in range(32):
            ser.dma(lambda e, fi=fi: e.dma_start(out=wkvf[:], in_=wkv_in[fi * 128:(fi + 1) * 128, :]))
            ser.op("vector", lambda e, fi=fi: e.tensor_copy(wkvb[:, fi * 256:(fi + 1) * 256], wkvf[:]))
        for tt in range(NT):
            for fi in range(32):
                ser.op("tensor", lambda e, tt=tt, fi=fi: e.matmul(
                    pkv[:], xnT[:, fi * TC + tt * 128: fi * TC + (tt + 1) * 128],
                    wkvb[:, fi * 256:(fi + 1) * 256],
                    start=(fi == 0), stop=(fi == 31), skip_group_check=True))
            ser.op("vector", lambda e: e.tensor_copy(evac[:, 0:256], pkv[:]))
            ser.dma(lambda e, tt=tt: e.dma_start(
                out=kv_out[tt * 128:(tt + 1) * 128, :], in_=evac[:, 0:256]))
        ser.finish(block, chain)
    return nc


def run_p1(x_full, wq, wkv):
    nc = build_p1()
    idf = np.eye(128, dtype=np.float32)
    in_maps = [{"x": np.ascontiguousarray(x_full[c * TC:(c + 1) * TC]),
                "wq": wq, "wkv": wkv, "idf": idf} for c in range(NCORES)]
    res = run_bass_kernel_spmd(nc, in_maps, core_ids=list(range(NCORES)))
    qfm = np.concatenate([res.results[c]["qfm"] for c in range(NCORES)], axis=1)
    kv = np.concatenate([res.results[c]["kv"] for c in range(NCORES)], axis=0)
    return qfm, kv


NSLOT = 4
KT_OF_SLOT = [4 * (s + 1) for s in range(NSLOT)]
NKT = sum(KT_OF_SLOT)


def build_p2():
    nc = bass.Bass("TRN2", target_bir_lowering=False)
    qb_in = nc.dram_tensor("qb", [D, NSLOT * 256], F32, kind="ExternalInput").ap()
    kf_in = nc.dram_tensor("kf", [HD0, S], F32, kind="ExternalInput").ap()
    vt_in = nc.dram_tensor("vt", [S, HD0], F32, kind="ExternalInput").ap()
    pmjd_in = nc.dram_tensor("pmjd", [128, NKT * 256], F32, kind="ExternalInput").ap()
    ao_out = nc.dram_tensor("ao", [D, NSLOT * 256], F32, kind="ExternalOutput").ap()

    slopes = _alibi_slopes(NH0)
    with ExitStack() as st:
        sb = lambda name, shp, dt: st.enter_context(nc.sbuf_tensor("t_" + name, shp, dt))
        ps = lambda name, shp, dt: st.enter_context(nc.psum_tensor("p_" + name, shp, dt))
        kf_f = sb("kf_f", [128, S], F32)
        kf_b = sb("kf_b", [128, S], BF16)
        v_f = sb("v_f", [128, 16 * HD0], F32)
        v_b = sb("v_b", [128, 16 * HD0], BF16)
        pmjd = sb("pmjd", [128, NKT * 256], F32)
        q_f = sb("q_f", [128, NSLOT * 256], F32)
        q_b = sb("q_b", [128, NSLOT * 256], BF16)
        earg = sb("earg", [128, 256], F32)
        expT = sb("expT", [128, 16 * 256], BF16)
        ones1 = sb("ones1", [128, 1], BF16)
        onesr = sb("onesr", [1, 128], F32)
        recip = sb("recip", [1, 256], F32)
        cm16 = sb("cm16", [128, 1], F32)
        recb = sb("recb", [128, 256], F32)
        aout = sb("aout", [128, 256], F32)
        psc = ps("psc", [128, 256], F32)
        pav = ps("pav", [128, 256], F32)
        psum1 = ps("psum1", [1, 256], F32)
        prb = ps("prb", [128, 256], F32)
        chain = st.enter_context(nc.semaphore("chain"))
        block = st.enter_context(nc.Block())

        ser = Ser()
        ser.dma(lambda e: e.dma_start(out=kf_f[:], in_=kf_in))
        ser.op("vector", lambda e: e.tensor_copy(kf_b[:], kf_f[:]))
        ser.dma(lambda e: e.dma_start(
            out=v_f[:].rearrange("p (kt e) -> p kt e", kt=16),
            in_=vt_in.rearrange("(kt p) e -> p kt e", p=128)))
        ser.op("vector", lambda e: e.tensor_copy(v_b[:], v_f[:]))
        ser.dma(lambda e: e.dma_start(out=pmjd[:], in_=pmjd_in))
        ser.op("vector", lambda e: e.memset(ones1[:], 1.0))
        ser.op("vector", lambda e: e.memset(onesr[:], 1.0))
        ser.op("vector", lambda e: e.memset(cm16[:], -16.0))

        for h in range(NH0):
            sl = float(slopes[h])
            ser.dma(lambda e, h=h: e.dma_start(out=q_f[:], in_=qb_in[h * 128:(h + 1) * 128, :]))
            ser.op("vector", lambda e: e.tensor_copy(q_b[:], q_f[:]))
            kt_base = 0
            for s in range(NSLOT):
                nkt = KT_OF_SLOT[s]
                for kt in range(nkt):
                    ser.op("tensor", lambda e, kt=kt, s=s: e.matmul(
                        psc[:], kf_b[:, kt * 128:(kt + 1) * 128], q_b[:, s * 256:(s + 1) * 256],
                        start=True, stop=True, skip_group_check=True))
                    col = (kt_base + kt) * 256
                    ser.op("vector", lambda e, col=col, sl=sl: e.scalar_tensor_tensor(
                        earg[:], psc[:], QSCALE / sl, pmjd[:, col:col + 256], OP.mult, OP.add))
                    ser.op("scalar", lambda e, kt=kt, sl=sl: e.activation(
                        expT[:, kt * 256:(kt + 1) * 256], earg[:], AF.Exp, bias=cm16[:, 0:1], scale=sl))
                for kt in range(nkt):
                    ser.op("tensor", lambda e, kt=kt, nkt=nkt: e.matmul(
                        pav[:], v_b[:, kt * 128:(kt + 1) * 128], expT[:, kt * 256:(kt + 1) * 256],
                        start=(kt == 0), stop=(kt == nkt - 1), skip_group_check=True))
                for kt in range(nkt):
                    ser.op("tensor", lambda e, kt=kt, nkt=nkt: e.matmul(
                        psum1[:], ones1[:], expT[:, kt * 256:(kt + 1) * 256],
                        start=(kt == 0), stop=(kt == nkt - 1), skip_group_check=True))
                ser.op("vector", lambda e: e.reciprocal(recip[:], psum1[:]))
                ser.op("tensor", lambda e: e.matmul(
                    prb[:], onesr[:], recip[:], start=True, stop=True, skip_group_check=True))
                ser.op("vector", lambda e: e.tensor_copy(recb[:], prb[:]))
                ser.op("vector", lambda e: e.tensor_mul(aout[:], pav[:], recb[:]))
                ser.dma(lambda e, h=h, s=s: e.dma_start(
                    out=ao_out[h * 128:(h + 1) * 128, s * 256:(s + 1) * 256], in_=aout[:]))
                kt_base += nkt
        ser.finish(block, chain)
    return nc


def make_pmjd(core):
    out = np.zeros((128, NKT * 256), dtype=np.float32)
    p = np.arange(128)[:, None]
    j = np.arange(256)[None, :]
    base = 0
    for s in range(NSLOT):
        i_c = 2 * s + (core % 2)
        for kt in range(KT_OF_SLOT[s]):
            v = (p - j + 128 * (kt - 2 * i_c)).astype(np.float32)
            v[v > 0] = -1e33
            out[:, (base + kt) * 256:(base + kt + 1) * 256] = v
        base += KT_OF_SLOT[s]
    return out


def run_p2(qfm, kv):
    nc = build_p2()
    in_maps = []
    for c in range(NCORES):
        b = c // 2
        blocks = [2 * s + (c % 2) for s in range(NSLOT)]
        qcols = np.concatenate(
            [qfm[:, b * S + i * 256: b * S + (i + 1) * 256] for i in blocks], axis=1)
        kvb = kv[b * S:(b + 1) * S]
        in_maps.append({
            "qb": np.ascontiguousarray(qcols),
            "kf": np.ascontiguousarray(kvb[:, :HD0].T),
            "vt": np.ascontiguousarray(kvb[:, HD0:]),
            "pmjd": make_pmjd(c),
        })
    res = run_bass_kernel_spmd(nc, in_maps, core_ids=list(range(NCORES)))
    ao = np.zeros((D, T), dtype=np.float32)
    for c in range(NCORES):
        b = c // 2
        r = res.results[c]["ao"]
        for s in range(NSLOT):
            i = 2 * s + (c % 2)
            ao[:, b * S + i * 256: b * S + (i + 1) * 256] = r[:, s * 256:(s + 1) * 256]
    return ao


def build_p3():
    nc = bass.Bass("TRN2", target_bir_lowering=False)
    ao_in = nc.dram_tensor("ao", [D, TC], F32, kind="ExternalInput").ap()
    x_in = nc.dram_tensor("x", [TC, D], F32, kind="ExternalInput").ap()
    wo_in = nc.dram_tensor("wo", [D, D], F32, kind="ExternalInput").ap()
    rwt_in = nc.dram_tensor("rwt", [D, 4], F32, kind="ExternalInput").ap()
    at_in = nc.dram_tensor("at", [4, D, 16], F32, kind="ExternalInput").ap()
    bt_in = nc.dram_tensor("bt", [4, 16, D], F32, kind="ExternalInput").ap()
    idf_in = nc.dram_tensor("idf", [128, 128], F32, kind="ExternalInput").ap()
    x2_out = nc.dram_tensor("x2", [TC, D], F32, kind="ExternalOutput").ap()
    xl2_out = nc.dram_tensor("xl2", [D, TC], F32, kind="ExternalOutput").ap()
    x1_scr = nc.dram_tensor("x1_scr", [TC, D], F32, kind="Internal").ap()

    with ExitStack() as st:
        sb = lambda name, shp, dt: st.enter_context(nc.sbuf_tensor("t_" + name, shp, dt))
        ps = lambda name, shp, dt: st.enter_context(nc.psum_tensor("p_" + name, shp, dt))
        sta = sb("sta", [128, TC], F32)
        aob = sb("aob", [128, 32 * TC], BF16)
        ldf = sb("ldf", [128, 256], F32)
        wtb = sb("wtb", [128, 256], BF16)
        xr = sb("xr", [128, 256], F32)
        ev = sb("ev", [128, 512], F32)
        sx1 = sb("sx1", [128, D], F32)
        sx2 = sb("sx2", [128, D], F32)
        sq = sb("sq", [128, D], F32)
        cols = sb("cols", [128, 16], F32)
        ceps = sb("ceps", [128, 1], F32)
        xn = sb("xn", [128, D], F32)
        xnT = sb("xnT", [128, 32 * 128], BF16)
        idf = sb("idf", [128, 128], F32)
        rwtb = sb("rwtb", [128, 32 * 4], BF16)
        atb = [sb(f"atb{i}", [128, 32 * 16], BF16) for i in range(4)]
        btb = [sb(f"btb{i}", [16, D], BF16) for i in range(4)]
        btf = sb("btf", [16, D], F32)
        lg = sb("lg", [128, 4], F32)
        gsc = sb("gsc", [128, 16], F32)
        gates = sb("gates", [128, 4], F32)
        gT = sb("gT", [1, 4 * 128], F32)
        gbc = sb("gbc", [16, 128], F32)
        wmid4 = sb("wmid4", [16, 4 * 128], BF16)
        ones16 = sb("ones16", [1, 16], F32)
        pcc = [ps(f"pcc{i}", [128, 256], F32) for i in range(8)]
        chain = st.enter_context(nc.semaphore("chain"))
        block = st.enter_context(nc.Block())
        ptrf = pcc[7][:, 0:128]
        plg = pcc[0][:, 0:4]
        pg1 = pcc[6][0:1, 0:128]
        pgb = pcc[5][0:16, 0:128]
        pmid = pcc[4][0:16, 0:128]
        pdl = pcc[1]

        ser = Ser()
        ser.dma(lambda e: e.dma_start(out=idf[:], in_=idf_in))
        ser.op("vector", lambda e: e.memset(ones16[:], 1.0))
        ser.op("vector", lambda e: e.memset(ceps[:], LN_EPS))
        for fi in range(32):
            ser.dma(lambda e, fi=fi: e.dma_start(out=sta[:], in_=ao_in[fi * 128:(fi + 1) * 128, :]))
            ser.op("vector", lambda e, fi=fi: e.tensor_copy(aob[:, fi * TC:(fi + 1) * TC], sta[:]))
        for fog in range(8):
            for half in range(2):
                for fi in range(32):
                    co = fog * 512 + half * 256
                    ser.dma(lambda e, fi=fi, co=co: e.dma_start(
                        out=ldf[:], in_=wo_in[fi * 128:(fi + 1) * 128, co:co + 256]))
                    ser.op("vector", lambda e: e.tensor_copy(wtb[:], ldf[:]))
                    for tt in range(NT):
                        pv = pcc[tt][:]
                        ser.op("tensor", lambda e, fi=fi, tt=tt, pv=pv: e.matmul(
                            pv, aob[:, fi * TC + tt * 128: fi * TC + (tt + 1) * 128], wtb[:],
                            start=(fi == 0), stop=(fi == 31), skip_group_check=True))
                for tt in range(NT):
                    co = fog * 512 + half * 256
                    pv = pcc[tt][:]
                    ser.dma(lambda e, tt=tt, co=co: e.dma_start(
                        out=xr[:], in_=x_in[tt * 128:(tt + 1) * 128, co:co + 256]))
                    ser.op("vector", lambda e, pv=pv: e.tensor_add(ev[:, 0:256], pv, xr[:]))
                    ser.dma(lambda e, tt=tt, co=co: e.dma_start(
                        out=x1_scr[tt * 128:(tt + 1) * 128, co:co + 256], in_=ev[:, 0:256]))
        for fi in range(32):
            ser.dma(lambda e, fi=fi: e.dma_start(out=ldf[:, 0:4], in_=rwt_in[fi * 128:(fi + 1) * 128, :]))
            ser.op("vector", lambda e, fi=fi: e.tensor_copy(rwtb[:, fi * 4:(fi + 1) * 4], ldf[:, 0:4]))
        for ei in range(4):
            ser.dma(lambda e, ei=ei: e.dma_start(
                out=sq[:, 0:512].rearrange("p (f r) -> p f r", f=32),
                in_=at_in[ei].rearrange("(f p) r -> p f r", p=128)))
            ser.op("vector", lambda e, ei=ei: e.tensor_copy(atb[ei][:], sq[:, 0:512]))
            ser.dma(lambda e, ei=ei: e.dma_start(out=btf[:], in_=bt_in[ei]))
            ser.op("vector", lambda e, ei=ei: e.tensor_copy(btb[ei][:], btf[:]))
        for tt in range(NT):
            ser.dma(lambda e, tt=tt: e.dma_start(out=sx1[:], in_=x1_scr[tt * 128:(tt + 1) * 128, :]))
            _ln_ops(ser, sx1[:], xn[:], sq[:], cols, D, ceps[:, 0:1])
            for fi in range(32):
                ser.op("tensor", lambda e, fi=fi: e.transpose(ptrf, xn[:, fi * 128:(fi + 1) * 128], idf[:]))
                ser.op("vector", lambda e, fi=fi: e.tensor_copy(xnT[:, fi * 128:(fi + 1) * 128], ptrf))
            for fi in range(32):
                ser.op("tensor", lambda e, fi=fi: e.matmul(
                    plg, xnT[:, fi * 128:(fi + 1) * 128], rwtb[:, fi * 4:(fi + 1) * 4],
                    start=(fi == 0), stop=(fi == 31), skip_group_check=True))
            ser.op("vector", lambda e: e.tensor_copy(lg[:], plg))
            c0, c1, c2, c3, c4 = (cols[:, 8 + i:9 + i] for i in range(5))
            ser.op("vector", lambda e: e.reduce_max(c0, lg[:], AX))
            ser.op("vector", lambda e: e.tensor_scalar(gsc[:, 0:4], lg[:], c0, None, OP.is_equal))
            ser.op("vector", lambda e: e.scalar_tensor_tensor(gsc[:, 4:8], gsc[:, 0:4], -1e30, lg[:], OP.mult, OP.add))
            ser.op("vector", lambda e: e.reduce_max(c1, gsc[:, 4:8], AX))
            ser.op("vector", lambda e: e.tensor_scalar(gsc[:, 8:12], lg[:], c1, None, OP.is_ge))
            ser.op("vector", lambda e: e.tensor_scalar(c2, c0, -1.0, None, OP.mult))
            ser.op("scalar", lambda e: e.activation(gsc[:, 12:16], lg[:], AF.Exp, bias=c2))
            ser.op("vector", lambda e: e.tensor_mul(gsc[:, 12:16], gsc[:, 12:16], gsc[:, 8:12]))
            ser.op("vector", lambda e: e.reduce_sum(c3, gsc[:, 12:16], AX))
            ser.op("vector", lambda e: e.reciprocal(c4, c3))
            ser.op("vector", lambda e: e.tensor_scalar(gates[:], gsc[:, 12:16], c4, None, OP.mult))
            for ei in range(4):
                ser.op("tensor", lambda e, ei=ei: e.transpose(pg1, gates[:, ei:ei + 1], idf[:]))
                ser.op("vector", lambda e, ei=ei: e.tensor_copy(gT[0:1, ei * 128:(ei + 1) * 128], pg1))
            for ei in range(4):
                for fi in range(32):
                    ser.op("tensor", lambda e, ei=ei, fi=fi: e.matmul(
                        pmid, atb[ei][:, fi * 16:(fi + 1) * 16], xnT[:, fi * 128:(fi + 1) * 128],
                        start=(fi == 0), stop=(fi == 31), skip_group_check=True))
                ser.op("tensor", lambda e, ei=ei: e.matmul(
                    pgb, ones16[:], gT[0:1, ei * 128:(ei + 1) * 128],
                    start=True, stop=True, skip_group_check=True))
                ser.op("vector", lambda e: e.tensor_copy(gbc[:], pgb))
                ser.op("vector", lambda e, ei=ei: e.tensor_mul(wmid4[:, ei * 128:(ei + 1) * 128], pmid, gbc[:]))
            for og in range(16):
                for ei in range(4):
                    ser.op("tensor", lambda e, og=og, ei=ei: e.matmul(
                        pdl[:], wmid4[:, ei * 128:(ei + 1) * 128], btb[ei][:, og * 256:(og + 1) * 256],
                        start=(ei == 0), stop=(ei == 3), skip_group_check=True))
                ser.op("vector", lambda e, og=og: e.scalar_tensor_tensor(
                    sx2[:, og * 256:(og + 1) * 256], pdl[:], 4.0, sx1[:, og * 256:(og + 1) * 256],
                    OP.mult, OP.add))
                ser.dma(lambda e, tt=tt, og=og: e.dma_start(
                    out=x2_out[tt * 128:(tt + 1) * 128, og * 256:(og + 1) * 256],
                    in_=sx2[:, og * 256:(og + 1) * 256]))
            _ln_ops(ser, sx2[:], xn[:], sq[:], cols, D, ceps[:, 0:1])
            for fi in range(32):
                ser.op("tensor", lambda e, fi=fi: e.transpose(ptrf, xn[:, fi * 128:(fi + 1) * 128], idf[:]))
                ser.op("vector", lambda e: e.tensor_copy(ev[:, 0:128], ptrf))
                ser.dma(lambda e, tt=tt, fi=fi: e.dma_start(
                    out=xl2_out[fi * 128:(fi + 1) * 128, tt * 128:(tt + 1) * 128], in_=ev[:, 0:128]))
        ser.finish(block, chain)
    return nc


def run_p3(ao, x_full, wo, router_w, A, Bm):
    nc = build_p3()
    idf = np.eye(128, dtype=np.float32)
    rwt = np.ascontiguousarray(router_w.T)
    at = np.ascontiguousarray(A.transpose(0, 2, 1))
    bt = np.ascontiguousarray(Bm.transpose(0, 2, 1))
    in_maps = [{"ao": np.ascontiguousarray(ao[:, c * TC:(c + 1) * TC]),
                "x": np.ascontiguousarray(x_full[c * TC:(c + 1) * TC]),
                "wo": wo, "rwt": rwt, "at": at, "bt": bt, "idf": idf}
               for c in range(NCORES)]
    res = run_bass_kernel_spmd(nc, in_maps, core_ids=list(range(NCORES)))
    x2 = np.concatenate([res.results[c]["x2"] for c in range(NCORES)], axis=0)
    xl2 = np.concatenate([res.results[c]["xl2"] for c in range(NCORES)], axis=1)
    return x2, xl2


def build_p4():
    nc = bass.Bass("TRN2", target_bir_lowering=False)
    xl_in = nc.dram_tensor("xl", [D, TC], F32, kind="ExternalInput").ap()
    halo_in = nc.dram_tensor("halo", [D, 1], F32, kind="ExternalInput").ap()
    w_ins = [nc.dram_tensor(f"w{i}", [D, D], F32, kind="ExternalInput").ap() for i in range(4)]
    tm_in = nc.dram_tensor("tm", [D, 4], F32, kind="ExternalInput").ap()
    outs = [nc.dram_tensor(nm, [D, TC], BF16, kind="ExternalOutput").ap()
            for nm in ("kfm", "vfm", "rfm", "gfm")]

    with ExitStack() as st:
        sb = lambda name, shp, dt: st.enter_context(nc.sbuf_tensor("t_" + name, shp, dt))
        ps = lambda name, shp, dt: st.enter_context(nc.psum_tensor("p_" + name, shp, dt))
        ext = sb("ext", [128, TC + 1], F32)
        mf32 = sb("mf32", [128, TC], F32)
        mixf = sb("mixf", [128, 32 * TC], BF16)
        tmv = sb("tmv", [128, 4 * 32], F32)
        tm1 = sb("tm1", [128, 4 * 32], F32)
        wtf = sb("wtf", [128, 128], F32)
        wtb = sb("wtb", [128, 128], BF16)
        evb = sb("evb", [128, 512], BF16)
        pf0 = ps("pf0", [128, 512], F32)
        pf1 = ps("pf1", [128, 512], F32)
        chain = st.enter_context(nc.semaphore("chain"))
        block = st.enter_context(nc.Block())

        ser = Ser()
        ser.dma(lambda e: e.dma_start(
            out=tmv[:].rearrange("p (f m) -> p f m", f=32),
            in_=tm_in.rearrange("(f p) m -> p f m", p=128)))
        ser.op("vector", lambda e: e.tensor_scalar(tm1[:], tmv[:], -1.0, 1.0, OP.mult, OP.add))
        for pj in range(4):
            for fi in range(32):
                ser.dma(lambda e, fi=fi: e.dma_start(
                    out=ext[:, 1:TC + 1], in_=xl_in[fi * 128:(fi + 1) * 128, :]))
                ser.dma(lambda e, fi=fi: e.dma_start(
                    out=ext[:, 0:1], in_=halo_in[fi * 128:(fi + 1) * 128, :]))
                mcol = tmv[:, fi * 4 + pj: fi * 4 + pj + 1]
                m1col = tm1[:, fi * 4 + pj: fi * 4 + pj + 1]
                ser.op("vector", lambda e, mcol=mcol: e.tensor_scalar(
                    mf32[:], ext[:, 1:TC + 1], mcol, None, OP.mult))
                ser.op("vector", lambda e, fi=fi, m1col=m1col: e.scalar_tensor_tensor(
                    mixf[:, fi * TC:(fi + 1) * TC], ext[:, 0:TC], m1col, mf32[:], OP.mult, OP.add))
            for fo in range(32):
                for fi in range(32):
                    ser.dma(lambda e, pj=pj, fo=fo, fi=fi: e.dma_start(
                        out=wtf[:], in_=w_ins[pj][fi * 128:(fi + 1) * 128, fo * 128:(fo + 1) * 128]))
                    ser.op("vector", lambda e: e.tensor_copy(wtb[:], wtf[:]))
                    for nh in range(2):
                        pq = pf0 if nh == 0 else pf1
                        ser.op("tensor", lambda e, fi=fi, nh=nh, pq=pq: e.matmul(
                            pq[:], wtb[:], mixf[:, fi * TC + nh * 512: fi * TC + (nh + 1) * 512],
                            start=(fi == 0), stop=(fi == 31), skip_group_check=True))
                for nh in range(2):
                    pq = pf0 if nh == 0 else pf1
                    if pj == 3:
                        ser.op("scalar", lambda e, pq=pq: e.activation(evb[:], pq[:], AF.Silu))
                    else:
                        ser.op("vector", lambda e, pq=pq: e.tensor_copy(evb[:], pq[:]))
                    ser.dma(lambda e, pj=pj, fo=fo, nh=nh: e.dma_start(
                        out=outs[pj][fo * 128:(fo + 1) * 128, nh * 512:(nh + 1) * 512], in_=evb[:]))
        ser.finish(block, chain)
    return nc


def run_p4(xl2, w_key, w_value, w_receptance, w_gate, tmk, tmv_, tmr, tmg):
    nc = build_p4()
    tm = np.ascontiguousarray(np.stack([tmk, tmv_, tmr, tmg], axis=1))
    ws = [np.ascontiguousarray(w.T) for w in (w_key, w_value, w_receptance, w_gate)]
    in_maps = []
    for c in range(NCORES):
        t0 = c * TC
        halo = np.zeros((D, 1), dtype=np.float32) if t0 % S == 0 else \
            np.ascontiguousarray(xl2[:, t0 - 1: t0])
        in_maps.append({"xl": np.ascontiguousarray(xl2[:, t0:t0 + TC]), "halo": halo,
                        "w0": ws[0], "w1": ws[1], "w2": ws[2], "w3": ws[3], "tm": tm})
    res = run_bass_kernel_spmd(nc, in_maps, core_ids=list(range(NCORES)))
    mats = []
    for nm in ("kfm", "vfm", "rfm", "gfm"):
        mats.append(np.concatenate([res.results[c][nm] for c in range(NCORES)], axis=1))
    return mats


def build_p5():
    nc = bass.Bass("TRN2", target_bir_lowering=False)
    kf_in = nc.dram_tensor("kf", [FPC, T], BF16, kind="ExternalInput").ap()
    rf_in = nc.dram_tensor("rf", [FPC, T], BF16, kind="ExternalInput").ap()
    kt_in = nc.dram_tensor("kt", [T, FPC], BF16, kind="ExternalInput").ap()
    vt_in = nc.dram_tensor("vt", [T, FPC], BF16, kind="ExternalInput").ap()
    dft_in = nc.dram_tensor("dft", [4, 128], F32, kind="ExternalInput").ap()
    fcol_in = nc.dram_tensor("fcol", [128, 4], F32, kind="ExternalInput").ap()
    d8f_in = nc.dram_tensor("d8f", [HPC, 128], F32, kind="ExternalInput").ap()
    mkf_in = nc.dram_tensor("mkf", [HPC, 128], F32, kind="ExternalInput").ap()
    gnw_in = nc.dram_tensor("gnw", [HPC, HD1], F32, kind="ExternalInput").ap()
    gnb_in = nc.dram_tensor("gnb", [HPC, HD1], F32, kind="ExternalInput").ap()
    ind8p_in = nc.dram_tensor("ind8p", [HPC, 16 * HD1], F32, kind="ExternalInput").ap()
    ind8b_in = nc.dram_tensor("ind8b", [HPC, NU * CH], F32, kind="ExternalInput").ap()
    taucmp_in = nc.dram_tensor("taucmp", [1, 128], F32, kind="ExternalInput").ap()
    taud_in = nc.dram_tensor("taud", [1, 128], F32, kind="ExternalInput").ap()
    taukd_in = nc.dram_tensor("taukd", [1, 128], F32, kind="ExternalInput").ap()
    taux_in = nc.dram_tensor("taux", [1, 64], F32, kind="ExternalInput").ap()
    taukx_in = nc.dram_tensor("taukx", [1, 64], F32, kind="ExternalInput").ap()
    tauir_in = nc.dram_tensor("tauir", [1, 128], F32, kind="ExternalInput").ap()
    tau31_in = nc.dram_tensor("tau31", [1, 32], F32, kind="ExternalInput").ap()
    msu_in = nc.dram_tensor("msu", [CH, NU * CH], F32, kind="ExternalInput").ap()
    mid_in = nc.dram_tensor("mid", [CH, NU * CH], F32, kind="ExternalInput").ap()
    xn_out = nc.dram_tensor("xno", [FPC, T], BF16, kind="ExternalOutput").ap()

    kf_r = kf_in.rearrange("(ft p) (b cc tau) -> ft cc p b tau", ft=4, b=4, tau=CH)
    rf_r = rf_in.rearrange("(ft p) (b cc tau) -> ft cc p b tau", ft=4, b=4, tau=CH)
    kt_r = kt_in.rearrange("(b cc tau) f -> cc tau b f", b=4, tau=CH)
    vt_r = vt_in.rearrange("(b cc tau) f -> cc tau b f", b=4, tau=CH)
    xo_r = xn_out.rearrange("(h e) (b cc tau) -> h cc e b tau", h=HPC, b=4, tau=CH)

    with ExitStack() as st:
        sb = lambda name, shp, dt: st.enter_context(nc.sbuf_tensor("t_" + name, shp, dt))
        ps = lambda name, shp, dt: st.enter_context(nc.psum_tensor("p_" + name, shp, dt))
        s_ft = [sb(f"sft{i}", [1, 128], F32) for i in range(4)]
        fcol = sb("fcol", [128, 4], F32)
        s8f = sb("s8f", [HPC, 128], F32)
        mskf = sb("mskf", [HPC, 128], F32)
        srep = sb("srep", [1, 4 * FPC], F32)
        tard = sb("tard", [128, 512], BF16)
        takd = sb("takd", [128, 512], BF16)
        tarx = sb("tarx", [128, 256], BF16)
        takx = sb("takx", [128, 256], BF16)
        tir = sb("tir", [128, 512], BF16)
        tfb = sb("tfb", [128, 512], BF16)
        tkh = sb("tkh", [CH, 4 * FPC], BF16)
        tdec = sb("tdec", [128, 16 * HD1], F32)
        gnwb = sb("gnwb", [64, NU * CH], F32)
        gnbb = sb("gnbb", [64, NU * CH], F32)
        msu = sb("msu", [CH, NU * CH], F32)
        midm = sb("midm", [CH, NU * CH], F32)
        taus = {}
        for nm, w in (("taud", 128), ("taukd", 128), ("taux", 64), ("taukx", 64),
                      ("tauir", 128), ("tau31", 32), ("taucmp", 128)):
            taus[nm] = sb("s_" + nm, [1, w], F32)
        clampf = sb("clampf", [128, 128], F32)
        ind8p = sb("ind8p", [HPC, 16 * HD1], F32)
        ind8b = sb("ind8b", [HPC, NU * CH], F32)
        gnw8 = sb("gnw8", [HPC, HD1], F32)
        gnb8 = sb("gnb8", [HPC, HD1], F32)
        ones64 = sb("ones64", [64, 1], F32)
        ones1r = sb("ones1r", [1, 64], F32)
        kc = sb("kc", [128, 512], BF16)
        rc = sb("rc", [128, 512], BF16)
        ktc = sb("ktc", [CH, 4 * FPC], BF16)
        vtc = sb("vtc", [CH, 4 * FPC], BF16)
        kdtf = sb("kdtf", [128, 1024], BF16)
        rtd = sb("rtd", [128, 512], BF16)
        rx = sb("rx", [128, 256], BF16)
        kx = sb("kx", [128, 256], BF16)
        ri = sb("ri", [128, 512], BF16)
        kh = sb("kh", [CH, 4 * FPC], BF16)
        msb = sb("msb", [CH, NU * CH], BF16)
        clA = sb("clA", [CH, NU * CH], F32)
        msc = sb("msc", [CH, NU * CH], BF16)
        sf32 = sb("sf32", [128, 16 * HD1], F32)
        sbf = sb("sbf", [128, 16 * HD1], BF16)
        osb = sb("osb", [64, NU * CH], F32)
        osq = sb("osq", [64, NU * CH], F32)
        t1 = sb("t1", [64, NU * CH], F32)
        xnb = sb("xnb", [64, NU * CH], BF16)
        mrow = sb("mrow", [1, NU * CH], F32)
        mrow2 = sb("mrow2", [1, NU * CH], F32)
        mrow3 = sb("mrow3", [1, NU * CH], F32)
        cgn = sb("cgn", [128, 1], F32)
        PA = ps("PA", [64, 1024], F32)
        PX = ps("PX", [16, 512], F32)
        PO = ps("PO", [64, 1024], F32)
        PS = ps("PS", [128, 512], F32)
        PT = ps("PT", [128, 512], F32)
        chain = st.enter_context(nc.semaphore("chain"))
        block = st.enter_context(nc.Block())

        ser = Ser()
        for nm, inp in (("taud", taud_in), ("taukd", taukd_in), ("taux", taux_in),
                        ("taukx", taukx_in), ("tauir", tauir_in), ("tau31", tau31_in),
                        ("taucmp", taucmp_in)):
            ser.dma(lambda e, nm=nm, inp=inp: e.dma_start(out=taus[nm][:], in_=inp))
        ser.dma(lambda e: e.dma_start(out=msu[:], in_=msu_in))
        ser.dma(lambda e: e.dma_start(out=midm[:], in_=mid_in))
        ser.dma(lambda e: e.dma_start(out=ind8p[:], in_=ind8p_in))
        ser.dma(lambda e: e.dma_start(out=ind8b[:], in_=ind8b_in))
        ser.dma(lambda e: e.dma_start(out=gnw8[:], in_=gnw_in))
        ser.dma(lambda e: e.dma_start(out=gnb8[:], in_=gnb_in))
        ser.dma(lambda e: e.dma_start(out=s8f[:], in_=d8f_in))
        ser.dma(lambda e: e.dma_start(out=mskf[:], in_=mkf_in))
        ser.dma(lambda e: e.dma_start(out=fcol[:], in_=fcol_in))
        ser.op("vector", lambda e: e.memset(ones64[:], 1.0))
        ser.op("vector", lambda e: e.memset(ones1r[:], 1.0))
        ser.op("vector", lambda e: e.memset(sf32[:], 0.0))
        ser.op("vector", lambda e: e.memset(sbf[:], 0.0))
        ser.op("vector", lambda e: e.memset(cgn[:], 1e-5))
        for ft in range(4):
            ser.dma(lambda e, ft=ft: e.dma_start(out=s_ft[ft][:], in_=dft_in[ft:ft + 1, :]))
            ser.op("scalar", lambda e, ft=ft: e.activation(s_ft[ft][:], s_ft[ft][:], AF.Exp))
            ser.op("vector", lambda e, ft=ft: e.tensor_scalar(s_ft[ft][:], s_ft[ft][:], -1.0, None, OP.mult))
            ser.op("vector", lambda e, ft=ft: e.tensor_copy(srep[0:1, ft * 128:(ft + 1) * 128], s_ft[ft][:]))
        ser.op("scalar", lambda e: e.activation(s8f[:], s8f[:], AF.Exp))
        ser.op("vector", lambda e: e.tensor_scalar(s8f[:], s8f[:], -1.0, None, OP.mult))
        ser.op("vector", lambda e: e.tensor_mul(s8f[:], s8f[:], mskf[:]))
        for b_ in range(1, 4):
            ser.op("vector", lambda e, b_=b_: e.tensor_copy(
                srep[0:1, b_ * FPC:(b_ + 1) * FPC], srep[0:1, 0:FPC]))
        def mk_tab(dst, tau_nm, width, clamp, post_tf=False):
            for ft in range(4):
                ser.op("tensor", lambda e, ft=ft, tau_nm=tau_nm, width=width: e.matmul(
                    PT[:, 0:width], s_ft[ft][:], taus[tau_nm][:],
                    start=True, stop=True, skip_group_check=True))
                if clamp:
                    ser.op("vector", lambda e, width=width: e.tensor_scalar(
                        clampf[:, 0:width], PT[:, 0:width], 75.0, -75.0, OP.min, OP.max))
                else:
                    ser.op("vector", lambda e, width=width: e.tensor_copy(
                        clampf[:, 0:width], PT[:, 0:width]))
                if post_tf:
                    ser.op("scalar", lambda e, width=width: e.activation(
                        clampf[:, 0:width], clampf[:, 0:width], AF.Exp))
                    ser.op("vector", lambda e, ft=ft, width=width: e.tensor_scalar(
                        dst[:, ft * width:(ft + 1) * width], clampf[:, 0:width],
                        fcol[:, ft:ft + 1], None, OP.mult))
                else:
                    ser.op("scalar", lambda e, ft=ft, width=width: e.activation(
                        dst[:, ft * width:(ft + 1) * width], clampf[:, 0:width], AF.Exp))
        mk_tab(tard, "taud", 128, True)
        mk_tab(takd, "taukd", 128, True)
        mk_tab(tarx, "taux", 64, False)
        mk_tab(takx, "taukx", 64, False)
        mk_tab(tir, "tauir", 128, False)
        mk_tab(tfb, "taucmp", 128, True, post_tf=True)
        for q in range(4):
            ser.op("tensor", lambda e, q=q: e.matmul(
                PT[0:CH, 0:512], taus["tau31"][:], srep[0:1, q * 512:(q + 1) * 512],
                start=True, stop=True, skip_group_check=True))
            ser.op("scalar", lambda e, q=q: e.activation(
                tkh[:, q * 512:(q + 1) * 512], PT[0:CH, 0:512], AF.Exp))
        for q in range(2):
            ser.op("tensor", lambda e, q=q: e.matmul(
                PT[:, 0:512], s8f[:], ind8p[:, q * 512:(q + 1) * 512],
                start=True, stop=True, skip_group_check=True))
            ser.op("scalar", lambda e, q=q: e.activation(
                tdec[:, q * 512:(q + 1) * 512], PT[:, 0:512], AF.Exp, scale=float(CH)))
        for q in range(2):
            ser.op("tensor", lambda e, q=q: e.matmul(
                PT[0:64, 0:512], gnw8[:], ind8b[:, q * 512:(q + 1) * 512],
                start=True, stop=True, skip_group_check=True))
            ser.op("vector", lambda e, q=q: e.tensor_copy(gnwb[:, q * 512:(q + 1) * 512], PT[0:64, 0:512]))
            ser.op("tensor", lambda e, q=q: e.matmul(
                PT[0:64, 0:512], gnb8[:], ind8b[:, q * 512:(q + 1) * 512],
                start=True, stop=True, skip_group_check=True))
            ser.op("vector", lambda e, q=q: e.tensor_copy(gnbb[:, q * 512:(q + 1) * 512], PT[0:64, 0:512]))

        kc_v = kc[:].rearrange("p (g t) -> p g t", t=CH)
        rc_v = rc[:].rearrange("p (g t) -> p g t", t=CH)
        kdtf_v = kdtf[:].rearrange("p (g x) -> p g x", x=64)
        tard_v = tard[:].rearrange("p (g t) -> p g t", t=CH)
        takd_v = takd[:].rearrange("p (g t) -> p g t", t=CH)
        tfb_v = tfb[:].rearrange("p (g t) -> p g t", t=CH)
        tarx_v = tarx[:].rearrange("p (g t) -> p g t", t=16)
        takx_v = takx[:].rearrange("p (g t) -> p g t", t=16)
        rx_v = rx[:].rearrange("p (g t) -> p g t", t=16)
        kx_v = kx[:].rearrange("p (g t) -> p g t", t=16)

        for c in range(NCH):
            for ft_ in range(4):
                ser.dma(lambda e, c=c, ft_=ft_: e.dma_start(
                    out=kc[:, ft_ * 128:(ft_ + 1) * 128].rearrange("p (b tau) -> p b tau", b=4),
                    in_=kf_r[ft_, c]))
                ser.dma(lambda e, c=c, ft_=ft_: e.dma_start(
                    out=rc[:, ft_ * 128:(ft_ + 1) * 128].rearrange("p (b tau) -> p b tau", b=4),
                    in_=rf_r[ft_, c]))
            ser.dma(lambda e, c=c: e.dma_start(
                out=ktc[:].rearrange("t (b f) -> t b f", b=4), in_=kt_r[c]))
            ser.dma(lambda e, c=c: e.dma_start(
                out=vtc[:].rearrange("t (b f) -> t b f", b=4), in_=vt_r[c]))
            ser.op("vector", lambda e: e.tensor_mul(rtd[:], rc[:], tard[:]))
            ser.op("vector", lambda e: e.tensor_tensor(kdtf_v[:, :, 0:CH], kc_v, takd_v, OP.mult))
            ser.op("vector", lambda e: e.tensor_tensor(kdtf_v[:, :, CH:64], kc_v, tfb_v, OP.mult))
            ser.op("vector", lambda e: e.tensor_tensor(rx_v, rc_v[:, :, 16:CH], tarx_v, OP.mult))
            ser.op("vector", lambda e: e.tensor_tensor(kx_v, kc_v[:, :, 0:16], takx_v, OP.mult))
            ser.op("vector", lambda e: e.tensor_mul(ri[:], rc[:], tir[:]))
            ser.op("vector", lambda e: e.tensor_mul(kh[:], ktc[:], tkh[:]))
            for u in range(NU):
                h, b_ = u // 4, u % 4
                ft, pr = h // 2, (h % 2) * 64
                g = ft * 4 + b_
                ser.op("tensor", lambda e, u=u, pr=pr, g=g: e.matmul(
                    PA[:, u * CH:(u + 1) * CH],
                    kdtf[pr:pr + 64, g * 64:(g + 1) * 64],
                    rtd[pr:pr + 64, g * CH:(g + 1) * CH],
                    start=True, stop=True, skip_group_check=True))
                ser.op("tensor", lambda e, u=u, pr=pr, g=g: e.matmul(
                    PX[:, u * 16:(u + 1) * 16],
                    kx[pr:pr + 64, g * 16:(g + 1) * 16],
                    rx[pr:pr + 64, g * 16:(g + 1) * 16],
                    start=True, stop=True, skip_group_check=True))
            ser.op("vector", lambda e: e.tensor_scalar(clA[:], PA[0:CH, :], 1e30, -1e30, OP.min, OP.max))
            ser.op("vector", lambda e: e.tensor_mul(msb[:], clA[:], msu[:]))
            ser.op("vector", lambda e: e.tensor_scalar(clA[:], PA[CH:64, :], 1e30, -1e30, OP.min, OP.max))
            ser.op("vector", lambda e: e.tensor_mul(msc[:], clA[:], midm[:]))
            ser.op("vector", lambda e: e.tensor_add(msb[:], msb[:], msc[:]))
            msb_x = msb[0:16, :].rearrange("p (u t) -> p u t", t=CH)[:, :, 16:CH]
            px_v = PX[:].rearrange("p (u t) -> p u t", t=16)
            ser.op("vector", lambda e: e.tensor_add(msb_x, msb_x, px_v))
            for u in range(NU):
                h, b_ = u // 4, u % 4
                ft, pr = h // 2, (h % 2) * 64
                g = ft * 4 + b_
                fo = b_ * FPC + h * 64
                ser.op("tensor", lambda e, u=u, fo=fo: e.matmul(
                    PO[:, u * CH:(u + 1) * CH],
                    vtc[:, fo:fo + 64], msb[:, u * CH:(u + 1) * CH],
                    start=True, stop=False, skip_group_check=True))
                ser.op("tensor", lambda e, u=u, pr=pr, g=g: e.matmul(
                    PO[:, u * CH:(u + 1) * CH],
                    sbf[pr:pr + 64, g * 64:(g + 1) * 64], ri[pr:pr + 64, g * CH:(g + 1) * CH],
                    start=False, stop=True, skip_group_check=True))
            for half in range(2):
                for ui in range(16):
                    u = half * 16 + ui
                    h, b_ = u // 4, u % 4
                    hh = h % 2
                    g = (h // 2) * 4 + b_
                    fo = b_ * FPC + h * 64
                    ser.op("tensor", lambda e, hh=hh, g=g, half=half, fo=fo: e.matmul(
                        PS[hh * 64:(hh + 1) * 64, (g - half * 8) * 64:(g - half * 8 + 1) * 64],
                        kh[:, fo:fo + 64], vtc[:, fo:fo + 64],
                        start=True, stop=True, skip_group_check=True))
                sl = slice(half * 512, (half + 1) * 512)
                ser.op("vector", lambda e, sl=sl: e.tensor_mul(sf32[:, sl], sf32[:, sl], tdec[:, sl]))
                ser.op("vector", lambda e, sl=sl: e.tensor_add(sf32[:, sl], sf32[:, sl], PS[:]))
            ser.op("vector", lambda e: e.tensor_copy(osb[:], PO[:]))
            ser.op("scalar", lambda e: e.activation(osq[:], osb[:], AF.Square))
            for q in range(2):
                ser.op("tensor", lambda e, q=q: e.matmul(
                    PA[0:1, q * 512:(q + 1) * 512], ones64[:], osb[:, q * 512:(q + 1) * 512],
                    start=True, stop=True, skip_group_check=True))
                ser.op("tensor", lambda e, q=q: e.matmul(
                    PO[0:1, q * 512:(q + 1) * 512], ones64[:], osq[:, q * 512:(q + 1) * 512],
                    start=True, stop=True, skip_group_check=True))
            ser.op("vector", lambda e: e.tensor_scalar(mrow[:], PA[0:1, :], 1.0 / 64, None, OP.mult))
            ser.op("vector", lambda e: e.tensor_scalar(mrow2[:], PO[0:1, :], 1.0 / 64, None, OP.mult))
            ser.op("vector", lambda e: e.tensor_mul(mrow3[:], mrow[:], mrow[:]))
            ser.op("vector", lambda e: e.tensor_sub(mrow2[:], mrow2[:], mrow3[:]))
            ser.op("scalar", lambda e: e.activation(mrow2[:], mrow2[:], AF.Ln, bias=cgn[0:1, 0:1]))
            ser.op("scalar", lambda e: e.activation(mrow2[:], mrow2[:], AF.Exp, scale=-0.5))
            for q in range(2):
                qs = slice(q * 512, (q + 1) * 512)
                ser.op("tensor", lambda e, q=q, qs=qs: e.matmul(
                    PA[0:64, qs], ones1r[:], mrow[0:1, qs],
                    start=True, stop=True, skip_group_check=True))
                ser.op("tensor", lambda e, q=q, qs=qs: e.matmul(
                    PT[0:64, 0:512], ones1r[:], mrow2[0:1, qs],
                    start=True, stop=True, skip_group_check=True))
                ser.op("vector", lambda e, qs=qs: e.tensor_sub(t1[:, qs], osb[:, qs], PA[0:64, qs]))
                ser.op("vector", lambda e, qs=qs: e.tensor_mul(t1[:, qs], t1[:, qs], PT[0:64, 0:512]))
            ser.op("vector", lambda e: e.tensor_mul(t1[:], t1[:], gnwb[:]))
            ser.op("vector", lambda e: e.tensor_add(xnb[:], t1[:], gnbb[:]))
            for h_ in range(HPC):
                ser.dma(lambda e, c=c, h_=h_: e.dma_start(
                    out=xo_r[h_, c],
                    in_=xnb[:, h_ * 128:(h_ + 1) * 128].rearrange("e (b tau) -> e b tau", b=4)))
            ser.op("vector", lambda e: e.tensor_copy(sbf[:], sf32[:]))
        ser.finish(block, chain)
    return nc


def run_p5(kfm, vfm, rfm, time_decay, time_faaaa, gn_weight, gn_bias):
    nc = build_p5()
    ktm = np.ascontiguousarray(kfm.T)
    vtm = np.ascontiguousarray(vfm.T)
    tau = np.arange(128) % CH
    tau16 = tau % 16
    taud = (tau16 - 8).astype(np.float32)[None, :]
    taukd = (7 - tau16).astype(np.float32)[None, :]
    taucmp = (8 - tau16).astype(np.float32)[None, :]
    tauir = tau.astype(np.float32)[None, :]
    tx = np.arange(64) % 16
    taux = tx.astype(np.float32)[None, :]
    taukx = (15 - tx).astype(np.float32)[None, :]
    tau31 = (31 - np.arange(CH)).astype(np.float32)[None, :]
    ind8p = np.zeros((HPC, 16 * HD1), dtype=np.float32)
    for h in range(HPC):
        for g in range(16):
            if g // 4 == h // 2:
                ind8p[h, g * HD1:(g + 1) * HD1] = 1.0
    ind8b = np.zeros((HPC, NU * CH), dtype=np.float32)
    for h in range(HPC):
        for u in range(NU):
            if u // 4 == h:
                ind8b[h, u * CH:(u + 1) * CH] = 1.0
    tp = np.arange(CH)[:, None]
    tq = np.arange(CH)[None, :]
    msu1 = ((tp < tq) & (tp // 16 == tq // 16)).astype(np.float32)
    mid1 = (tp == tq).astype(np.float32)
    msu = np.tile(msu1, (1, NU))
    mid = np.tile(mid1, (1, NU))
    in_maps = []
    for c in range(NCORES):
        fs = c * FPC
        hs = c * HPC
        dsl = time_decay[hs:hs + HPC]
        dft = dsl.reshape(4, 128)
        fsl = time_faaaa[hs:hs + HPC]
        fcol = np.ascontiguousarray(fsl.reshape(4, 128).T)
        d8f = np.zeros((HPC, 128), dtype=np.float32)
        mkf = np.zeros((HPC, 128), dtype=np.float32)
        for h in range(HPC):
            off = (h % 2) * 64
            d8f[h, off:off + 64] = dsl[h]
            mkf[h, off:off + 64] = 1.0
        in_maps.append({
            "kf": np.ascontiguousarray(kfm[fs:fs + FPC]),
            "rf": np.ascontiguousarray(rfm[fs:fs + FPC]),
            "kt": np.ascontiguousarray(ktm[:, fs:fs + FPC]),
            "vt": np.ascontiguousarray(vtm[:, fs:fs + FPC]),
            "dft": np.ascontiguousarray(dft.astype(np.float32)),
            "fcol": fcol.astype(np.float32),
            "d8f": d8f, "mkf": mkf,
            "gnw": np.ascontiguousarray(gn_weight[fs:fs + FPC].reshape(HPC, HD1)),
            "gnb": np.ascontiguousarray(gn_bias[fs:fs + FPC].reshape(HPC, HD1)),
            "ind8p": ind8p, "ind8b": ind8b,
            "taud": taud, "taukd": taukd, "taux": taux, "taukx": taukx,
            "tauir": tauir, "tau31": tau31, "taucmp": taucmp,
            "msu": msu, "mid": mid,
        })
    res = run_bass_kernel_spmd(nc, in_maps, core_ids=list(range(NCORES)))
    xn = np.concatenate([res.results[c]["xno"] for c in range(NCORES)], axis=0)
    return xn


NSK = 50


def build_p6():
    nc = bass.Bass("TRN2", target_bir_lowering=False)
    xnf_in = nc.dram_tensor("xnf", [D, TC], BF16, kind="ExternalInput").ap()
    gf_in = nc.dram_tensor("gf", [D, TC], BF16, kind="ExternalInput").ap()
    x2_in = nc.dram_tensor("x2", [TC, D], F32, kind="ExternalInput").ap()
    wot_in = nc.dram_tensor("wot", [D, D], F32, kind="ExternalInput").ap()
    wspt_in = nc.dram_tensor("wspt", [D, 24], F32, kind="ExternalInput").ap()
    bvec_in = nc.dram_tensor("bvec", [1, 24], F32, kind="ExternalInput").ap()
    gate_in = nc.dram_tensor("gate", [1, 1], F32, kind="ExternalInput").ap()
    wb_in = nc.dram_tensor("wb", [1024, 1024], F32, kind="ExternalInput").ap()
    idf_in = nc.dram_tensor("idf", [128, 128], F32, kind="ExternalInput").ap()
    out_out = nc.dram_tensor("out", [TC, D], F32, kind="ExternalOutput").ap()
    x3_scr = nc.dram_tensor("x3_scr", [TC, D], F32, kind="Internal").ap()

    with ExitStack() as st:
        sb = lambda name, shp, dt: st.enter_context(nc.sbuf_tensor("t_" + name, shp, dt))
        ps = lambda name, shp, dt: st.enter_context(nc.psum_tensor("p_" + name, shp, dt))
        stga = sb("stga", [128, TC], BF16)
        stgb = sb("stgb", [128, TC], BF16)
        gxnb = sb("gxnb", [128, 32 * TC], BF16)
        ldf = sb("ldf", [128, 1024], F32)
        wtb = sb("wtb", [128, 512], BF16)
        xr = sb("xr", [128, 512], F32)
        ev = sb("ev", [128, 512], F32)
        sx3 = sb("sx3", [128, D], F32)
        sxn = sb("sxn", [128, D], F32)
        sq = sb("sq", [128, D], F32)
        cols = sb("cols", [128, 16], F32)
        ceps = sb("ceps", [128, 1], F32)
        crms = sb("crms", [128, 1], F32)
        xnT = sb("xnT", [128, 32 * 128], BF16)
        idf = sb("idf", [128, 128], F32)
        wsptb = sb("wsptb", [128, 32 * 24], BF16)
        bvbc = sb("bvbc", [128, 24], F32)
        onesr = sb("onesr", [1, 128], F32)
        lg24 = sb("lg24", [128, 24], F32)
        hpp = sb("hpp", [128, 8 * 8], F32)
        z16all = sb("z16all", [128, 8 * 16], F32)
        t16 = sb("t16", [128, 16], F32)
        e16 = sb("e16", [128, 16], F32)
        mx4 = sb("mx4", [128, 4], F32)
        s4 = sb("s4", [128, 4], F32)
        ln4 = sb("ln4", [128, 4], F32)
        uneg = sb("uneg", [128, 4], F32)
        vneg = sb("vneg", [128, 4], F32)
        hrt = sb("hrt", [128, 8 * 16], F32)
        gg = sb("gg", [1, 1], F32)
        gcol = sb("gcol", [128, 1], F32)
        gm1 = sb("gm1", [128, 1], F32)
        wbb = sb("wbb", [128, 8 * 1024], BF16)
        xp = sb("xp", [128, 1024], F32)
        xpb = sb("xpb", [128, 1024], BF16)
        y_sb = sb("y_sb", [128, 1024], F32)
        to = sb("to", [128, 1024], F32)
        pf = [ps(f"pf{i}", [128, 512], F32) for i in range(4)]
        chain = st.enter_context(nc.semaphore("chain"))
        block = st.enter_context(nc.Block())
        ptrf = pf[2][:, 0:128]
        plg = pf[1][:, 0:24]
        pbv = pf[1][:, 128:152]
        pg128 = pf[1][:, 256:257]

        ser = Ser()
        ser.dma(lambda e: e.dma_start(out=idf[:], in_=idf_in))
        ser.op("vector", lambda e: e.memset(onesr[:], 1.0))
        ser.op("vector", lambda e: e.memset(ceps[:], LN_EPS))
        ser.op("vector", lambda e: e.memset(crms[:], RMS_EPS))
        ser.dma(lambda e: e.dma_start(out=gg[:], in_=gate_in))
        for fi in range(32):
            ser.dma(lambda e, fi=fi: e.dma_start(out=stga[:], in_=xnf_in[fi * 128:(fi + 1) * 128, :]))
            ser.dma(lambda e, fi=fi: e.dma_start(out=stgb[:], in_=gf_in[fi * 128:(fi + 1) * 128, :]))
            ser.op("vector", lambda e, fi=fi: e.tensor_mul(gxnb[:, fi * TC:(fi + 1) * TC], stga[:], stgb[:]))
        for fog in range(8):
            for rnd in range(2):
                for fi in range(32):
                    co = fog * 512
                    ser.dma(lambda e, fi=fi, co=co: e.dma_start(
                        out=ldf[:, 0:512], in_=wot_in[fi * 128:(fi + 1) * 128, co:co + 512]))
                    ser.op("vector", lambda e: e.tensor_copy(wtb[:], ldf[:, 0:512]))
                    for ti in range(4):
                        tt = rnd * 4 + ti
                        pv = pf[ti][:]
                        ser.op("tensor", lambda e, fi=fi, tt=tt, pv=pv: e.matmul(
                            pv, gxnb[:, fi * TC + tt * 128: fi * TC + (tt + 1) * 128], wtb[:],
                            start=(fi == 0), stop=(fi == 31), skip_group_check=True))
                for ti in range(4):
                    tt = rnd * 4 + ti
                    co = fog * 512
                    pv = pf[ti][:]
                    ser.dma(lambda e, tt=tt, co=co: e.dma_start(
                        out=xr[:], in_=x2_in[tt * 128:(tt + 1) * 128, co:co + 512]))
                    ser.op("vector", lambda e, pv=pv: e.tensor_add(ev[:, 0:512], pv, xr[:]))
                    ser.dma(lambda e, tt=tt, co=co: e.dma_start(
                        out=x3_scr[tt * 128:(tt + 1) * 128, co:co + 512], in_=ev[:, 0:512]))
        ser.dma(lambda e: e.dma_start(
            out=ldf[:, 0:768].rearrange("p (f e) -> p f e", f=32),
            in_=wspt_in.rearrange("(f p) e -> p f e", p=128)))
        ser.op("vector", lambda e: e.tensor_copy(wsptb[:], ldf[:, 0:768]))
        ser.dma(lambda e: e.dma_start(out=ldf[0:1, 0:24], in_=bvec_in))
        ser.op("tensor", lambda e: e.matmul(pbv, onesr[:], ldf[0:1, 0:24],
                                            start=True, stop=True, skip_group_check=True))
        ser.op("vector", lambda e: e.tensor_copy(bvbc[:], pbv))
        for ci in range(8):
            ser.dma(lambda e, ci=ci: e.dma_start(out=ldf[:], in_=wb_in[ci * 128:(ci + 1) * 128, :]))
            ser.op("vector", lambda e, ci=ci: e.tensor_copy(wbb[:, ci * 1024:(ci + 1) * 1024], ldf[:]))
        c5, c6, c7, c8, c9, c10, c11 = (cols[:, 8 + i:9 + i] for i in range(7))
        for tt in range(NT):
            ser.dma(lambda e, tt=tt: e.dma_start(out=sx3[:], in_=x3_scr[tt * 128:(tt + 1) * 128, :]))
            _ln_ops(ser, sx3[:], sxn[:], sq[:], cols, D, ceps[:, 0:1])
            ser.op("scalar", lambda e: e.activation(sq[:], sxn[:], AF.Square))
            ser.op("vector", lambda e: e.reduce_sum(c5, sq[:], AX))
            ser.op("vector", lambda e: e.tensor_scalar(c5, c5, 1.0 / D, None, OP.mult))
            ser.op("scalar", lambda e: e.activation(c6, c5, AF.Ln, bias=crms[:, 0:1]))
            ser.op("scalar", lambda e: e.activation(c7, c6, AF.Exp, scale=-0.5))
            ser.op("vector", lambda e: e.tensor_scalar(sq[:], sxn[:], c7, None, OP.mult))
            for fi in range(32):
                ser.op("tensor", lambda e, fi=fi: e.transpose(ptrf, sq[:, fi * 128:(fi + 1) * 128], idf[:]))
                ser.op("vector", lambda e, fi=fi: e.tensor_copy(xnT[:, fi * 128:(fi + 1) * 128], ptrf))
            for fi in range(32):
                ser.op("tensor", lambda e, fi=fi: e.matmul(
                    plg, xnT[:, fi * 128:(fi + 1) * 128], wsptb[:, fi * 24:(fi + 1) * 24],
                    start=(fi == 0), stop=(fi == 31), skip_group_check=True))
            ser.op("vector", lambda e: e.tensor_add(lg24[:], plg, bvbc[:]))
            for gi, off in ((0, 16), (1, 20)):
                ser.op("vector", lambda e, off=off: e.reduce_max(c8, lg24[:, off:off + 4], AX))
                ser.op("vector", lambda e: e.tensor_scalar(c9, c8, -1.0, None, OP.mult))
                hv = hpp[:, tt * 8 + gi * 4: tt * 8 + gi * 4 + 4]
                ser.op("scalar", lambda e, off=off, hv=hv: e.activation(
                    hv, lg24[:, off:off + 4], AF.Exp, bias=c9))
                ser.op("vector", lambda e, hv=hv: e.reduce_sum(c10, hv, AX))
                ser.op("vector", lambda e: e.reciprocal(c11, c10))
                ser.op("vector", lambda e, hv=hv: e.tensor_scalar(hv, hv, c11, None, OP.mult))
            ser.op("vector", lambda e, tt=tt: e.tensor_scalar(
                z16all[:, tt * 16:(tt + 1) * 16], lg24[:, 0:16], 10.0, None, OP.mult))
        ser.op("scalar", lambda e: e.activation(gg[:], gg[:], AF.Sigmoid))
        ser.op("tensor", lambda e: e.matmul(pg128, onesr[:], gg[:],
                                            start=True, stop=True, skip_group_check=True))
        ser.op("vector", lambda e: e.tensor_copy(gcol[:], pg128))
        ser.op("vector", lambda e: e.tensor_scalar(gm1[:], gcol[:], -1.0, 1.0, OP.mult, OP.add))
        for tt in range(NT):
            Z = z16all[:, tt * 16:(tt + 1) * 16]
            Zv = Z.rearrange("p (i j) -> p i j", i=4)
            t16v = t16[:].rearrange("p (i j) -> p i j", i=4)
            e16v = e16[:].rearrange("p (i j) -> p i j", i=4)
            ser.op("vector", lambda e, Z=Z: e.tensor_copy(t16[:], Z))
            ser.op("vector", lambda e: e.memset(vneg[:], 0.0))
            for it in range(NSK):
                for j in range(4):
                    ser.op("vector", lambda e, Z=Zv, j=j: e.tensor_scalar(
                        t16v[:, :, j], Z[:, :, j], vneg[:, j:j + 1], None, OP.subtract))
                ser.op("vector", lambda e: e.reduce_max(mx4[:], t16v, AX))
                for i in range(4):
                    ser.op("vector", lambda e, i=i: e.tensor_scalar(
                        t16[:, i * 4:(i + 1) * 4], t16[:, i * 4:(i + 1) * 4],
                        mx4[:, i:i + 1], None, OP.subtract))
                ser.op("scalar", lambda e: e.activation(e16[:], t16[:], AF.Exp))
                ser.op("vector", lambda e: e.reduce_sum(s4[:], e16v, AX))
                ser.op("scalar", lambda e: e.activation(ln4[:], s4[:], AF.Ln))
                ser.op("vector", lambda e: e.tensor_add(uneg[:], mx4[:], ln4[:]))
                for i in range(4):
                    ser.op("vector", lambda e, Z=Z, i=i: e.tensor_scalar(
                        t16[:, i * 4:(i + 1) * 4], Z[:, i * 4:(i + 1) * 4],
                        uneg[:, i:i + 1], None, OP.subtract))
                tjv = t16[:].rearrange("p (i j) -> p j i", i=4)
                ejv = e16[:].rearrange("p (i j) -> p j i", i=4)
                ser.op("vector", lambda e, tjv=tjv: e.reduce_max(mx4[:], tjv, AX))
                for j in range(4):
                    ser.op("vector", lambda e, j=j: e.tensor_scalar(
                        t16v[:, :, j], t16v[:, :, j], mx4[:, j:j + 1], None, OP.subtract))
                ser.op("scalar", lambda e: e.activation(e16[:], t16[:], AF.Exp))
                ser.op("vector", lambda e, ejv=ejv: e.reduce_sum(s4[:], ejv, AX))
                ser.op("scalar", lambda e: e.activation(ln4[:], s4[:], AF.Ln))
                ser.op("vector", lambda e: e.tensor_add(vneg[:], mx4[:], ln4[:]))
            for i in range(4):
                ser.op("vector", lambda e, Z=Z, i=i: e.tensor_scalar(
                    t16[:, i * 4:(i + 1) * 4], Z[:, i * 4:(i + 1) * 4],
                    uneg[:, i:i + 1], None, OP.subtract))
            for j in range(4):
                ser.op("vector", lambda e, j=j: e.tensor_scalar(
                    t16v[:, :, j], t16v[:, :, j], vneg[:, j:j + 1], None, OP.subtract))
            ser.op("scalar", lambda e: e.activation(e16[:], t16[:], AF.Exp))
            hv = hrt[:, tt * 16:(tt + 1) * 16]
            ser.op("vector", lambda e, hv=hv: e.tensor_scalar(hv, e16[:], gcol[:, 0:1], None, OP.mult))
            for ii in range(4):
                col = tt * 16 + ii * 5
                ser.op("vector", lambda e, col=col: e.tensor_scalar(
                    hrt[:, col:col + 1], hrt[:, col:col + 1], gm1[:, 0:1], None, OP.add))
        for tt in range(NT):
            ser.dma(lambda e, tt=tt: e.dma_start(out=sx3[:], in_=x3_scr[tt * 128:(tt + 1) * 128, :]))
            _ln_ops(ser, sx3[:], sxn[:], sq[:], cols, D, ceps[:, 0:1])
            ser.op("vector", lambda e, tt=tt: e.tensor_scalar(
                xp[:], sxn[:, 0:1024], hpp[:, tt * 8: tt * 8 + 1], None, OP.mult))
            for j in range(1, 4):
                ser.op("vector", lambda e, tt=tt, j=j: e.scalar_tensor_tensor(
                    xp[:], sxn[:, j * 1024:(j + 1) * 1024], hpp[:, tt * 8 + j: tt * 8 + j + 1],
                    xp[:], OP.mult, OP.add))
            for ci in range(8):
                ser.op("tensor", lambda e, ci=ci: e.transpose(ptrf, xp[:, ci * 128:(ci + 1) * 128], idf[:]))
                ser.op("vector", lambda e, ci=ci: e.tensor_copy(xnT[:, ci * 128:(ci + 1) * 128], ptrf))
            for nh in range(2):
                pv = pf[nh]
                for ci in range(8):
                    ser.op("tensor", lambda e, ci=ci, nh=nh, pv=pv: e.matmul(
                        pv[:], xnT[:, ci * 128:(ci + 1) * 128],
                        wbb[:, ci * 1024 + nh * 512: ci * 1024 + (nh + 1) * 512],
                        start=(ci == 0), stop=(ci == 7), skip_group_check=True))
                ser.op("vector", lambda e, nh=nh, pv=pv: e.tensor_copy(y_sb[:, nh * 512:(nh + 1) * 512], pv[:]))
            for i in range(4):
                ser.op("vector", lambda e, tt=tt, i=i: e.tensor_scalar(
                    to[:], sxn[:, 0:1024], hrt[:, tt * 16 + i * 4: tt * 16 + i * 4 + 1], None, OP.mult))
                for j in range(1, 4):
                    ser.op("vector", lambda e, tt=tt, i=i, j=j: e.scalar_tensor_tensor(
                        to[:], sxn[:, j * 1024:(j + 1) * 1024],
                        hrt[:, tt * 16 + i * 4 + j: tt * 16 + i * 4 + j + 1],
                        to[:], OP.mult, OP.add))
                ser.op("vector", lambda e, tt=tt, i=i: e.scalar_tensor_tensor(
                    to[:], y_sb[:], hpp[:, tt * 8 + 4 + i: tt * 8 + 4 + i + 1],
                    to[:], OP.mult, OP.add))
                ser.op("vector", lambda e, i=i: e.tensor_add(to[:], to[:], sx3[:, i * 1024:(i + 1) * 1024]))
                ser.dma(lambda e, tt=tt, i=i: e.dma_start(
                    out=out_out[tt * 128:(tt + 1) * 128, i * 1024:(i + 1) * 1024], in_=to[:]))
        ser.finish(block, chain)
    return nc


def run_p6(xnf, gfm, x2, w_output, W_res, W_pre, W_post, H_res_base, H_pre_base,
           H_post_base, gate, W_branch):
    nc = build_p6()
    idf = np.eye(128, dtype=np.float32)
    wot = np.ascontiguousarray(w_output.T)
    wspt = np.ascontiguousarray(np.concatenate([W_res, W_pre, W_post], axis=0).T)
    bvec = np.concatenate([H_res_base.reshape(-1), H_pre_base, H_post_base])[None, :].astype(np.float32)
    in_maps = []
    for c in range(NCORES):
        t0 = c * TC
        in_maps.append({
            "xnf": np.ascontiguousarray(xnf[:, t0:t0 + TC]),
            "gf": np.ascontiguousarray(gfm[:, t0:t0 + TC]),
            "x2": np.ascontiguousarray(x2[t0:t0 + TC]),
            "wot": wot, "wspt": wspt, "bvec": bvec,
            "gate": gate.reshape(1, 1).astype(np.float32),
            "wb": W_branch, "idf": idf,
        })
    res = run_bass_kernel_spmd(nc, in_maps, core_ids=list(range(NCORES)))
    out = np.concatenate([res.results[c]["out"] for c in range(NCORES)], axis=0)
    return out


def kernel(L0_attn_hidden_states, L0_attn_Wq, L0_attn_Wkv, L0_attn_Wo,
           L0_ffn_router_w, L0_ffn_A, L0_ffn_B,
           L1_attn_w_key, L1_attn_w_value, L1_attn_w_receptance, L1_attn_w_gate,
           L1_attn_w_output, L1_attn_time_mix_key, L1_attn_time_mix_value,
           L1_attn_time_mix_receptance, L1_attn_time_mix_gate, L1_attn_time_decay,
           L1_attn_time_faaaa, L1_attn_gn_weight, L1_attn_gn_bias,
           L1_ffn_W_res, L1_ffn_W_pre, L1_ffn_W_post, L1_ffn_H_res_base,
           L1_ffn_H_pre_base, L1_ffn_H_post_base, L1_ffn_gate, L1_ffn_W_branch):
    x = np.ascontiguousarray(np.asarray(L0_attn_hidden_states, dtype=np.float32).reshape(T, D))
    qfm, kv = run_p1(x, np.asarray(L0_attn_Wq), np.asarray(L0_attn_Wkv))
    ao = run_p2(qfm, kv)
    x2, xl2 = run_p3(ao, x, np.asarray(L0_attn_Wo), np.asarray(L0_ffn_router_w),
                     np.asarray(L0_ffn_A), np.asarray(L0_ffn_B))
    kfm, vfm, rfm, gfm = run_p4(xl2, np.asarray(L1_attn_w_key), np.asarray(L1_attn_w_value),
                                np.asarray(L1_attn_w_receptance), np.asarray(L1_attn_w_gate),
                                np.asarray(L1_attn_time_mix_key), np.asarray(L1_attn_time_mix_value),
                                np.asarray(L1_attn_time_mix_receptance), np.asarray(L1_attn_time_mix_gate))
    xnf = run_p5(kfm, vfm, rfm, np.asarray(L1_attn_time_decay), np.asarray(L1_attn_time_faaaa),
                 np.asarray(L1_attn_gn_weight), np.asarray(L1_attn_gn_bias))
    out = run_p6(xnf, gfm, x2, np.asarray(L1_attn_w_output), np.asarray(L1_ffn_W_res),
                 np.asarray(L1_ffn_W_pre), np.asarray(L1_ffn_W_post),
                 np.asarray(L1_ffn_H_res_base), np.asarray(L1_ffn_H_pre_base),
                 np.asarray(L1_ffn_H_post_base), np.asarray(L1_ffn_gate),
                 np.asarray(L1_ffn_W_branch))
    return out.reshape(B, S, D)


# version 02
# speedup vs baseline: 1.3720x; 1.3720x over previous
from contextlib import ExitStack

import numpy as np
import ml_dtypes
import concourse.bass as bass
import concourse.mybir as mybir
from concourse.bass_utils import run_bass_kernel_spmd

F32 = mybir.dt.float32
BF16 = mybir.dt.bfloat16
AX = mybir.AxisListType.X
AF = mybir.ActivationFunctionType
OP = mybir.AluOpType
BF = ml_dtypes.bfloat16

NCORES = 8
B, S, D = 4, 2048, 4096
T = B * S
TC = T // NCORES
NT = TC // 128
NH0, HD0 = 32, 128
NH1, HD1 = 64, 64
HPC = NH1 // NCORES
FPC = HPC * HD1
LN_EPS = 1e-5
RMS_EPS = 1.1920929e-07
QSCALE = float(HD0) ** -0.5
NU = 32
CH = 32
NCH = S // CH


def _alibi_slopes(n):
    start = 2.0 ** (-(2.0 ** (-(np.log2(n) - 3))))
    return np.array([start ** (i + 1) for i in range(n)], dtype=np.float64)


class Ser:

    def __init__(self):
        self.ops = []

    def op(self, eng, fn):
        self.ops.append((eng, fn, 1))

    def dma(self, fn):
        self.ops.append(("sync", fn, 16))

    def finish(self, block, chain):
        total = sum(i for _, _, i in self.ops)

        def runner_for(name):
            def runner(eng):
                cnt = 0
                for e, fn, inc in self.ops:
                    if e == name:
                        eng.wait_ge(chain, cnt)
                        fn(eng).then_inc(chain, inc)
                    cnt += inc
                if name == "gpsimd":
                    eng.wait_ge(chain, total)
            return runner

        block.sync(runner_for("sync"))
        block.vector(runner_for("vector"))
        block.scalar(runner_for("scalar"))
        block.tensor(runner_for("tensor"))
        block.gpsimd(runner_for("gpsimd"))


def _ln_ops(ser, x, xn, sq, cols, n, eps_col):
    mean, msq, negv, lnv, rstd = (cols[:, i:i + 1] for i in range(5))
    ser.op("vector", lambda e: e.reduce_sum(mean, x, AX))
    ser.op("vector", lambda e: e.tensor_scalar(mean, mean, 1.0 / n, None, OP.mult))
    ser.op("scalar", lambda e: e.activation(sq, x, AF.Square))
    ser.op("vector", lambda e: e.reduce_sum(msq, sq, AX))
    ser.op("vector", lambda e: e.tensor_scalar(msq, msq, 1.0 / n, None, OP.mult))
    ser.op("vector", lambda e: e.scalar_tensor_tensor(negv, mean, mean, msq, OP.mult, OP.subtract))
    ser.op("scalar", lambda e: e.activation(lnv, negv, AF.Ln, bias=eps_col, scale=-1.0))
    ser.op("scalar", lambda e: e.activation(rstd, lnv, AF.Exp, scale=-0.5))
    ser.op("vector", lambda e: e.tensor_scalar(xn, x, mean, rstd, OP.subtract, OP.mult))


def build_p1():
    nc = bass.Bass("TRN2", target_bir_lowering=False)
    x_in = nc.dram_tensor("x", [TC, D], F32, kind="ExternalInput").ap()
    wq_in = nc.dram_tensor("wq", [32, 128, 32, 128], F32, kind="ExternalInput").ap()
    wkv_in = nc.dram_tensor("wkv", [D, 256], F32, kind="ExternalInput").ap()
    idf_in = nc.dram_tensor("idf", [128, 128], F32, kind="ExternalInput").ap()
    qfm_out = nc.dram_tensor("qfm", [D, TC], F32, kind="ExternalOutput").ap()
    kv_out = nc.dram_tensor("kv", [TC, 256], F32, kind="ExternalOutput").ap()

    with ExitStack() as st:
        sb = lambda name, shp, dt: st.enter_context(nc.sbuf_tensor("t_" + name, shp, dt))
        ps = lambda name, shp, dt: st.enter_context(nc.psum_tensor("p_" + name, shp, dt))
        sx = sb("sx", [128, D], F32)
        sq = sb("sq", [128, D], F32)
        cols = sb("cols", [128, 8], F32)
        ceps = sb("ceps", [128, 1], F32)
        xn = sb("xn", [128, D], F32)
        xnT = sb("xnT", [128, 32 * TC], BF16)
        idf = sb("idf", [128, 128], F32)
        wtf = sb("wtf", [128, 32 * 128], F32)
        wtb = sb("wtb", [128, 32 * 128], BF16)
        wkvf = sb("wkvf", [128, 256], F32)
        wkvb = sb("wkvb", [128, 32 * 256], BF16)
        evac = sb("evac", [128, 512], F32)
        pf0 = ps("pf0", [128, 512], F32)
        pf1 = ps("pf1", [128, 512], F32)
        pf2 = ps("pf2", [128, 512], F32)
        ptrf = pf2[:, 0:128]
        pkv = ps("pkv", [128, 256], F32)
        chain = st.enter_context(nc.semaphore("chain"))
        block = st.enter_context(nc.Block())

        ser = Ser()
        ser.dma(lambda e: e.dma_start(out=idf[:], in_=idf_in))
        ser.op("vector", lambda e: e.memset(ceps[:], LN_EPS))
        for tt in range(NT):
            ser.dma(lambda e, tt=tt: e.dma_start(out=sx[:], in_=x_in[tt * 128:(tt + 1) * 128, :]))
            _ln_ops(ser, sx[:], xn[:], sq[:], cols, D, ceps[:, 0:1])
            for fi in range(32):
                ser.op("tensor", lambda e, fi=fi: e.transpose(ptrf, xn[:, fi * 128:(fi + 1) * 128], idf[:]))
                ser.op("vector", lambda e, fi=fi, tt=tt:
                       e.tensor_copy(xnT[:, fi * TC + tt * 128: fi * TC + (tt + 1) * 128], ptrf))
        for fo in range(32):
            ser.dma(lambda e, fo=fo: e.dma_start(
                out=wtf[:].rearrange("p (f c) -> p f c", f=32), in_=wq_in[fo]))
            ser.op("vector", lambda e: e.tensor_copy(wtb[:], wtf[:]))
            for fi in range(32):
                for th in range(2):
                    pq = pf0 if th == 0 else pf1
                    ser.op("tensor", lambda e, fi=fi, th=th, pq=pq: e.matmul(
                        pq[:], wtb[:, fi * 128:(fi + 1) * 128],
                        xnT[:, fi * TC + th * 512: fi * TC + (th + 1) * 512],
                        start=(fi == 0), stop=(fi == 31), skip_group_check=True))
            for th in range(2):
                pq = pf0 if th == 0 else pf1
                ser.op("vector", lambda e, pq=pq: e.tensor_copy(evac[:], pq[:]))
                ser.dma(lambda e, fo=fo, th=th: e.dma_start(
                    out=qfm_out[fo * 128:(fo + 1) * 128, th * 512:(th + 1) * 512], in_=evac[:]))
        for fi in range(32):
            ser.dma(lambda e, fi=fi: e.dma_start(out=wkvf[:], in_=wkv_in[fi * 128:(fi + 1) * 128, :]))
            ser.op("vector", lambda e, fi=fi: e.tensor_copy(wkvb[:, fi * 256:(fi + 1) * 256], wkvf[:]))
        for tt in range(NT):
            for fi in range(32):
                ser.op("tensor", lambda e, tt=tt, fi=fi: e.matmul(
                    pkv[:], xnT[:, fi * TC + tt * 128: fi * TC + (tt + 1) * 128],
                    wkvb[:, fi * 256:(fi + 1) * 256],
                    start=(fi == 0), stop=(fi == 31), skip_group_check=True))
            ser.op("vector", lambda e: e.tensor_copy(evac[:, 0:256], pkv[:]))
            ser.dma(lambda e, tt=tt: e.dma_start(
                out=kv_out[tt * 128:(tt + 1) * 128, :], in_=evac[:, 0:256]))
        ser.finish(block, chain)
    return nc


def run_p1(x_full, wq, wkv):
    nc = build_p1()
    idf = np.eye(128, dtype=np.float32)
    wq_p = np.ascontiguousarray(wq.reshape(32, 128, 32, 128).transpose(2, 1, 0, 3))
    in_maps = [{"x": np.ascontiguousarray(x_full[c * TC:(c + 1) * TC]),
                "wq": wq_p, "wkv": wkv, "idf": idf} for c in range(NCORES)]
    res = run_bass_kernel_spmd(nc, in_maps, core_ids=list(range(NCORES)))
    qfm = np.concatenate([res.results[c]["qfm"] for c in range(NCORES)], axis=1)
    kv = np.concatenate([res.results[c]["kv"] for c in range(NCORES)], axis=0)
    return qfm, kv


NSLOT = 4
KT_OF_SLOT = [4 * (s + 1) for s in range(NSLOT)]
NKT = sum(KT_OF_SLOT)


def build_p2():
    nc = bass.Bass("TRN2", target_bir_lowering=False)
    qb_in = nc.dram_tensor("qb", [D, NSLOT * 256], F32, kind="ExternalInput").ap()
    kf_in = nc.dram_tensor("kf", [HD0, S], F32, kind="ExternalInput").ap()
    vt_in = nc.dram_tensor("vt", [S, HD0], F32, kind="ExternalInput").ap()
    pmjd_in = nc.dram_tensor("pmjd", [128, NKT * 256], F32, kind="ExternalInput").ap()
    ao_out = nc.dram_tensor("ao", [D, NSLOT * 256], F32, kind="ExternalOutput").ap()

    slopes = _alibi_slopes(NH0)
    with ExitStack() as st:
        sb = lambda name, shp, dt: st.enter_context(nc.sbuf_tensor("t_" + name, shp, dt))
        ps = lambda name, shp, dt: st.enter_context(nc.psum_tensor("p_" + name, shp, dt))
        kf_f = sb("kf_f", [128, S], F32)
        kf_b = sb("kf_b", [128, S], BF16)
        v_f = sb("v_f", [128, 16 * HD0], F32)
        v_b = sb("v_b", [128, 16 * HD0], BF16)
        pmjd = sb("pmjd", [128, NKT * 256], F32)
        q_f = sb("q_f", [128, NSLOT * 256], F32)
        q_b = sb("q_b", [128, NSLOT * 256], BF16)
        earg = sb("earg", [128, 256], F32)
        expT = sb("expT", [128, 16 * 256], BF16)
        ones1 = sb("ones1", [128, 1], BF16)
        onesr = sb("onesr", [1, 128], F32)
        recip = sb("recip", [1, 256], F32)
        cm16 = sb("cm16", [128, 1], F32)
        recb = sb("recb", [128, 256], F32)
        aout = sb("aout", [128, 256], F32)
        psc = ps("psc", [128, 256], F32)
        pav = ps("pav", [128, 256], F32)
        psum1 = ps("psum1", [1, 256], F32)
        prb = ps("prb", [128, 256], F32)
        chain = st.enter_context(nc.semaphore("chain"))
        block = st.enter_context(nc.Block())

        ser = Ser()
        ser.dma(lambda e: e.dma_start(out=kf_f[:], in_=kf_in))
        ser.op("vector", lambda e: e.tensor_copy(kf_b[:], kf_f[:]))
        ser.dma(lambda e: e.dma_start(
            out=v_f[:].rearrange("p (kt e) -> p kt e", kt=16),
            in_=vt_in.rearrange("(kt p) e -> p kt e", p=128)))
        ser.op("vector", lambda e: e.tensor_copy(v_b[:], v_f[:]))
        ser.dma(lambda e: e.dma_start(out=pmjd[:], in_=pmjd_in))
        ser.op("vector", lambda e: e.memset(ones1[:], 1.0))
        ser.op("vector", lambda e: e.memset(onesr[:], 1.0))
        ser.op("vector", lambda e: e.memset(cm16[:], -16.0))

        for h in range(NH0):
            sl = float(slopes[h])
            ser.dma(lambda e, h=h: e.dma_start(out=q_f[:], in_=qb_in[h * 128:(h + 1) * 128, :]))
            ser.op("vector", lambda e: e.tensor_copy(q_b[:], q_f[:]))
            kt_base = 0
            for s in range(NSLOT):
                nkt = KT_OF_SLOT[s]
                for kt in range(nkt):
                    ser.op("tensor", lambda e, kt=kt, s=s: e.matmul(
                        psc[:], kf_b[:, kt * 128:(kt + 1) * 128], q_b[:, s * 256:(s + 1) * 256],
                        start=True, stop=True, skip_group_check=True))
                    col = (kt_base + kt) * 256
                    ser.op("vector", lambda e, col=col, sl=sl: e.scalar_tensor_tensor(
                        earg[:], psc[:], QSCALE / sl, pmjd[:, col:col + 256], OP.mult, OP.add))
                    ser.op("scalar", lambda e, kt=kt, sl=sl: e.activation(
                        expT[:, kt * 256:(kt + 1) * 256], earg[:], AF.Exp, bias=cm16[:, 0:1], scale=sl))
                for kt in range(nkt):
                    ser.op("tensor", lambda e, kt=kt, nkt=nkt: e.matmul(
                        pav[:], v_b[:, kt * 128:(kt + 1) * 128], expT[:, kt * 256:(kt + 1) * 256],
                        start=(kt == 0), stop=(kt == nkt - 1), skip_group_check=True))
                for kt in range(nkt):
                    ser.op("tensor", lambda e, kt=kt, nkt=nkt: e.matmul(
                        psum1[:], ones1[:], expT[:, kt * 256:(kt + 1) * 256],
                        start=(kt == 0), stop=(kt == nkt - 1), skip_group_check=True))
                ser.op("vector", lambda e: e.reciprocal(recip[:], psum1[:]))
                ser.op("tensor", lambda e: e.matmul(
                    prb[:], onesr[:], recip[:], start=True, stop=True, skip_group_check=True))
                ser.op("vector", lambda e: e.tensor_copy(recb[:], prb[:]))
                ser.op("vector", lambda e: e.tensor_mul(aout[:], pav[:], recb[:]))
                ser.dma(lambda e, h=h, s=s: e.dma_start(
                    out=ao_out[h * 128:(h + 1) * 128, s * 256:(s + 1) * 256], in_=aout[:]))
                kt_base += nkt
        ser.finish(block, chain)
    return nc


def make_pmjd(core):
    out = np.zeros((128, NKT * 256), dtype=np.float32)
    p = np.arange(128)[:, None]
    j = np.arange(256)[None, :]
    base = 0
    for s in range(NSLOT):
        i_c = 2 * s + (core % 2)
        for kt in range(KT_OF_SLOT[s]):
            v = (p - j + 128 * (kt - 2 * i_c)).astype(np.float32)
            v[v > 0] = -1e33
            out[:, (base + kt) * 256:(base + kt + 1) * 256] = v
        base += KT_OF_SLOT[s]
    return out


def run_p2(qfm, kv):
    nc = build_p2()
    in_maps = []
    for c in range(NCORES):
        b = c // 2
        blocks = [2 * s + (c % 2) for s in range(NSLOT)]
        qcols = np.concatenate(
            [qfm[:, b * S + i * 256: b * S + (i + 1) * 256] for i in blocks], axis=1)
        kvb = kv[b * S:(b + 1) * S]
        in_maps.append({
            "qb": np.ascontiguousarray(qcols),
            "kf": np.ascontiguousarray(kvb[:, :HD0].T),
            "vt": np.ascontiguousarray(kvb[:, HD0:]),
            "pmjd": make_pmjd(c),
        })
    res = run_bass_kernel_spmd(nc, in_maps, core_ids=list(range(NCORES)))
    ao = np.zeros((D, T), dtype=np.float32)
    for c in range(NCORES):
        b = c // 2
        r = res.results[c]["ao"]
        for s in range(NSLOT):
            i = 2 * s + (c % 2)
            ao[:, b * S + i * 256: b * S + (i + 1) * 256] = r[:, s * 256:(s + 1) * 256]
    return ao


def build_p3():
    nc = bass.Bass("TRN2", target_bir_lowering=False)
    ao_in = nc.dram_tensor("ao", [D, TC], F32, kind="ExternalInput").ap()
    x_in = nc.dram_tensor("x", [TC, D], F32, kind="ExternalInput").ap()
    wo_in = nc.dram_tensor("wo", [16, 128, 32, 256], F32, kind="ExternalInput").ap()
    rwt_in = nc.dram_tensor("rwt", [D, 4], F32, kind="ExternalInput").ap()
    at_in = nc.dram_tensor("at", [4, D, 16], F32, kind="ExternalInput").ap()
    bt_in = nc.dram_tensor("bt", [4, 16, D], F32, kind="ExternalInput").ap()
    idf_in = nc.dram_tensor("idf", [128, 128], F32, kind="ExternalInput").ap()
    x2_out = nc.dram_tensor("x2", [TC, D], F32, kind="ExternalOutput").ap()
    xl2_out = nc.dram_tensor("xl2", [D, TC], F32, kind="ExternalOutput").ap()
    x1_scr = nc.dram_tensor("x1_scr", [TC, D], F32, kind="Internal").ap()

    with ExitStack() as st:
        sb = lambda name, shp, dt: st.enter_context(nc.sbuf_tensor("t_" + name, shp, dt))
        ps = lambda name, shp, dt: st.enter_context(nc.psum_tensor("p_" + name, shp, dt))
        aob = sb("aob", [128, 32 * TC], BF16)
        ldf = sb("ldf", [128, 16 * 256], F32)
        wtb = sb("wtb", [128, 32 * 256], BF16)
        sx1 = sb("sx1", [128, D], F32)
        sx2 = sb("sx2", [128, D], F32)
        sq = sb("sq", [128, D], F32)
        cols = sb("cols", [128, 16], F32)
        ceps = sb("ceps", [128, 1], F32)
        xn = sb("xn", [128, D], F32)
        xnT = sb("xnT", [128, 32 * 128], BF16)
        idf = sb("idf", [128, 128], F32)
        rwtb = sb("rwtb", [128, 32 * 4], BF16)
        atb = [sb(f"atb{i}", [128, 32 * 16], BF16) for i in range(4)]
        btb = [sb(f"btb{i}", [16, D], BF16) for i in range(4)]
        lg = sb("lg", [128, 4], F32)
        gsc = sb("gsc", [128, 16], F32)
        gates = sb("gates", [128, 4], F32)
        gbc = sb("gbc", [16, 128], F32)
        wmid4 = sb("wmid4", [16, 4 * 128], BF16)
        ones16 = sb("ones16", [1, 16], F32)
        pcc = [ps(f"pcc{i}", [128, 256], F32) for i in range(8)]
        chain = st.enter_context(nc.semaphore("chain"))
        block = st.enter_context(nc.Block())
        ptrf = pcc[7][:, 0:128]
        plg = pcc[0][:, 0:4]
        pg1 = pcc[6][0:1, 0:128]
        pgb = pcc[5][0:16, 0:128]
        pmid = pcc[4][0:16, 0:128]
        pdl = pcc[1]

        gT = ldf[0:1, 3584:4096]
        ser = Ser()
        ser.dma(lambda e: e.dma_start(out=idf[:], in_=idf_in))
        ser.op("vector", lambda e: e.memset(ones16[:], 1.0))
        ser.op("vector", lambda e: e.memset(ceps[:], LN_EPS))
        for fi in range(32):
            ser.dma(lambda e, fi=fi: e.dma_start(out=ldf[:, 0:TC], in_=ao_in[fi * 128:(fi + 1) * 128, :]))
            ser.op("vector", lambda e, fi=fi: e.tensor_copy(aob[:, fi * TC:(fi + 1) * TC], ldf[:, 0:TC]))
        for fog in range(8):
            for half in range(2):
                cb = fog * 2 + half
                for hq in range(2):
                    ser.dma(lambda e, cb=cb, hq=hq: e.dma_start(
                        out=ldf[:].rearrange("p (f c) -> p f c", f=16),
                        in_=wo_in[cb][:, hq * 16:(hq + 1) * 16, :]))
                    ser.op("vector", lambda e, hq=hq: e.tensor_copy(
                        wtb[:, hq * 4096:(hq + 1) * 4096], ldf[:]))
                for fi in range(32):
                    for tt in range(NT):
                        pv = pcc[tt][:]
                        ser.op("tensor", lambda e, fi=fi, tt=tt, pv=pv: e.matmul(
                            pv, aob[:, fi * TC + tt * 128: fi * TC + (tt + 1) * 128],
                            wtb[:, fi * 256:(fi + 1) * 256],
                            start=(fi == 0), stop=(fi == 31), skip_group_check=True))
                xr_v = ldf[:, 3072:3328]
                ev_v = ldf[:, 3328:3584]
                for tt in range(NT):
                    co = fog * 512 + half * 256
                    pv = pcc[tt][:]
                    ser.dma(lambda e, tt=tt, co=co, xr_v=xr_v: e.dma_start(
                        out=xr_v, in_=x_in[tt * 128:(tt + 1) * 128, co:co + 256]))
                    ser.op("vector", lambda e, pv=pv, xr_v=xr_v, ev_v=ev_v: e.tensor_add(ev_v, pv, xr_v))
                    ser.dma(lambda e, tt=tt, co=co, ev_v=ev_v: e.dma_start(
                        out=x1_scr[tt * 128:(tt + 1) * 128, co:co + 256], in_=ev_v))
        for fi in range(32):
            ser.dma(lambda e, fi=fi: e.dma_start(out=ldf[:, 0:4], in_=rwt_in[fi * 128:(fi + 1) * 128, :]))
            ser.op("vector", lambda e, fi=fi: e.tensor_copy(rwtb[:, fi * 4:(fi + 1) * 4], ldf[:, 0:4]))
        for ei in range(4):
            ser.dma(lambda e, ei=ei: e.dma_start(
                out=sq[:, 0:512].rearrange("p (f r) -> p f r", f=32),
                in_=at_in[ei].rearrange("(f p) r -> p f r", p=128)))
            ser.op("vector", lambda e, ei=ei: e.tensor_copy(atb[ei][:], sq[:, 0:512]))
            ser.dma(lambda e, ei=ei: e.dma_start(out=ldf[0:16, 0:D], in_=bt_in[ei]))
            ser.op("vector", lambda e, ei=ei: e.tensor_copy(btb[ei][:], ldf[0:16, 0:D]))
        for tt in range(NT):
            ser.dma(lambda e, tt=tt: e.dma_start(out=sx1[:], in_=x1_scr[tt * 128:(tt + 1) * 128, :]))
            _ln_ops(ser, sx1[:], xn[:], sq[:], cols, D, ceps[:, 0:1])
            for fi in range(32):
                ser.op("tensor", lambda e, fi=fi: e.transpose(ptrf, xn[:, fi * 128:(fi + 1) * 128], idf[:]))
                ser.op("vector", lambda e, fi=fi: e.tensor_copy(xnT[:, fi * 128:(fi + 1) * 128], ptrf))
            for fi in range(32):
                ser.op("tensor", lambda e, fi=fi: e.matmul(
                    plg, xnT[:, fi * 128:(fi + 1) * 128], rwtb[:, fi * 4:(fi + 1) * 4],
                    start=(fi == 0), stop=(fi == 31), skip_group_check=True))
            ser.op("vector", lambda e: e.tensor_copy(lg[:], plg))
            c0, c1, c2, c3, c4 = (cols[:, 8 + i:9 + i] for i in range(5))
            ser.op("vector", lambda e: e.reduce_max(c0, lg[:], AX))
            ser.op("vector", lambda e: e.tensor_scalar(gsc[:, 0:4], lg[:], c0, None, OP.is_equal))
            ser.op("vector", lambda e: e.scalar_tensor_tensor(gsc[:, 4:8], gsc[:, 0:4], -1e30, lg[:], OP.mult, OP.add))
            ser.op("vector", lambda e: e.reduce_max(c1, gsc[:, 4:8], AX))
            ser.op("vector", lambda e: e.tensor_scalar(gsc[:, 8:12], lg[:], c1, None, OP.is_ge))
            ser.op("vector", lambda e: e.tensor_scalar(c2, c0, -1.0, None, OP.mult))
            ser.op("scalar", lambda e: e.activation(gsc[:, 12:16], lg[:], AF.Exp, bias=c2))
            ser.op("vector", lambda e: e.tensor_mul(gsc[:, 12:16], gsc[:, 12:16], gsc[:, 8:12]))
            ser.op("vector", lambda e: e.reduce_sum(c3, gsc[:, 12:16], AX))
            ser.op("vector", lambda e: e.reciprocal(c4, c3))
            ser.op("vector", lambda e: e.tensor_scalar(gates[:], gsc[:, 12:16], c4, None, OP.mult))
            for ei in range(4):
                ser.op("tensor", lambda e, ei=ei: e.transpose(pg1, gates[:, ei:ei + 1], idf[:]))
                ser.op("vector", lambda e, ei=ei: e.tensor_copy(gT[0:1, ei * 128:(ei + 1) * 128], pg1))
            for ei in range(4):
                for fi in range(32):
                    ser.op("tensor", lambda e, ei=ei, fi=fi: e.matmul(
                        pmid, atb[ei][:, fi * 16:(fi + 1) * 16], xnT[:, fi * 128:(fi + 1) * 128],
                        start=(fi == 0), stop=(fi == 31), skip_group_check=True))
                ser.op("tensor", lambda e, ei=ei: e.matmul(
                    pgb, ones16[:], gT[0:1, ei * 128:(ei + 1) * 128],
                    start=True, stop=True, skip_group_check=True))
                ser.op("vector", lambda e: e.tensor_copy(gbc[:], pgb))
                ser.op("vector", lambda e, ei=ei: e.tensor_mul(wmid4[:, ei * 128:(ei + 1) * 128], pmid, gbc[:]))
            for og in range(16):
                for ei in range(4):
                    ser.op("tensor", lambda e, og=og, ei=ei: e.matmul(
                        pdl[:], wmid4[:, ei * 128:(ei + 1) * 128], btb[ei][:, og * 256:(og + 1) * 256],
                        start=(ei == 0), stop=(ei == 3), skip_group_check=True))
                ser.op("vector", lambda e, og=og: e.scalar_tensor_tensor(
                    sx2[:, og * 256:(og + 1) * 256], pdl[:], 4.0, sx1[:, og * 256:(og + 1) * 256],
                    OP.mult, OP.add))
                ser.dma(lambda e, tt=tt, og=og: e.dma_start(
                    out=x2_out[tt * 128:(tt + 1) * 128, og * 256:(og + 1) * 256],
                    in_=sx2[:, og * 256:(og + 1) * 256]))
            _ln_ops(ser, sx2[:], xn[:], sq[:], cols, D, ceps[:, 0:1])
            for fi in range(32):
                ser.op("tensor", lambda e, fi=fi: e.transpose(ptrf, xn[:, fi * 128:(fi + 1) * 128], idf[:]))
                ser.op("vector", lambda e: e.tensor_copy(ldf[:, 3072:3200], ptrf))
                ser.dma(lambda e, tt=tt, fi=fi: e.dma_start(
                    out=xl2_out[fi * 128:(fi + 1) * 128, tt * 128:(tt + 1) * 128], in_=ldf[:, 3072:3200]))
        ser.finish(block, chain)
    return nc


def run_p3(ao, x_full, wo, router_w, A, Bm):
    nc = build_p3()
    idf = np.eye(128, dtype=np.float32)
    wo_p = np.ascontiguousarray(wo.reshape(32, 128, 16, 256).transpose(2, 1, 0, 3))
    rwt = np.ascontiguousarray(router_w.T)
    at = np.ascontiguousarray(A.transpose(0, 2, 1))
    bt = np.ascontiguousarray(Bm.transpose(0, 2, 1))
    in_maps = [{"ao": np.ascontiguousarray(ao[:, c * TC:(c + 1) * TC]),
                "x": np.ascontiguousarray(x_full[c * TC:(c + 1) * TC]),
                "wo": wo_p, "rwt": rwt, "at": at, "bt": bt, "idf": idf}
               for c in range(NCORES)]
    res = run_bass_kernel_spmd(nc, in_maps, core_ids=list(range(NCORES)))
    x2 = np.concatenate([res.results[c]["x2"] for c in range(NCORES)], axis=0)
    xl2 = np.concatenate([res.results[c]["xl2"] for c in range(NCORES)], axis=1)
    return x2, xl2


def build_p4():
    nc = bass.Bass("TRN2", target_bir_lowering=False)
    xl_in = nc.dram_tensor("xl", [D, TC], F32, kind="ExternalInput").ap()
    halo_in = nc.dram_tensor("halo", [D, 1], F32, kind="ExternalInput").ap()
    w_ins = [nc.dram_tensor(f"w{i}", [32, 128, 32, 128], F32, kind="ExternalInput").ap() for i in range(4)]
    tm_in = nc.dram_tensor("tm", [D, 4], F32, kind="ExternalInput").ap()
    outs = [nc.dram_tensor(nm, [D, TC], BF16, kind="ExternalOutput").ap()
            for nm in ("kfm", "vfm", "rfm", "gfm")]

    with ExitStack() as st:
        sb = lambda name, shp, dt: st.enter_context(nc.sbuf_tensor("t_" + name, shp, dt))
        ps = lambda name, shp, dt: st.enter_context(nc.psum_tensor("p_" + name, shp, dt))
        ext = sb("ext", [128, TC + 1], F32)
        mf32 = sb("mf32", [128, TC], F32)
        mixf = sb("mixf", [128, 32 * TC], BF16)
        tmv = sb("tmv", [128, 4 * 32], F32)
        tm1 = sb("tm1", [128, 4 * 32], F32)
        wtf = sb("wtf", [128, 32 * 128], F32)
        wtb = sb("wtb", [128, 32 * 128], BF16)
        evb = sb("evb", [128, 512], BF16)
        pf0 = ps("pf0", [128, 512], F32)
        pf1 = ps("pf1", [128, 512], F32)
        chain = st.enter_context(nc.semaphore("chain"))
        block = st.enter_context(nc.Block())

        ser = Ser()
        ser.dma(lambda e: e.dma_start(
            out=tmv[:].rearrange("p (f m) -> p f m", f=32),
            in_=tm_in.rearrange("(f p) m -> p f m", p=128)))
        ser.op("vector", lambda e: e.tensor_scalar(tm1[:], tmv[:], -1.0, 1.0, OP.mult, OP.add))
        for pj in range(4):
            for fi in range(32):
                ser.dma(lambda e, fi=fi: e.dma_start(
                    out=ext[:, 1:TC + 1], in_=xl_in[fi * 128:(fi + 1) * 128, :]))
                ser.dma(lambda e, fi=fi: e.dma_start(
                    out=ext[:, 0:1], in_=halo_in[fi * 128:(fi + 1) * 128, :]))
                mcol = tmv[:, fi * 4 + pj: fi * 4 + pj + 1]
                m1col = tm1[:, fi * 4 + pj: fi * 4 + pj + 1]
                ser.op("vector", lambda e, mcol=mcol: e.tensor_scalar(
                    mf32[:], ext[:, 1:TC + 1], mcol, None, OP.mult))
                ser.op("vector", lambda e, fi=fi, m1col=m1col: e.scalar_tensor_tensor(
                    mixf[:, fi * TC:(fi + 1) * TC], ext[:, 0:TC], m1col, mf32[:], OP.mult, OP.add))
            for fo in range(32):
                ser.dma(lambda e, pj=pj, fo=fo: e.dma_start(
                    out=wtf[:].rearrange("p (f c) -> p f c", f=32), in_=w_ins[pj][fo]))
                ser.op("vector", lambda e: e.tensor_copy(wtb[:], wtf[:]))
                for fi in range(32):
                    for nh in range(2):
                        pq = pf0 if nh == 0 else pf1
                        ser.op("tensor", lambda e, fi=fi, nh=nh, pq=pq: e.matmul(
                            pq[:], wtb[:, fi * 128:(fi + 1) * 128],
                            mixf[:, fi * TC + nh * 512: fi * TC + (nh + 1) * 512],
                            start=(fi == 0), stop=(fi == 31), skip_group_check=True))
                for nh in range(2):
                    pq = pf0 if nh == 0 else pf1
                    if pj == 3:
                        ser.op("scalar", lambda e, pq=pq: e.activation(evb[:], pq[:], AF.Silu))
                    else:
                        ser.op("vector", lambda e, pq=pq: e.tensor_copy(evb[:], pq[:]))
                    ser.dma(lambda e, pj=pj, fo=fo, nh=nh: e.dma_start(
                        out=outs[pj][fo * 128:(fo + 1) * 128, nh * 512:(nh + 1) * 512], in_=evb[:]))
        ser.finish(block, chain)
    return nc


def run_p4(xl2, w_key, w_value, w_receptance, w_gate, tmk, tmv_, tmr, tmg):
    nc = build_p4()
    tm = np.ascontiguousarray(np.stack([tmk, tmv_, tmr, tmg], axis=1))
    ws = [np.ascontiguousarray(w.T.reshape(32, 128, 32, 128).transpose(2, 1, 0, 3))
          for w in (w_key, w_value, w_receptance, w_gate)]
    in_maps = []
    for c in range(NCORES):
        t0 = c * TC
        halo = np.zeros((D, 1), dtype=np.float32) if t0 % S == 0 else \
            np.ascontiguousarray(xl2[:, t0 - 1: t0])
        in_maps.append({"xl": np.ascontiguousarray(xl2[:, t0:t0 + TC]), "halo": halo,
                        "w0": ws[0], "w1": ws[1], "w2": ws[2], "w3": ws[3], "tm": tm})
    res = run_bass_kernel_spmd(nc, in_maps, core_ids=list(range(NCORES)))
    mats = []
    for nm in ("kfm", "vfm", "rfm", "gfm"):
        mats.append(np.concatenate([res.results[c][nm] for c in range(NCORES)], axis=1))
    return mats


def build_p5():
    nc = bass.Bass("TRN2", target_bir_lowering=False)
    kf_in = nc.dram_tensor("kf", [FPC, T], BF16, kind="ExternalInput").ap()
    rf_in = nc.dram_tensor("rf", [FPC, T], BF16, kind="ExternalInput").ap()
    kt_in = nc.dram_tensor("kt", [T, FPC], BF16, kind="ExternalInput").ap()
    vt_in = nc.dram_tensor("vt", [T, FPC], BF16, kind="ExternalInput").ap()
    dft_in = nc.dram_tensor("dft", [4, 128], F32, kind="ExternalInput").ap()
    fcol_in = nc.dram_tensor("fcol", [128, 4], F32, kind="ExternalInput").ap()
    d8f_in = nc.dram_tensor("d8f", [HPC, 128], F32, kind="ExternalInput").ap()
    mkf_in = nc.dram_tensor("mkf", [HPC, 128], F32, kind="ExternalInput").ap()
    gnw_in = nc.dram_tensor("gnw", [HPC, HD1], F32, kind="ExternalInput").ap()
    gnb_in = nc.dram_tensor("gnb", [HPC, HD1], F32, kind="ExternalInput").ap()
    ind8p_in = nc.dram_tensor("ind8p", [HPC, 16 * HD1], F32, kind="ExternalInput").ap()
    ind8b_in = nc.dram_tensor("ind8b", [HPC, NU * CH], F32, kind="ExternalInput").ap()
    taucmp_in = nc.dram_tensor("taucmp", [1, 128], F32, kind="ExternalInput").ap()
    taud_in = nc.dram_tensor("taud", [1, 128], F32, kind="ExternalInput").ap()
    taukd_in = nc.dram_tensor("taukd", [1, 128], F32, kind="ExternalInput").ap()
    taux_in = nc.dram_tensor("taux", [1, 64], F32, kind="ExternalInput").ap()
    taukx_in = nc.dram_tensor("taukx", [1, 64], F32, kind="ExternalInput").ap()
    tauir_in = nc.dram_tensor("tauir", [1, 128], F32, kind="ExternalInput").ap()
    tau31_in = nc.dram_tensor("tau31", [1, 32], F32, kind="ExternalInput").ap()
    msu_in = nc.dram_tensor("msu", [CH, NU * CH], F32, kind="ExternalInput").ap()
    mid_in = nc.dram_tensor("mid", [CH, NU * CH], F32, kind="ExternalInput").ap()
    xn_out = nc.dram_tensor("xno", [FPC, T], BF16, kind="ExternalOutput").ap()

    kf_r = kf_in.rearrange("(ft p) (b cc tau) -> ft cc p b tau", ft=4, b=4, tau=CH)
    rf_r = rf_in.rearrange("(ft p) (b cc tau) -> ft cc p b tau", ft=4, b=4, tau=CH)
    kt_r = kt_in.rearrange("(b cc tau) f -> cc tau b f", b=4, tau=CH)
    vt_r = vt_in.rearrange("(b cc tau) f -> cc tau b f", b=4, tau=CH)
    xo_r = xn_out.rearrange("(h e) (b cc tau) -> h cc e b tau", h=HPC, b=4, tau=CH)

    with ExitStack() as st:
        sb = lambda name, shp, dt: st.enter_context(nc.sbuf_tensor("t_" + name, shp, dt))
        ps = lambda name, shp, dt: st.enter_context(nc.psum_tensor("p_" + name, shp, dt))
        s_ft = [sb(f"sft{i}", [1, 128], F32) for i in range(4)]
        fcol = sb("fcol", [128, 4], F32)
        s8f = sb("s8f", [HPC, 128], F32)
        mskf = sb("mskf", [HPC, 128], F32)
        srep = sb("srep", [1, 4 * FPC], F32)
        tard = sb("tard", [128, 512], BF16)
        takd = sb("takd", [128, 512], BF16)
        tarx = sb("tarx", [128, 256], BF16)
        takx = sb("takx", [128, 256], BF16)
        tir = sb("tir", [128, 512], BF16)
        tfb = sb("tfb", [128, 512], BF16)
        tkh = sb("tkh", [CH, 4 * FPC], BF16)
        tdec = sb("tdec", [128, 16 * HD1], F32)
        gnwb = sb("gnwb", [64, NU * CH], F32)
        gnbb = sb("gnbb", [64, NU * CH], F32)
        msu = sb("msu", [CH, NU * CH], F32)
        midm = sb("midm", [CH, NU * CH], F32)
        taus = {}
        for nm, w in (("taud", 128), ("taukd", 128), ("taux", 64), ("taukx", 64),
                      ("tauir", 128), ("tau31", 32), ("taucmp", 128)):
            taus[nm] = sb("s_" + nm, [1, w], F32)
        clampf = sb("clampf", [128, 128], F32)
        ind8p = sb("ind8p", [HPC, 16 * HD1], F32)
        ind8b = sb("ind8b", [HPC, NU * CH], F32)
        gnw8 = sb("gnw8", [HPC, HD1], F32)
        gnb8 = sb("gnb8", [HPC, HD1], F32)
        ones64 = sb("ones64", [64, 1], F32)
        ones1r = sb("ones1r", [1, 64], F32)
        kc = sb("kc", [128, 512], BF16)
        rc = sb("rc", [128, 512], BF16)
        ktc = sb("ktc", [CH, 4 * FPC], BF16)
        vtc = sb("vtc", [CH, 4 * FPC], BF16)
        kdtf = sb("kdtf", [128, 1024], BF16)
        rtd = sb("rtd", [128, 512], BF16)
        rx = sb("rx", [128, 256], BF16)
        kx = sb("kx", [128, 256], BF16)
        ri = sb("ri", [128, 512], BF16)
        kh = sb("kh", [CH, 4 * FPC], BF16)
        msb = sb("msb", [CH, NU * CH], BF16)
        clA = sb("clA", [CH, NU * CH], F32)
        msc = sb("msc", [CH, NU * CH], BF16)
        sf32 = sb("sf32", [128, 16 * HD1], F32)
        sbf = sb("sbf", [128, 16 * HD1], BF16)
        osb = sb("osb", [64, NU * CH], F32)
        osq = sb("osq", [64, NU * CH], F32)
        t1 = sb("t1", [64, NU * CH], F32)
        xnb = sb("xnb", [64, NU * CH], BF16)
        mrow = sb("mrow", [1, NU * CH], F32)
        mrow2 = sb("mrow2", [1, NU * CH], F32)
        mrow3 = sb("mrow3", [1, NU * CH], F32)
        cgn = sb("cgn", [128, 1], F32)
        PA = ps("PA", [64, 1024], F32)
        PX = ps("PX", [16, 512], F32)
        PO = ps("PO", [64, 1024], F32)
        PS = ps("PS", [128, 512], F32)
        PT = ps("PT", [128, 512], F32)
        chain = st.enter_context(nc.semaphore("chain"))
        block = st.enter_context(nc.Block())

        ser = Ser()
        for nm, inp in (("taud", taud_in), ("taukd", taukd_in), ("taux", taux_in),
                        ("taukx", taukx_in), ("tauir", tauir_in), ("tau31", tau31_in),
                        ("taucmp", taucmp_in)):
            ser.dma(lambda e, nm=nm, inp=inp: e.dma_start(out=taus[nm][:], in_=inp))
        ser.dma(lambda e: e.dma_start(out=msu[:], in_=msu_in))
        ser.dma(lambda e: e.dma_start(out=midm[:], in_=mid_in))
        ser.dma(lambda e: e.dma_start(out=ind8p[:], in_=ind8p_in))
        ser.dma(lambda e: e.dma_start(out=ind8b[:], in_=ind8b_in))
        ser.dma(lambda e: e.dma_start(out=gnw8[:], in_=gnw_in))
        ser.dma(lambda e: e.dma_start(out=gnb8[:], in_=gnb_in))
        ser.dma(lambda e: e.dma_start(out=s8f[:], in_=d8f_in))
        ser.dma(lambda e: e.dma_start(out=mskf[:], in_=mkf_in))
        ser.dma(lambda e: e.dma_start(out=fcol[:], in_=fcol_in))
        ser.op("vector", lambda e: e.memset(ones64[:], 1.0))
        ser.op("vector", lambda e: e.memset(ones1r[:], 1.0))
        ser.op("vector", lambda e: e.memset(sf32[:], 0.0))
        ser.op("vector", lambda e: e.memset(sbf[:], 0.0))
        ser.op("vector", lambda e: e.memset(cgn[:], 1e-5))
        for ft in range(4):
            ser.dma(lambda e, ft=ft: e.dma_start(out=s_ft[ft][:], in_=dft_in[ft:ft + 1, :]))
            ser.op("scalar", lambda e, ft=ft: e.activation(s_ft[ft][:], s_ft[ft][:], AF.Exp))
            ser.op("vector", lambda e, ft=ft: e.tensor_scalar(s_ft[ft][:], s_ft[ft][:], -1.0, None, OP.mult))
            ser.op("vector", lambda e, ft=ft: e.tensor_copy(srep[0:1, ft * 128:(ft + 1) * 128], s_ft[ft][:]))
        ser.op("scalar", lambda e: e.activation(s8f[:], s8f[:], AF.Exp))
        ser.op("vector", lambda e: e.tensor_scalar(s8f[:], s8f[:], -1.0, None, OP.mult))
        ser.op("vector", lambda e: e.tensor_mul(s8f[:], s8f[:], mskf[:]))
        for b_ in range(1, 4):
            ser.op("vector", lambda e, b_=b_: e.tensor_copy(
                srep[0:1, b_ * FPC:(b_ + 1) * FPC], srep[0:1, 0:FPC]))
        def mk_tab(dst, tau_nm, width, clamp, post_tf=False):
            for ft in range(4):
                ser.op("tensor", lambda e, ft=ft, tau_nm=tau_nm, width=width: e.matmul(
                    PT[:, 0:width], s_ft[ft][:], taus[tau_nm][:],
                    start=True, stop=True, skip_group_check=True))
                if clamp:
                    ser.op("vector", lambda e, width=width: e.tensor_scalar(
                        clampf[:, 0:width], PT[:, 0:width], 75.0, -75.0, OP.min, OP.max))
                else:
                    ser.op("vector", lambda e, width=width: e.tensor_copy(
                        clampf[:, 0:width], PT[:, 0:width]))
                if post_tf:
                    ser.op("scalar", lambda e, width=width: e.activation(
                        clampf[:, 0:width], clampf[:, 0:width], AF.Exp))
                    ser.op("vector", lambda e, ft=ft, width=width: e.tensor_scalar(
                        dst[:, ft * width:(ft + 1) * width], clampf[:, 0:width],
                        fcol[:, ft:ft + 1], None, OP.mult))
                else:
                    ser.op("scalar", lambda e, ft=ft, width=width: e.activation(
                        dst[:, ft * width:(ft + 1) * width], clampf[:, 0:width], AF.Exp))
        mk_tab(tard, "taud", 128, True)
        mk_tab(takd, "taukd", 128, True)
        mk_tab(tarx, "taux", 64, False)
        mk_tab(takx, "taukx", 64, False)
        mk_tab(tir, "tauir", 128, False)
        mk_tab(tfb, "taucmp", 128, True, post_tf=True)
        for q in range(4):
            ser.op("tensor", lambda e, q=q: e.matmul(
                PT[0:CH, 0:512], taus["tau31"][:], srep[0:1, q * 512:(q + 1) * 512],
                start=True, stop=True, skip_group_check=True))
            ser.op("scalar", lambda e, q=q: e.activation(
                tkh[:, q * 512:(q + 1) * 512], PT[0:CH, 0:512], AF.Exp))
        for q in range(2):
            ser.op("tensor", lambda e, q=q: e.matmul(
                PT[:, 0:512], s8f[:], ind8p[:, q * 512:(q + 1) * 512],
                start=True, stop=True, skip_group_check=True))
            ser.op("scalar", lambda e, q=q: e.activation(
                tdec[:, q * 512:(q + 1) * 512], PT[:, 0:512], AF.Exp, scale=float(CH)))
        for q in range(2):
            ser.op("tensor", lambda e, q=q: e.matmul(
                PT[0:64, 0:512], gnw8[:], ind8b[:, q * 512:(q + 1) * 512],
                start=True, stop=True, skip_group_check=True))
            ser.op("vector", lambda e, q=q: e.tensor_copy(gnwb[:, q * 512:(q + 1) * 512], PT[0:64, 0:512]))
            ser.op("tensor", lambda e, q=q: e.matmul(
                PT[0:64, 0:512], gnb8[:], ind8b[:, q * 512:(q + 1) * 512],
                start=True, stop=True, skip_group_check=True))
            ser.op("vector", lambda e, q=q: e.tensor_copy(gnbb[:, q * 512:(q + 1) * 512], PT[0:64, 0:512]))

        kc_v = kc[:].rearrange("p (g t) -> p g t", t=CH)
        rc_v = rc[:].rearrange("p (g t) -> p g t", t=CH)
        kdtf_v = kdtf[:].rearrange("p (g x) -> p g x", x=64)
        tard_v = tard[:].rearrange("p (g t) -> p g t", t=CH)
        takd_v = takd[:].rearrange("p (g t) -> p g t", t=CH)
        tfb_v = tfb[:].rearrange("p (g t) -> p g t", t=CH)
        tarx_v = tarx[:].rearrange("p (g t) -> p g t", t=16)
        takx_v = takx[:].rearrange("p (g t) -> p g t", t=16)
        rx_v = rx[:].rearrange("p (g t) -> p g t", t=16)
        kx_v = kx[:].rearrange("p (g t) -> p g t", t=16)

        for c in range(NCH):
            for ft_ in range(4):
                ser.dma(lambda e, c=c, ft_=ft_: e.dma_start(
                    out=kc[:, ft_ * 128:(ft_ + 1) * 128].rearrange("p (b tau) -> p b tau", b=4),
                    in_=kf_r[ft_, c]))
                ser.dma(lambda e, c=c, ft_=ft_: e.dma_start(
                    out=rc[:, ft_ * 128:(ft_ + 1) * 128].rearrange("p (b tau) -> p b tau", b=4),
                    in_=rf_r[ft_, c]))
            ser.dma(lambda e, c=c: e.dma_start(
                out=ktc[:].rearrange("t (b f) -> t b f", b=4), in_=kt_r[c]))
            ser.dma(lambda e, c=c: e.dma_start(
                out=vtc[:].rearrange("t (b f) -> t b f", b=4), in_=vt_r[c]))
            ser.op("vector", lambda e: e.tensor_mul(rtd[:], rc[:], tard[:]))
            ser.op("vector", lambda e: e.tensor_tensor(kdtf_v[:, :, 0:CH], kc_v, takd_v, OP.mult))
            ser.op("vector", lambda e: e.tensor_tensor(kdtf_v[:, :, CH:64], kc_v, tfb_v, OP.mult))
            ser.op("vector", lambda e: e.tensor_tensor(rx_v, rc_v[:, :, 16:CH], tarx_v, OP.mult))
            ser.op("vector", lambda e: e.tensor_tensor(kx_v, kc_v[:, :, 0:16], takx_v, OP.mult))
            ser.op("vector", lambda e: e.tensor_mul(ri[:], rc[:], tir[:]))
            ser.op("vector", lambda e: e.tensor_mul(kh[:], ktc[:], tkh[:]))
            for u in range(NU):
                h, b_ = u // 4, u % 4
                ft, pr = h // 2, (h % 2) * 64
                g = ft * 4 + b_
                ser.op("tensor", lambda e, u=u, pr=pr, g=g: e.matmul(
                    PA[:, u * CH:(u + 1) * CH],
                    kdtf[pr:pr + 64, g * 64:(g + 1) * 64],
                    rtd[pr:pr + 64, g * CH:(g + 1) * CH],
                    start=True, stop=True, skip_group_check=True))
                ser.op("tensor", lambda e, u=u, pr=pr, g=g: e.matmul(
                    PX[:, u * 16:(u + 1) * 16],
                    kx[pr:pr + 64, g * 16:(g + 1) * 16],
                    rx[pr:pr + 64, g * 16:(g + 1) * 16],
                    start=True, stop=True, skip_group_check=True))
            ser.op("vector", lambda e: e.tensor_scalar(clA[:], PA[0:CH, :], 1e30, -1e30, OP.min, OP.max))
            ser.op("vector", lambda e: e.tensor_mul(msb[:], clA[:], msu[:]))
            ser.op("vector", lambda e: e.tensor_scalar(clA[:], PA[CH:64, :], 1e30, -1e30, OP.min, OP.max))
            ser.op("vector", lambda e: e.tensor_mul(msc[:], clA[:], midm[:]))
            ser.op("vector", lambda e: e.tensor_add(msb[:], msb[:], msc[:]))
            msb_x = msb[0:16, :].rearrange("p (u t) -> p u t", t=CH)[:, :, 16:CH]
            px_v = PX[:].rearrange("p (u t) -> p u t", t=16)
            ser.op("vector", lambda e: e.tensor_add(msb_x, msb_x, px_v))
            for u in range(NU):
                h, b_ = u // 4, u % 4
                ft, pr = h // 2, (h % 2) * 64
                g = ft * 4 + b_
                fo = b_ * FPC + h * 64
                ser.op("tensor", lambda e, u=u, fo=fo: e.matmul(
                    PO[:, u * CH:(u + 1) * CH],
                    vtc[:, fo:fo + 64], msb[:, u * CH:(u + 1) * CH],
                    start=True, stop=False, skip_group_check=True))
                ser.op("tensor", lambda e, u=u, pr=pr, g=g: e.matmul(
                    PO[:, u * CH:(u + 1) * CH],
                    sbf[pr:pr + 64, g * 64:(g + 1) * 64], ri[pr:pr + 64, g * CH:(g + 1) * CH],
                    start=False, stop=True, skip_group_check=True))
            for half in range(2):
                for ui in range(16):
                    u = half * 16 + ui
                    h, b_ = u // 4, u % 4
                    hh = h % 2
                    g = (h // 2) * 4 + b_
                    fo = b_ * FPC + h * 64
                    ser.op("tensor", lambda e, hh=hh, g=g, half=half, fo=fo: e.matmul(
                        PS[hh * 64:(hh + 1) * 64, (g - half * 8) * 64:(g - half * 8 + 1) * 64],
                        kh[:, fo:fo + 64], vtc[:, fo:fo + 64],
                        start=True, stop=True, skip_group_check=True))
                sl = slice(half * 512, (half + 1) * 512)
                ser.op("vector", lambda e, sl=sl: e.tensor_mul(sf32[:, sl], sf32[:, sl], tdec[:, sl]))
                ser.op("vector", lambda e, sl=sl: e.tensor_add(sf32[:, sl], sf32[:, sl], PS[:]))
            ser.op("vector", lambda e: e.tensor_copy(osb[:], PO[:]))
            ser.op("scalar", lambda e: e.activation(osq[:], osb[:], AF.Square))
            for q in range(2):
                ser.op("tensor", lambda e, q=q: e.matmul(
                    PA[0:1, q * 512:(q + 1) * 512], ones64[:], osb[:, q * 512:(q + 1) * 512],
                    start=True, stop=True, skip_group_check=True))
                ser.op("tensor", lambda e, q=q: e.matmul(
                    PO[0:1, q * 512:(q + 1) * 512], ones64[:], osq[:, q * 512:(q + 1) * 512],
                    start=True, stop=True, skip_group_check=True))
            ser.op("vector", lambda e: e.tensor_scalar(mrow[:], PA[0:1, :], 1.0 / 64, None, OP.mult))
            ser.op("vector", lambda e: e.tensor_scalar(mrow2[:], PO[0:1, :], 1.0 / 64, None, OP.mult))
            ser.op("vector", lambda e: e.tensor_mul(mrow3[:], mrow[:], mrow[:]))
            ser.op("vector", lambda e: e.tensor_sub(mrow2[:], mrow2[:], mrow3[:]))
            ser.op("scalar", lambda e: e.activation(mrow2[:], mrow2[:], AF.Ln, bias=cgn[0:1, 0:1]))
            ser.op("scalar", lambda e: e.activation(mrow2[:], mrow2[:], AF.Exp, scale=-0.5))
            for q in range(2):
                qs = slice(q * 512, (q + 1) * 512)
                ser.op("tensor", lambda e, q=q, qs=qs: e.matmul(
                    PA[0:64, qs], ones1r[:], mrow[0:1, qs],
                    start=True, stop=True, skip_group_check=True))
                ser.op("tensor", lambda e, q=q, qs=qs: e.matmul(
                    PT[0:64, 0:512], ones1r[:], mrow2[0:1, qs],
                    start=True, stop=True, skip_group_check=True))
                ser.op("vector", lambda e, qs=qs: e.tensor_sub(t1[:, qs], osb[:, qs], PA[0:64, qs]))
                ser.op("vector", lambda e, qs=qs: e.tensor_mul(t1[:, qs], t1[:, qs], PT[0:64, 0:512]))
            ser.op("vector", lambda e: e.tensor_mul(t1[:], t1[:], gnwb[:]))
            ser.op("vector", lambda e: e.tensor_add(xnb[:], t1[:], gnbb[:]))
            for h_ in range(HPC):
                ser.dma(lambda e, c=c, h_=h_: e.dma_start(
                    out=xo_r[h_, c],
                    in_=xnb[:, h_ * 128:(h_ + 1) * 128].rearrange("e (b tau) -> e b tau", b=4)))
            ser.op("vector", lambda e: e.tensor_copy(sbf[:], sf32[:]))
        ser.finish(block, chain)
    return nc


def run_p5(kfm, vfm, rfm, time_decay, time_faaaa, gn_weight, gn_bias):
    nc = build_p5()
    ktm = np.ascontiguousarray(kfm.T)
    vtm = np.ascontiguousarray(vfm.T)
    tau = np.arange(128) % CH
    tau16 = tau % 16
    taud = (tau16 - 8).astype(np.float32)[None, :]
    taukd = (7 - tau16).astype(np.float32)[None, :]
    taucmp = (8 - tau16).astype(np.float32)[None, :]
    tauir = tau.astype(np.float32)[None, :]
    tx = np.arange(64) % 16
    taux = tx.astype(np.float32)[None, :]
    taukx = (15 - tx).astype(np.float32)[None, :]
    tau31 = (31 - np.arange(CH)).astype(np.float32)[None, :]
    ind8p = np.zeros((HPC, 16 * HD1), dtype=np.float32)
    for h in range(HPC):
        for g in range(16):
            if g // 4 == h // 2:
                ind8p[h, g * HD1:(g + 1) * HD1] = 1.0
    ind8b = np.zeros((HPC, NU * CH), dtype=np.float32)
    for h in range(HPC):
        for u in range(NU):
            if u // 4 == h:
                ind8b[h, u * CH:(u + 1) * CH] = 1.0
    tp = np.arange(CH)[:, None]
    tq = np.arange(CH)[None, :]
    msu1 = ((tp < tq) & (tp // 16 == tq // 16)).astype(np.float32)
    mid1 = (tp == tq).astype(np.float32)
    msu = np.tile(msu1, (1, NU))
    mid = np.tile(mid1, (1, NU))
    in_maps = []
    for c in range(NCORES):
        fs = c * FPC
        hs = c * HPC
        dsl = time_decay[hs:hs + HPC]
        dft = dsl.reshape(4, 128)
        fsl = time_faaaa[hs:hs + HPC]
        fcol = np.ascontiguousarray(fsl.reshape(4, 128).T)
        d8f = np.zeros((HPC, 128), dtype=np.float32)
        mkf = np.zeros((HPC, 128), dtype=np.float32)
        for h in range(HPC):
            off = (h % 2) * 64
            d8f[h, off:off + 64] = dsl[h]
            mkf[h, off:off + 64] = 1.0
        in_maps.append({
            "kf": np.ascontiguousarray(kfm[fs:fs + FPC]),
            "rf": np.ascontiguousarray(rfm[fs:fs + FPC]),
            "kt": np.ascontiguousarray(ktm[:, fs:fs + FPC]),
            "vt": np.ascontiguousarray(vtm[:, fs:fs + FPC]),
            "dft": np.ascontiguousarray(dft.astype(np.float32)),
            "fcol": fcol.astype(np.float32),
            "d8f": d8f, "mkf": mkf,
            "gnw": np.ascontiguousarray(gn_weight[fs:fs + FPC].reshape(HPC, HD1)),
            "gnb": np.ascontiguousarray(gn_bias[fs:fs + FPC].reshape(HPC, HD1)),
            "ind8p": ind8p, "ind8b": ind8b,
            "taud": taud, "taukd": taukd, "taux": taux, "taukx": taukx,
            "tauir": tauir, "tau31": tau31, "taucmp": taucmp,
            "msu": msu, "mid": mid,
        })
    res = run_bass_kernel_spmd(nc, in_maps, core_ids=list(range(NCORES)))
    xn = np.concatenate([res.results[c]["xno"] for c in range(NCORES)], axis=0)
    return xn


NSK = 50


def build_p6():
    nc = bass.Bass("TRN2", target_bir_lowering=False)
    xnf_in = nc.dram_tensor("xnf", [D, TC], BF16, kind="ExternalInput").ap()
    gf_in = nc.dram_tensor("gf", [D, TC], BF16, kind="ExternalInput").ap()
    x2_in = nc.dram_tensor("x2", [TC, D], F32, kind="ExternalInput").ap()
    wot_in = nc.dram_tensor("wot", [8, 128, 32, 512], F32, kind="ExternalInput").ap()
    wspt_in = nc.dram_tensor("wspt", [D, 24], F32, kind="ExternalInput").ap()
    bvec_in = nc.dram_tensor("bvec", [1, 24], F32, kind="ExternalInput").ap()
    gate_in = nc.dram_tensor("gate", [1, 1], F32, kind="ExternalInput").ap()
    wb_in = nc.dram_tensor("wb", [1024, 1024], F32, kind="ExternalInput").ap()
    idf_in = nc.dram_tensor("idf", [128, 128], F32, kind="ExternalInput").ap()
    out_out = nc.dram_tensor("out", [TC, D], F32, kind="ExternalOutput").ap()
    x3_scr = nc.dram_tensor("x3_scr", [TC, D], F32, kind="Internal").ap()

    with ExitStack() as st:
        sb = lambda name, shp, dt: st.enter_context(nc.sbuf_tensor("t_" + name, shp, dt))
        ps = lambda name, shp, dt: st.enter_context(nc.psum_tensor("p_" + name, shp, dt))
        stga = sb("stga", [128, TC], BF16)
        stgb = sb("stgb", [128, TC], BF16)
        gxnb = sb("gxnb", [128, 32 * TC], BF16)
        ldf = sb("ldf", [128, 8 * 512], F32)
        wtb = sb("wtb", [128, 32 * 512], BF16)
        xr = sb("xr", [128, 512], F32)
        ev = sb("ev", [128, 512], F32)
        sx3 = sb("sx3", [128, D], F32)
        sxn = sb("sxn", [128, D], F32)
        sq = sb("sq", [128, D], F32)
        cols = sb("cols", [128, 16], F32)
        ceps = sb("ceps", [128, 1], F32)
        crms = sb("crms", [128, 1], F32)
        xnT = sb("xnT", [128, 32 * 128], BF16)
        idf = sb("idf", [128, 128], F32)
        wsptb = sb("wsptb", [128, 32 * 24], BF16)
        bvbc = sb("bvbc", [128, 24], F32)
        onesr = sb("onesr", [1, 128], F32)
        lg24 = sb("lg24", [128, 24], F32)
        hpp = sb("hpp", [128, 8 * 8], F32)
        z16all = sb("z16all", [128, 8 * 16], F32)
        t16 = sb("t16", [128, 16], F32)
        e16 = sb("e16", [128, 16], F32)
        mx4 = sb("mx4", [128, 4], F32)
        s4 = sb("s4", [128, 4], F32)
        ln4 = sb("ln4", [128, 4], F32)
        uneg = sb("uneg", [128, 4], F32)
        vneg = sb("vneg", [128, 4], F32)
        hrt = sb("hrt", [128, 8 * 16], F32)
        gg = sb("gg", [1, 1], F32)
        gcol = sb("gcol", [128, 1], F32)
        gm1 = sb("gm1", [128, 1], F32)
        wbb = sb("wbb", [128, 8 * 1024], BF16)
        to = sb("to", [128, 1024], F32)
        xp = ldf[:, 0:1024]
        y_sb = ldf[:, 1024:2048]
        pf = [ps(f"pf{i}", [128, 512], F32) for i in range(4)]
        chain = st.enter_context(nc.semaphore("chain"))
        block = st.enter_context(nc.Block())
        ptrf = pf[2][:, 0:128]
        plg = pf[1][:, 0:24]
        pbv = pf[1][:, 128:152]
        pg128 = pf[1][:, 256:257]

        ser = Ser()
        ser.dma(lambda e: e.dma_start(out=idf[:], in_=idf_in))
        ser.op("vector", lambda e: e.memset(onesr[:], 1.0))
        ser.op("vector", lambda e: e.memset(ceps[:], LN_EPS))
        ser.op("vector", lambda e: e.memset(crms[:], RMS_EPS))
        ser.dma(lambda e: e.dma_start(out=gg[:], in_=gate_in))
        for fi in range(32):
            ser.dma(lambda e, fi=fi: e.dma_start(out=stga[:], in_=xnf_in[fi * 128:(fi + 1) * 128, :]))
            ser.dma(lambda e, fi=fi: e.dma_start(out=stgb[:], in_=gf_in[fi * 128:(fi + 1) * 128, :]))
            ser.op("vector", lambda e, fi=fi: e.tensor_mul(gxnb[:, fi * TC:(fi + 1) * TC], stga[:], stgb[:]))
        for fog in range(8):
            for fq in range(4):
                ser.dma(lambda e, fog=fog, fq=fq: e.dma_start(
                    out=ldf[:].rearrange("p (f c) -> p f c", f=8),
                    in_=wot_in[fog][:, fq * 8:(fq + 1) * 8, :]))
                ser.op("vector", lambda e, fq=fq: e.tensor_copy(
                    wtb[:, fq * 4096:(fq + 1) * 4096], ldf[:]))
            for rnd in range(2):
                for fi in range(32):
                    for ti in range(4):
                        tt = rnd * 4 + ti
                        pv = pf[ti][:]
                        ser.op("tensor", lambda e, fi=fi, tt=tt, pv=pv: e.matmul(
                            pv, gxnb[:, fi * TC + tt * 128: fi * TC + (tt + 1) * 128],
                            wtb[:, fi * 512:(fi + 1) * 512],
                            start=(fi == 0), stop=(fi == 31), skip_group_check=True))
                for ti in range(4):
                    tt = rnd * 4 + ti
                    co = fog * 512
                    pv = pf[ti][:]
                    ser.dma(lambda e, tt=tt, co=co: e.dma_start(
                        out=xr[:], in_=x2_in[tt * 128:(tt + 1) * 128, co:co + 512]))
                    ser.op("vector", lambda e, pv=pv: e.tensor_add(ev[:, 0:512], pv, xr[:]))
                    ser.dma(lambda e, tt=tt, co=co: e.dma_start(
                        out=x3_scr[tt * 128:(tt + 1) * 128, co:co + 512], in_=ev[:, 0:512]))
        ser.dma(lambda e: e.dma_start(
            out=ldf[:, 0:768].rearrange("p (f e) -> p f e", f=32),
            in_=wspt_in.rearrange("(f p) e -> p f e", p=128)))
        ser.op("vector", lambda e: e.tensor_copy(wsptb[:], ldf[:, 0:768]))
        ser.dma(lambda e: e.dma_start(out=ldf[0:1, 0:24], in_=bvec_in))
        ser.op("tensor", lambda e: e.matmul(pbv, onesr[:], ldf[0:1, 0:24],
                                            start=True, stop=True, skip_group_check=True))
        ser.op("vector", lambda e: e.tensor_copy(bvbc[:], pbv))
        for ci in range(8):
            ser.dma(lambda e, ci=ci: e.dma_start(out=ldf[:, 0:1024], in_=wb_in[ci * 128:(ci + 1) * 128, :]))
            ser.op("vector", lambda e, ci=ci: e.tensor_copy(wbb[:, ci * 1024:(ci + 1) * 1024], ldf[:, 0:1024]))
        c5, c6, c7, c8, c9, c10, c11 = (cols[:, 8 + i:9 + i] for i in range(7))
        for tt in range(NT):
            ser.dma(lambda e, tt=tt: e.dma_start(out=sx3[:], in_=x3_scr[tt * 128:(tt + 1) * 128, :]))
            _ln_ops(ser, sx3[:], sxn[:], sq[:], cols, D, ceps[:, 0:1])
            ser.op("scalar", lambda e: e.activation(sq[:], sxn[:], AF.Square))
            ser.op("vector", lambda e: e.reduce_sum(c5, sq[:], AX))
            ser.op("vector", lambda e: e.tensor_scalar(c5, c5, 1.0 / D, None, OP.mult))
            ser.op("scalar", lambda e: e.activation(c6, c5, AF.Ln, bias=crms[:, 0:1]))
            ser.op("scalar", lambda e: e.activation(c7, c6, AF.Exp, scale=-0.5))
            ser.op("vector", lambda e: e.tensor_scalar(sq[:], sxn[:], c7, None, OP.mult))
            for fi in range(32):
                ser.op("tensor", lambda e, fi=fi: e.transpose(ptrf, sq[:, fi * 128:(fi + 1) * 128], idf[:]))
                ser.op("vector", lambda e, fi=fi: e.tensor_copy(xnT[:, fi * 128:(fi + 1) * 128], ptrf))
            for fi in range(32):
                ser.op("tensor", lambda e, fi=fi: e.matmul(
                    plg, xnT[:, fi * 128:(fi + 1) * 128], wsptb[:, fi * 24:(fi + 1) * 24],
                    start=(fi == 0), stop=(fi == 31), skip_group_check=True))
            ser.op("vector", lambda e: e.tensor_add(lg24[:], plg, bvbc[:]))
            for gi, off in ((0, 16), (1, 20)):
                ser.op("vector", lambda e, off=off: e.reduce_max(c8, lg24[:, off:off + 4], AX))
                ser.op("vector", lambda e: e.tensor_scalar(c9, c8, -1.0, None, OP.mult))
                hv = hpp[:, tt * 8 + gi * 4: tt * 8 + gi * 4 + 4]
                ser.op("scalar", lambda e, off=off, hv=hv: e.activation(
                    hv, lg24[:, off:off + 4], AF.Exp, bias=c9))
                ser.op("vector", lambda e, hv=hv: e.reduce_sum(c10, hv, AX))
                ser.op("vector", lambda e: e.reciprocal(c11, c10))
                ser.op("vector", lambda e, hv=hv: e.tensor_scalar(hv, hv, c11, None, OP.mult))
            ser.op("vector", lambda e, tt=tt: e.tensor_scalar(
                z16all[:, tt * 16:(tt + 1) * 16], lg24[:, 0:16], 10.0, None, OP.mult))
        ser.op("scalar", lambda e: e.activation(gg[:], gg[:], AF.Sigmoid))
        ser.op("tensor", lambda e: e.matmul(pg128, onesr[:], gg[:],
                                            start=True, stop=True, skip_group_check=True))
        ser.op("vector", lambda e: e.tensor_copy(gcol[:], pg128))
        ser.op("vector", lambda e: e.tensor_scalar(gm1[:], gcol[:], -1.0, 1.0, OP.mult, OP.add))
        for tt in range(NT):
            Z = z16all[:, tt * 16:(tt + 1) * 16]
            Zv = Z.rearrange("p (i j) -> p i j", i=4)
            t16v = t16[:].rearrange("p (i j) -> p i j", i=4)
            e16v = e16[:].rearrange("p (i j) -> p i j", i=4)
            ser.op("vector", lambda e, Z=Z: e.tensor_copy(t16[:], Z))
            ser.op("vector", lambda e: e.memset(vneg[:], 0.0))
            for it in range(NSK):
                for j in range(4):
                    ser.op("vector", lambda e, Z=Zv, j=j: e.tensor_scalar(
                        t16v[:, :, j], Z[:, :, j], vneg[:, j:j + 1], None, OP.subtract))
                ser.op("vector", lambda e: e.reduce_max(mx4[:], t16v, AX))
                for i in range(4):
                    ser.op("vector", lambda e, i=i: e.tensor_scalar(
                        t16[:, i * 4:(i + 1) * 4], t16[:, i * 4:(i + 1) * 4],
                        mx4[:, i:i + 1], None, OP.subtract))
                ser.op("scalar", lambda e: e.activation(e16[:], t16[:], AF.Exp))
                ser.op("vector", lambda e: e.reduce_sum(s4[:], e16v, AX))
                ser.op("scalar", lambda e: e.activation(ln4[:], s4[:], AF.Ln))
                ser.op("vector", lambda e: e.tensor_add(uneg[:], mx4[:], ln4[:]))
                for i in range(4):
                    ser.op("vector", lambda e, Z=Z, i=i: e.tensor_scalar(
                        t16[:, i * 4:(i + 1) * 4], Z[:, i * 4:(i + 1) * 4],
                        uneg[:, i:i + 1], None, OP.subtract))
                tjv = t16[:].rearrange("p (i j) -> p j i", i=4)
                ejv = e16[:].rearrange("p (i j) -> p j i", i=4)
                ser.op("vector", lambda e, tjv=tjv: e.reduce_max(mx4[:], tjv, AX))
                for j in range(4):
                    ser.op("vector", lambda e, j=j: e.tensor_scalar(
                        t16v[:, :, j], t16v[:, :, j], mx4[:, j:j + 1], None, OP.subtract))
                ser.op("scalar", lambda e: e.activation(e16[:], t16[:], AF.Exp))
                ser.op("vector", lambda e, ejv=ejv: e.reduce_sum(s4[:], ejv, AX))
                ser.op("scalar", lambda e: e.activation(ln4[:], s4[:], AF.Ln))
                ser.op("vector", lambda e: e.tensor_add(vneg[:], mx4[:], ln4[:]))
            for i in range(4):
                ser.op("vector", lambda e, Z=Z, i=i: e.tensor_scalar(
                    t16[:, i * 4:(i + 1) * 4], Z[:, i * 4:(i + 1) * 4],
                    uneg[:, i:i + 1], None, OP.subtract))
            for j in range(4):
                ser.op("vector", lambda e, j=j: e.tensor_scalar(
                    t16v[:, :, j], t16v[:, :, j], vneg[:, j:j + 1], None, OP.subtract))
            ser.op("scalar", lambda e: e.activation(e16[:], t16[:], AF.Exp))
            hv = hrt[:, tt * 16:(tt + 1) * 16]
            ser.op("vector", lambda e, hv=hv: e.tensor_scalar(hv, e16[:], gcol[:, 0:1], None, OP.mult))
            for ii in range(4):
                col = tt * 16 + ii * 5
                ser.op("vector", lambda e, col=col: e.tensor_scalar(
                    hrt[:, col:col + 1], hrt[:, col:col + 1], gm1[:, 0:1], None, OP.add))
        for tt in range(NT):
            ser.dma(lambda e, tt=tt: e.dma_start(out=sx3[:], in_=x3_scr[tt * 128:(tt + 1) * 128, :]))
            _ln_ops(ser, sx3[:], sxn[:], sq[:], cols, D, ceps[:, 0:1])
            ser.op("vector", lambda e, tt=tt: e.tensor_scalar(
                xp[:], sxn[:, 0:1024], hpp[:, tt * 8: tt * 8 + 1], None, OP.mult))
            for j in range(1, 4):
                ser.op("vector", lambda e, tt=tt, j=j: e.scalar_tensor_tensor(
                    xp[:], sxn[:, j * 1024:(j + 1) * 1024], hpp[:, tt * 8 + j: tt * 8 + j + 1],
                    xp[:], OP.mult, OP.add))
            for ci in range(8):
                ser.op("tensor", lambda e, ci=ci: e.transpose(ptrf, xp[:, ci * 128:(ci + 1) * 128], idf[:]))
                ser.op("vector", lambda e, ci=ci: e.tensor_copy(xnT[:, ci * 128:(ci + 1) * 128], ptrf))
            for nh in range(2):
                pv = pf[nh]
                for ci in range(8):
                    ser.op("tensor", lambda e, ci=ci, nh=nh, pv=pv: e.matmul(
                        pv[:], xnT[:, ci * 128:(ci + 1) * 128],
                        wbb[:, ci * 1024 + nh * 512: ci * 1024 + (nh + 1) * 512],
                        start=(ci == 0), stop=(ci == 7), skip_group_check=True))
                ser.op("vector", lambda e, nh=nh, pv=pv: e.tensor_copy(y_sb[:, nh * 512:(nh + 1) * 512], pv[:]))
            for i in range(4):
                ser.op("vector", lambda e, tt=tt, i=i: e.tensor_scalar(
                    to[:], sxn[:, 0:1024], hrt[:, tt * 16 + i * 4: tt * 16 + i * 4 + 1], None, OP.mult))
                for j in range(1, 4):
                    ser.op("vector", lambda e, tt=tt, i=i, j=j: e.scalar_tensor_tensor(
                        to[:], sxn[:, j * 1024:(j + 1) * 1024],
                        hrt[:, tt * 16 + i * 4 + j: tt * 16 + i * 4 + j + 1],
                        to[:], OP.mult, OP.add))
                ser.op("vector", lambda e, tt=tt, i=i: e.scalar_tensor_tensor(
                    to[:], y_sb[:], hpp[:, tt * 8 + 4 + i: tt * 8 + 4 + i + 1],
                    to[:], OP.mult, OP.add))
                ser.op("vector", lambda e, i=i: e.tensor_add(to[:], to[:], sx3[:, i * 1024:(i + 1) * 1024]))
                ser.dma(lambda e, tt=tt, i=i: e.dma_start(
                    out=out_out[tt * 128:(tt + 1) * 128, i * 1024:(i + 1) * 1024], in_=to[:]))
        ser.finish(block, chain)
    return nc


def run_p6(xnf, gfm, x2, w_output, W_res, W_pre, W_post, H_res_base, H_pre_base,
           H_post_base, gate, W_branch):
    nc = build_p6()
    idf = np.eye(128, dtype=np.float32)
    wot = np.ascontiguousarray(w_output.T.reshape(32, 128, 8, 512).transpose(2, 1, 0, 3))
    wspt = np.ascontiguousarray(np.concatenate([W_res, W_pre, W_post], axis=0).T)
    bvec = np.concatenate([H_res_base.reshape(-1), H_pre_base, H_post_base])[None, :].astype(np.float32)
    in_maps = []
    for c in range(NCORES):
        t0 = c * TC
        in_maps.append({
            "xnf": np.ascontiguousarray(xnf[:, t0:t0 + TC]),
            "gf": np.ascontiguousarray(gfm[:, t0:t0 + TC]),
            "x2": np.ascontiguousarray(x2[t0:t0 + TC]),
            "wot": wot, "wspt": wspt, "bvec": bvec,
            "gate": gate.reshape(1, 1).astype(np.float32),
            "wb": W_branch, "idf": idf,
        })
    res = run_bass_kernel_spmd(nc, in_maps, core_ids=list(range(NCORES)))
    out = np.concatenate([res.results[c]["out"] for c in range(NCORES)], axis=0)
    return out


def kernel(L0_attn_hidden_states, L0_attn_Wq, L0_attn_Wkv, L0_attn_Wo,
           L0_ffn_router_w, L0_ffn_A, L0_ffn_B,
           L1_attn_w_key, L1_attn_w_value, L1_attn_w_receptance, L1_attn_w_gate,
           L1_attn_w_output, L1_attn_time_mix_key, L1_attn_time_mix_value,
           L1_attn_time_mix_receptance, L1_attn_time_mix_gate, L1_attn_time_decay,
           L1_attn_time_faaaa, L1_attn_gn_weight, L1_attn_gn_bias,
           L1_ffn_W_res, L1_ffn_W_pre, L1_ffn_W_post, L1_ffn_H_res_base,
           L1_ffn_H_pre_base, L1_ffn_H_post_base, L1_ffn_gate, L1_ffn_W_branch):
    x = np.ascontiguousarray(np.asarray(L0_attn_hidden_states, dtype=np.float32).reshape(T, D))
    qfm, kv = run_p1(x, np.asarray(L0_attn_Wq), np.asarray(L0_attn_Wkv))
    ao = run_p2(qfm, kv)
    x2, xl2 = run_p3(ao, x, np.asarray(L0_attn_Wo), np.asarray(L0_ffn_router_w),
                     np.asarray(L0_ffn_A), np.asarray(L0_ffn_B))
    kfm, vfm, rfm, gfm = run_p4(xl2, np.asarray(L1_attn_w_key), np.asarray(L1_attn_w_value),
                                np.asarray(L1_attn_w_receptance), np.asarray(L1_attn_w_gate),
                                np.asarray(L1_attn_time_mix_key), np.asarray(L1_attn_time_mix_value),
                                np.asarray(L1_attn_time_mix_receptance), np.asarray(L1_attn_time_mix_gate))
    xnf = run_p5(kfm, vfm, rfm, np.asarray(L1_attn_time_decay), np.asarray(L1_attn_time_faaaa),
                 np.asarray(L1_attn_gn_weight), np.asarray(L1_attn_gn_bias))
    out = run_p6(xnf, gfm, x2, np.asarray(L1_attn_w_output), np.asarray(L1_ffn_W_res),
                 np.asarray(L1_ffn_W_pre), np.asarray(L1_ffn_W_post),
                 np.asarray(L1_ffn_H_res_base), np.asarray(L1_ffn_H_pre_base),
                 np.asarray(L1_ffn_H_post_base), np.asarray(L1_ffn_gate),
                 np.asarray(L1_ffn_W_branch))
    return out.reshape(B, S, D)


# version 03
# speedup vs baseline: 1.5290x; 1.1144x over previous
from contextlib import ExitStack

import numpy as np
import ml_dtypes
import concourse.bass as bass
import concourse.mybir as mybir
from concourse.bass_utils import run_bass_kernel_spmd

F32 = mybir.dt.float32
BF16 = mybir.dt.bfloat16
AX = mybir.AxisListType.X
AF = mybir.ActivationFunctionType
OP = mybir.AluOpType
BF = ml_dtypes.bfloat16

NCORES = 8
B, S, D = 4, 2048, 4096
T = B * S
TC = T // NCORES
NT = TC // 128
NH0, HD0 = 32, 128
NH1, HD1 = 64, 64
HPC = NH1 // NCORES
FPC = HPC * HD1
LN_EPS = 1e-5
RMS_EPS = 1.1920929e-07
QSCALE = float(HD0) ** -0.5
NU = 32
CH = 32
NCH = S // CH


def _alibi_slopes(n):
    start = 2.0 ** (-(2.0 ** (-(np.log2(n) - 3))))
    return np.array([start ** (i + 1) for i in range(n)], dtype=np.float64)


class Ser:

    def __init__(self):
        self.ops = []
        self.ncomp = 0
        self.ndma = 0

    def op(self, eng, fn, dwait=None):
        self.ops.append((eng, fn, "c", dwait))
        self.ncomp += 1

    def dma(self, fn):
        self.ops.append(("sync", fn, "d", None))
        self.ndma += 1

    def mark(self):
        return self.ndma

    def finish(self, block, chain, dchain):
        comp_before = []
        dma_before = []
        nc_ = nd_ = 0
        for e, fn, k, dw in self.ops:
            comp_before.append(nc_)
            dma_before.append(nd_)
            if k == "c":
                nc_ += 1
            else:
                nd_ += 1
        total_c, total_d = nc_, nd_

        def runner_for(name):
            def runner(eng):
                for i, (e, fn, k, dw) in enumerate(self.ops):
                    if e != name:
                        continue
                    if k == "d":
                        eng.wait_ge(chain, comp_before[i])
                        fn(eng).then_inc(dchain, 16)
                    else:
                        nd = dma_before[i] if dw is None else dw
                        eng.wait_ge(dchain, 16 * nd)
                        eng.wait_ge(chain, comp_before[i])
                        fn(eng).then_inc(chain, 1)
                if name == "gpsimd":
                    eng.wait_ge(chain, total_c)
                    eng.wait_ge(dchain, 16 * total_d)
            return runner

        block.sync(runner_for("sync"))
        block.vector(runner_for("vector"))
        block.scalar(runner_for("scalar"))
        block.tensor(runner_for("tensor"))
        block.gpsimd(runner_for("gpsimd"))


def _ln_ops(ser, x, xn, sq, cols, n, eps_col):
    mean, msq, negv, lnv, rstd = (cols[:, i:i + 1] for i in range(5))
    ser.op("vector", lambda e: e.reduce_sum(mean, x, AX))
    ser.op("vector", lambda e: e.tensor_scalar(mean, mean, 1.0 / n, None, OP.mult))
    ser.op("scalar", lambda e: e.activation(sq, x, AF.Square))
    ser.op("vector", lambda e: e.reduce_sum(msq, sq, AX))
    ser.op("vector", lambda e: e.tensor_scalar(msq, msq, 1.0 / n, None, OP.mult))
    ser.op("vector", lambda e: e.scalar_tensor_tensor(negv, mean, mean, msq, OP.mult, OP.subtract))
    ser.op("scalar", lambda e: e.activation(lnv, negv, AF.Ln, bias=eps_col, scale=-1.0))
    ser.op("scalar", lambda e: e.activation(rstd, lnv, AF.Exp, scale=-0.5))
    ser.op("vector", lambda e: e.tensor_scalar(xn, x, mean, rstd, OP.subtract, OP.mult))


def build_p1():
    nc = bass.Bass("TRN2", target_bir_lowering=False)
    x_in = nc.dram_tensor("x", [TC, D], F32, kind="ExternalInput").ap()
    wq_in = nc.dram_tensor("wq", [32, 128, 32, 128], F32, kind="ExternalInput").ap()
    wkv_in = nc.dram_tensor("wkv", [D, 256], F32, kind="ExternalInput").ap()
    idf_in = nc.dram_tensor("idf", [128, 128], F32, kind="ExternalInput").ap()
    qfm_out = nc.dram_tensor("qfm", [D, TC], F32, kind="ExternalOutput").ap()
    kv_out = nc.dram_tensor("kv", [TC, 256], F32, kind="ExternalOutput").ap()

    with ExitStack() as st:
        sb = lambda name, shp, dt: st.enter_context(nc.sbuf_tensor("t_" + name, shp, dt))
        ps = lambda name, shp, dt: st.enter_context(nc.psum_tensor("p_" + name, shp, dt))
        sx = sb("sx", [128, D], F32)
        sq = sb("sq", [128, D], F32)
        cols = sb("cols", [128, 8], F32)
        ceps = sb("ceps", [128, 1], F32)
        xn = sb("xn", [128, D], F32)
        xnT = sb("xnT", [128, 32 * TC], BF16)
        idf = sb("idf", [128, 128], F32)
        wtf = sb("wtf", [128, 32 * 128], F32)
        wtb = sb("wtb", [128, 32 * 128], BF16)
        wkvf = sb("wkvf", [128, 256], F32)
        wkvb = sb("wkvb", [128, 32 * 256], BF16)
        evac = sb("evac", [128, 512], F32)
        pf0 = ps("pf0", [128, 512], F32)
        pf1 = ps("pf1", [128, 512], F32)
        pf2 = ps("pf2", [128, 512], F32)
        ptrf = pf2[:, 0:128]
        pkv = ps("pkv", [128, 256], F32)
        chain = st.enter_context(nc.semaphore("chain"))
        dchain = st.enter_context(nc.semaphore("dchain"))
        block = st.enter_context(nc.Block())

        ser = Ser()
        ser.dma(lambda e: e.dma_start(out=idf[:], in_=idf_in))
        ser.op("vector", lambda e: e.memset(ceps[:], LN_EPS))
        for tt in range(NT):
            ser.dma(lambda e, tt=tt: e.dma_start(out=sx[:], in_=x_in[tt * 128:(tt + 1) * 128, :]))
            _ln_ops(ser, sx[:], xn[:], sq[:], cols, D, ceps[:, 0:1])
            for fi in range(32):
                ser.op("tensor", lambda e, fi=fi: e.transpose(ptrf, xn[:, fi * 128:(fi + 1) * 128], idf[:]))
                ser.op("vector", lambda e, fi=fi, tt=tt:
                       e.tensor_copy(xnT[:, fi * TC + tt * 128: fi * TC + (tt + 1) * 128], ptrf))
        marks = {}
        ser.dma(lambda e: e.dma_start(
            out=wtf[:].rearrange("p (f c) -> p f c", f=32), in_=wq_in[0]))
        marks[0] = ser.mark()
        for fo in range(32):
            mk = marks[fo]
            ser.op("vector", lambda e: e.tensor_copy(wtb[:], wtf[:]), dwait=mk)
            if fo + 1 < 32:
                ser.dma(lambda e, fo=fo: e.dma_start(
                    out=wtf[:].rearrange("p (f c) -> p f c", f=32), in_=wq_in[fo + 1]))
                marks[fo + 1] = ser.mark()
            for fi in range(32):
                for th in range(2):
                    pq = pf0 if th == 0 else pf1
                    ser.op("tensor", lambda e, fi=fi, th=th, pq=pq: e.matmul(
                        pq[:], wtb[:, fi * 128:(fi + 1) * 128],
                        xnT[:, fi * TC + th * 512: fi * TC + (th + 1) * 512],
                        start=(fi == 0), stop=(fi == 31), skip_group_check=True), dwait=mk)
            for th in range(2):
                pq = pf0 if th == 0 else pf1
                ser.op("vector", lambda e, pq=pq: e.tensor_copy(evac[:], pq[:]))
                ser.dma(lambda e, fo=fo, th=th: e.dma_start(
                    out=qfm_out[fo * 128:(fo + 1) * 128, th * 512:(th + 1) * 512], in_=evac[:]))
        for fi in range(32):
            ser.dma(lambda e, fi=fi: e.dma_start(out=wkvf[:], in_=wkv_in[fi * 128:(fi + 1) * 128, :]))
            ser.op("vector", lambda e, fi=fi: e.tensor_copy(wkvb[:, fi * 256:(fi + 1) * 256], wkvf[:]))
        for tt in range(NT):
            for fi in range(32):
                ser.op("tensor", lambda e, tt=tt, fi=fi: e.matmul(
                    pkv[:], xnT[:, fi * TC + tt * 128: fi * TC + (tt + 1) * 128],
                    wkvb[:, fi * 256:(fi + 1) * 256],
                    start=(fi == 0), stop=(fi == 31), skip_group_check=True))
            ser.op("vector", lambda e: e.tensor_copy(evac[:, 0:256], pkv[:]))
            ser.dma(lambda e, tt=tt: e.dma_start(
                out=kv_out[tt * 128:(tt + 1) * 128, :], in_=evac[:, 0:256]))
        ser.finish(block, chain, dchain)
    return nc


def run_p1(x_full, wq, wkv):
    nc = build_p1()
    idf = np.eye(128, dtype=np.float32)
    wq_p = np.ascontiguousarray(wq.reshape(32, 128, 32, 128).transpose(2, 1, 0, 3))
    in_maps = [{"x": np.ascontiguousarray(x_full[c * TC:(c + 1) * TC]),
                "wq": wq_p, "wkv": wkv, "idf": idf} for c in range(NCORES)]
    res = run_bass_kernel_spmd(nc, in_maps, core_ids=list(range(NCORES)))
    qfm = np.concatenate([res.results[c]["qfm"] for c in range(NCORES)], axis=1)
    kv = np.concatenate([res.results[c]["kv"] for c in range(NCORES)], axis=0)
    return qfm, kv


NSLOT = 4
KT_OF_SLOT = [4 * (s + 1) for s in range(NSLOT)]
NKT = sum(KT_OF_SLOT)


def build_p2():
    nc = bass.Bass("TRN2", target_bir_lowering=False)
    qb_in = nc.dram_tensor("qb", [D, NSLOT * 256], F32, kind="ExternalInput").ap()
    kf_in = nc.dram_tensor("kf", [HD0, S], F32, kind="ExternalInput").ap()
    vt_in = nc.dram_tensor("vt", [S, HD0], F32, kind="ExternalInput").ap()
    pmjd_in = nc.dram_tensor("pmjd", [128, NKT * 256], F32, kind="ExternalInput").ap()
    ao_out = nc.dram_tensor("ao", [D, NSLOT * 256], F32, kind="ExternalOutput").ap()

    slopes = _alibi_slopes(NH0)
    with ExitStack() as st:
        sb = lambda name, shp, dt: st.enter_context(nc.sbuf_tensor("t_" + name, shp, dt))
        ps = lambda name, shp, dt: st.enter_context(nc.psum_tensor("p_" + name, shp, dt))
        kf_f = sb("kf_f", [128, S], F32)
        kf_b = sb("kf_b", [128, S], BF16)
        v_f = sb("v_f", [128, 16 * HD0], F32)
        v_b = sb("v_b", [128, 16 * HD0], BF16)
        pmjd = sb("pmjd", [128, NKT * 256], F32)
        q_f = sb("q_f", [128, NSLOT * 256], F32)
        q_b = sb("q_b", [128, NSLOT * 256], BF16)
        earg = sb("earg", [128, 256], F32)
        expT = sb("expT", [128, 16 * 256], BF16)
        ones1 = sb("ones1", [128, 1], BF16)
        onesr = sb("onesr", [1, 128], F32)
        recip = sb("recip", [1, 256], F32)
        cm16 = sb("cm16", [128, 1], F32)
        recb = sb("recb", [128, 256], F32)
        aout = sb("aout", [128, 256], F32)
        psc = ps("psc", [128, 256], F32)
        pav = ps("pav", [128, 256], F32)
        psum1 = ps("psum1", [1, 256], F32)
        prb = ps("prb", [128, 256], F32)
        chain = st.enter_context(nc.semaphore("chain"))
        dchain = st.enter_context(nc.semaphore("dchain"))
        block = st.enter_context(nc.Block())

        ser = Ser()
        ser.dma(lambda e: e.dma_start(out=kf_f[:], in_=kf_in))
        ser.op("vector", lambda e: e.tensor_copy(kf_b[:], kf_f[:]))
        ser.dma(lambda e: e.dma_start(
            out=v_f[:].rearrange("p (kt e) -> p kt e", kt=16),
            in_=vt_in.rearrange("(kt p) e -> p kt e", p=128)))
        ser.op("vector", lambda e: e.tensor_copy(v_b[:], v_f[:]))
        ser.dma(lambda e: e.dma_start(out=pmjd[:], in_=pmjd_in))
        ser.op("vector", lambda e: e.memset(ones1[:], 1.0))
        ser.op("vector", lambda e: e.memset(onesr[:], 1.0))
        ser.op("vector", lambda e: e.memset(cm16[:], -16.0))

        marks = {}
        ser.dma(lambda e: e.dma_start(out=q_f[:], in_=qb_in[0:128, :]))
        marks[0] = ser.mark()
        for h in range(NH0):
            sl = float(slopes[h])
            mk = marks[h]
            ser.op("vector", lambda e: e.tensor_copy(q_b[:], q_f[:]), dwait=mk)
            if h + 1 < NH0:
                ser.dma(lambda e, h=h: e.dma_start(out=q_f[:], in_=qb_in[(h + 1) * 128:(h + 2) * 128, :]))
                marks[h + 1] = ser.mark()
            kt_base = 0
            for s in range(NSLOT):
                nkt = KT_OF_SLOT[s]
                for kt in range(nkt):
                    ser.op("tensor", lambda e, kt=kt, s=s: e.matmul(
                        psc[:], kf_b[:, kt * 128:(kt + 1) * 128], q_b[:, s * 256:(s + 1) * 256],
                        start=True, stop=True, skip_group_check=True), dwait=mk)
                    col = (kt_base + kt) * 256
                    ser.op("vector", lambda e, col=col, sl=sl: e.scalar_tensor_tensor(
                        earg[:], psc[:], QSCALE / sl, pmjd[:, col:col + 256], OP.mult, OP.add), dwait=mk)
                    ser.op("scalar", lambda e, kt=kt, sl=sl: e.activation(
                        expT[:, kt * 256:(kt + 1) * 256], earg[:], AF.Exp, bias=cm16[:, 0:1], scale=sl), dwait=mk)
                for kt in range(nkt):
                    ser.op("tensor", lambda e, kt=kt, nkt=nkt: e.matmul(
                        pav[:], v_b[:, kt * 128:(kt + 1) * 128], expT[:, kt * 256:(kt + 1) * 256],
                        start=(kt == 0), stop=(kt == nkt - 1), skip_group_check=True), dwait=mk)
                for kt in range(nkt):
                    ser.op("tensor", lambda e, kt=kt, nkt=nkt: e.matmul(
                        psum1[:], ones1[:], expT[:, kt * 256:(kt + 1) * 256],
                        start=(kt == 0), stop=(kt == nkt - 1), skip_group_check=True), dwait=mk)
                ser.op("vector", lambda e: e.reciprocal(recip[:], psum1[:]), dwait=mk)
                ser.op("tensor", lambda e: e.matmul(
                    prb[:], onesr[:], recip[:], start=True, stop=True, skip_group_check=True), dwait=mk)
                ser.op("vector", lambda e: e.tensor_copy(recb[:], prb[:]), dwait=mk)
                ser.op("vector", lambda e: e.tensor_mul(aout[:], pav[:], recb[:]))
                ser.dma(lambda e, h=h, s=s: e.dma_start(
                    out=ao_out[h * 128:(h + 1) * 128, s * 256:(s + 1) * 256], in_=aout[:]))
                kt_base += nkt
        ser.finish(block, chain, dchain)
    return nc


def make_pmjd(core):
    out = np.zeros((128, NKT * 256), dtype=np.float32)
    p = np.arange(128)[:, None]
    j = np.arange(256)[None, :]
    base = 0
    for s in range(NSLOT):
        i_c = 2 * s + (core % 2)
        for kt in range(KT_OF_SLOT[s]):
            v = (p - j + 128 * (kt - 2 * i_c)).astype(np.float32)
            v[v > 0] = -1e33
            out[:, (base + kt) * 256:(base + kt + 1) * 256] = v
        base += KT_OF_SLOT[s]
    return out


def run_p2(qfm, kv):
    nc = build_p2()
    in_maps = []
    for c in range(NCORES):
        b = c // 2
        blocks = [2 * s + (c % 2) for s in range(NSLOT)]
        qcols = np.concatenate(
            [qfm[:, b * S + i * 256: b * S + (i + 1) * 256] for i in blocks], axis=1)
        kvb = kv[b * S:(b + 1) * S]
        in_maps.append({
            "qb": np.ascontiguousarray(qcols),
            "kf": np.ascontiguousarray(kvb[:, :HD0].T),
            "vt": np.ascontiguousarray(kvb[:, HD0:]),
            "pmjd": make_pmjd(c),
        })
    res = run_bass_kernel_spmd(nc, in_maps, core_ids=list(range(NCORES)))
    ao = np.zeros((D, T), dtype=np.float32)
    for c in range(NCORES):
        b = c // 2
        r = res.results[c]["ao"]
        for s in range(NSLOT):
            i = 2 * s + (c % 2)
            ao[:, b * S + i * 256: b * S + (i + 1) * 256] = r[:, s * 256:(s + 1) * 256]
    return ao


def build_p3():
    nc = bass.Bass("TRN2", target_bir_lowering=False)
    ao_in = nc.dram_tensor("ao", [D, TC], F32, kind="ExternalInput").ap()
    x_in = nc.dram_tensor("x", [TC, D], F32, kind="ExternalInput").ap()
    wo_in = nc.dram_tensor("wo", [16, 128, 32, 256], F32, kind="ExternalInput").ap()
    rwt_in = nc.dram_tensor("rwt", [D, 4], F32, kind="ExternalInput").ap()
    at_in = nc.dram_tensor("at", [4, D, 16], F32, kind="ExternalInput").ap()
    bt_in = nc.dram_tensor("bt", [4, 16, D], F32, kind="ExternalInput").ap()
    idf_in = nc.dram_tensor("idf", [128, 128], F32, kind="ExternalInput").ap()
    x2_out = nc.dram_tensor("x2", [TC, D], F32, kind="ExternalOutput").ap()
    xl2_out = nc.dram_tensor("xl2", [D, TC], F32, kind="ExternalOutput").ap()
    x1_scr = nc.dram_tensor("x1_scr", [TC, D], F32, kind="Internal").ap()

    with ExitStack() as st:
        sb = lambda name, shp, dt: st.enter_context(nc.sbuf_tensor("t_" + name, shp, dt))
        ps = lambda name, shp, dt: st.enter_context(nc.psum_tensor("p_" + name, shp, dt))
        aob = sb("aob", [128, 32 * TC], BF16)
        ldf = sb("ldf", [128, 16 * 256], F32)
        wtb = sb("wtb", [128, 32 * 256], BF16)
        sx1 = sb("sx1", [128, D], F32)
        sx2 = sb("sx2", [128, D], F32)
        sq = sb("sq", [128, D], F32)
        cols = sb("cols", [128, 16], F32)
        ceps = sb("ceps", [128, 1], F32)
        xn = sb("xn", [128, D], F32)
        xnT = sb("xnT", [128, 32 * 128], BF16)
        idf = sb("idf", [128, 128], F32)
        rwtb = sb("rwtb", [128, 32 * 4], BF16)
        atb = [sb(f"atb{i}", [128, 32 * 16], BF16) for i in range(4)]
        btb = [sb(f"btb{i}", [16, D], BF16) for i in range(4)]
        lg = sb("lg", [128, 4], F32)
        gsc = sb("gsc", [128, 16], F32)
        gates = sb("gates", [128, 4], F32)
        gbc = sb("gbc", [16, 128], F32)
        wmid4 = sb("wmid4", [16, 4 * 128], BF16)
        ones16 = sb("ones16", [1, 16], F32)
        pcc = [ps(f"pcc{i}", [128, 256], F32) for i in range(8)]
        chain = st.enter_context(nc.semaphore("chain"))
        dchain = st.enter_context(nc.semaphore("dchain"))
        block = st.enter_context(nc.Block())
        ptrf = pcc[7][:, 0:128]
        plg = pcc[0][:, 0:4]
        pg1 = pcc[6][0:1, 0:128]
        pgb = pcc[5][0:16, 0:128]
        pmid = pcc[4][0:16, 0:128]
        pdl = pcc[1]

        gT = ldf[0:1, 3584:4096]
        ser = Ser()
        ser.dma(lambda e: e.dma_start(out=idf[:], in_=idf_in))
        ser.op("vector", lambda e: e.memset(ones16[:], 1.0))
        ser.op("vector", lambda e: e.memset(ceps[:], LN_EPS))
        for fi in range(32):
            ser.dma(lambda e, fi=fi: e.dma_start(out=ldf[:, 0:TC], in_=ao_in[fi * 128:(fi + 1) * 128, :]))
            ser.op("vector", lambda e, fi=fi: e.tensor_copy(aob[:, fi * TC:(fi + 1) * TC], ldf[:, 0:TC]))
        for fog in range(8):
            for half in range(2):
                cb = fog * 2 + half
                for hq in range(2):
                    ser.dma(lambda e, cb=cb, hq=hq: e.dma_start(
                        out=ldf[:].rearrange("p (f c) -> p f c", f=16),
                        in_=wo_in[cb][:, hq * 16:(hq + 1) * 16, :]))
                    ser.op("vector", lambda e, hq=hq: e.tensor_copy(
                        wtb[:, hq * 4096:(hq + 1) * 4096], ldf[:]))
                for fi in range(32):
                    for tt in range(NT):
                        pv = pcc[tt][:]
                        ser.op("tensor", lambda e, fi=fi, tt=tt, pv=pv: e.matmul(
                            pv, aob[:, fi * TC + tt * 128: fi * TC + (tt + 1) * 128],
                            wtb[:, fi * 256:(fi + 1) * 256],
                            start=(fi == 0), stop=(fi == 31), skip_group_check=True))
                xr_v = ldf[:, 3072:3328]
                ev_v = ldf[:, 3328:3584]
                for tt in range(NT):
                    co = fog * 512 + half * 256
                    pv = pcc[tt][:]
                    ser.dma(lambda e, tt=tt, co=co, xr_v=xr_v: e.dma_start(
                        out=xr_v, in_=x_in[tt * 128:(tt + 1) * 128, co:co + 256]))
                    ser.op("vector", lambda e, pv=pv, xr_v=xr_v, ev_v=ev_v: e.tensor_add(ev_v, pv, xr_v))
                    ser.dma(lambda e, tt=tt, co=co, ev_v=ev_v: e.dma_start(
                        out=x1_scr[tt * 128:(tt + 1) * 128, co:co + 256], in_=ev_v))
        for fi in range(32):
            ser.dma(lambda e, fi=fi: e.dma_start(out=ldf[:, 0:4], in_=rwt_in[fi * 128:(fi + 1) * 128, :]))
            ser.op("vector", lambda e, fi=fi: e.tensor_copy(rwtb[:, fi * 4:(fi + 1) * 4], ldf[:, 0:4]))
        for ei in range(4):
            ser.dma(lambda e, ei=ei: e.dma_start(
                out=sq[:, 0:512].rearrange("p (f r) -> p f r", f=32),
                in_=at_in[ei].rearrange("(f p) r -> p f r", p=128)))
            ser.op("vector", lambda e, ei=ei: e.tensor_copy(atb[ei][:], sq[:, 0:512]))
            ser.dma(lambda e, ei=ei: e.dma_start(out=ldf[0:16, 0:D], in_=bt_in[ei]))
            ser.op("vector", lambda e, ei=ei: e.tensor_copy(btb[ei][:], ldf[0:16, 0:D]))
        for tt in range(NT):
            ser.dma(lambda e, tt=tt: e.dma_start(out=sx1[:], in_=x1_scr[tt * 128:(tt + 1) * 128, :]))
            _ln_ops(ser, sx1[:], xn[:], sq[:], cols, D, ceps[:, 0:1])
            for fi in range(32):
                ser.op("tensor", lambda e, fi=fi: e.transpose(ptrf, xn[:, fi * 128:(fi + 1) * 128], idf[:]))
                ser.op("vector", lambda e, fi=fi: e.tensor_copy(xnT[:, fi * 128:(fi + 1) * 128], ptrf))
            for fi in range(32):
                ser.op("tensor", lambda e, fi=fi: e.matmul(
                    plg, xnT[:, fi * 128:(fi + 1) * 128], rwtb[:, fi * 4:(fi + 1) * 4],
                    start=(fi == 0), stop=(fi == 31), skip_group_check=True))
            ser.op("vector", lambda e: e.tensor_copy(lg[:], plg))
            c0, c1, c2, c3, c4 = (cols[:, 8 + i:9 + i] for i in range(5))
            ser.op("vector", lambda e: e.reduce_max(c0, lg[:], AX))
            ser.op("vector", lambda e: e.tensor_scalar(gsc[:, 0:4], lg[:], c0, None, OP.is_equal))
            ser.op("vector", lambda e: e.scalar_tensor_tensor(gsc[:, 4:8], gsc[:, 0:4], -1e30, lg[:], OP.mult, OP.add))
            ser.op("vector", lambda e: e.reduce_max(c1, gsc[:, 4:8], AX))
            ser.op("vector", lambda e: e.tensor_scalar(gsc[:, 8:12], lg[:], c1, None, OP.is_ge))
            ser.op("vector", lambda e: e.tensor_scalar(c2, c0, -1.0, None, OP.mult))
            ser.op("scalar", lambda e: e.activation(gsc[:, 12:16], lg[:], AF.Exp, bias=c2))
            ser.op("vector", lambda e: e.tensor_mul(gsc[:, 12:16], gsc[:, 12:16], gsc[:, 8:12]))
            ser.op("vector", lambda e: e.reduce_sum(c3, gsc[:, 12:16], AX))
            ser.op("vector", lambda e: e.reciprocal(c4, c3))
            ser.op("vector", lambda e: e.tensor_scalar(gates[:], gsc[:, 12:16], c4, None, OP.mult))
            for ei in range(4):
                ser.op("tensor", lambda e, ei=ei: e.transpose(pg1, gates[:, ei:ei + 1], idf[:]))
                ser.op("vector", lambda e, ei=ei: e.tensor_copy(gT[0:1, ei * 128:(ei + 1) * 128], pg1))
            for ei in range(4):
                for fi in range(32):
                    ser.op("tensor", lambda e, ei=ei, fi=fi: e.matmul(
                        pmid, atb[ei][:, fi * 16:(fi + 1) * 16], xnT[:, fi * 128:(fi + 1) * 128],
                        start=(fi == 0), stop=(fi == 31), skip_group_check=True))
                ser.op("tensor", lambda e, ei=ei: e.matmul(
                    pgb, ones16[:], gT[0:1, ei * 128:(ei + 1) * 128],
                    start=True, stop=True, skip_group_check=True))
                ser.op("vector", lambda e: e.tensor_copy(gbc[:], pgb))
                ser.op("vector", lambda e, ei=ei: e.tensor_mul(wmid4[:, ei * 128:(ei + 1) * 128], pmid, gbc[:]))
            for og in range(16):
                for ei in range(4):
                    ser.op("tensor", lambda e, og=og, ei=ei: e.matmul(
                        pdl[:], wmid4[:, ei * 128:(ei + 1) * 128], btb[ei][:, og * 256:(og + 1) * 256],
                        start=(ei == 0), stop=(ei == 3), skip_group_check=True))
                ser.op("vector", lambda e, og=og: e.scalar_tensor_tensor(
                    sx2[:, og * 256:(og + 1) * 256], pdl[:], 4.0, sx1[:, og * 256:(og + 1) * 256],
                    OP.mult, OP.add))
                ser.dma(lambda e, tt=tt, og=og: e.dma_start(
                    out=x2_out[tt * 128:(tt + 1) * 128, og * 256:(og + 1) * 256],
                    in_=sx2[:, og * 256:(og + 1) * 256]))
            _ln_ops(ser, sx2[:], xn[:], sq[:], cols, D, ceps[:, 0:1])
            for fi in range(32):
                ser.op("tensor", lambda e, fi=fi: e.transpose(ptrf, xn[:, fi * 128:(fi + 1) * 128], idf[:]))
                ser.op("vector", lambda e: e.tensor_copy(ldf[:, 3072:3200], ptrf))
                ser.dma(lambda e, tt=tt, fi=fi: e.dma_start(
                    out=xl2_out[fi * 128:(fi + 1) * 128, tt * 128:(tt + 1) * 128], in_=ldf[:, 3072:3200]))
        ser.finish(block, chain, dchain)
    return nc


def run_p3(ao, x_full, wo, router_w, A, Bm):
    nc = build_p3()
    idf = np.eye(128, dtype=np.float32)
    wo_p = np.ascontiguousarray(wo.reshape(32, 128, 16, 256).transpose(2, 1, 0, 3))
    rwt = np.ascontiguousarray(router_w.T)
    at = np.ascontiguousarray(A.transpose(0, 2, 1))
    bt = np.ascontiguousarray(Bm.transpose(0, 2, 1))
    in_maps = [{"ao": np.ascontiguousarray(ao[:, c * TC:(c + 1) * TC]),
                "x": np.ascontiguousarray(x_full[c * TC:(c + 1) * TC]),
                "wo": wo_p, "rwt": rwt, "at": at, "bt": bt, "idf": idf}
               for c in range(NCORES)]
    res = run_bass_kernel_spmd(nc, in_maps, core_ids=list(range(NCORES)))
    x2 = np.concatenate([res.results[c]["x2"] for c in range(NCORES)], axis=0)
    xl2 = np.concatenate([res.results[c]["xl2"] for c in range(NCORES)], axis=1)
    return x2, xl2


def build_p4():
    nc = bass.Bass("TRN2", target_bir_lowering=False)
    xl_in = nc.dram_tensor("xl", [D, TC], F32, kind="ExternalInput").ap()
    halo_in = nc.dram_tensor("halo", [D, 1], F32, kind="ExternalInput").ap()
    w_ins = [nc.dram_tensor(f"w{i}", [32, 128, 32, 128], F32, kind="ExternalInput").ap() for i in range(4)]
    tm_in = nc.dram_tensor("tm", [D, 4], F32, kind="ExternalInput").ap()
    outs = [nc.dram_tensor(nm, [D, TC], BF16, kind="ExternalOutput").ap()
            for nm in ("kfm", "vfm", "rfm", "gfm")]

    with ExitStack() as st:
        sb = lambda name, shp, dt: st.enter_context(nc.sbuf_tensor("t_" + name, shp, dt))
        ps = lambda name, shp, dt: st.enter_context(nc.psum_tensor("p_" + name, shp, dt))
        ext = sb("ext", [128, TC + 1], F32)
        mf32 = sb("mf32", [128, TC], F32)
        mixf = sb("mixf", [128, 32 * TC], BF16)
        tmv = sb("tmv", [128, 4 * 32], F32)
        tm1 = sb("tm1", [128, 4 * 32], F32)
        wtf = sb("wtf", [128, 32 * 128], F32)
        wtb = sb("wtb", [128, 32 * 128], BF16)
        evb = sb("evb", [128, 512], BF16)
        pf0 = ps("pf0", [128, 512], F32)
        pf1 = ps("pf1", [128, 512], F32)
        chain = st.enter_context(nc.semaphore("chain"))
        dchain = st.enter_context(nc.semaphore("dchain"))
        block = st.enter_context(nc.Block())

        ser = Ser()
        ser.dma(lambda e: e.dma_start(
            out=tmv[:].rearrange("p (f m) -> p f m", f=32),
            in_=tm_in.rearrange("(f p) m -> p f m", p=128)))
        ser.op("vector", lambda e: e.tensor_scalar(tm1[:], tmv[:], -1.0, 1.0, OP.mult, OP.add))
        for pj in range(4):
            for fi in range(32):
                ser.dma(lambda e, fi=fi: e.dma_start(
                    out=ext[:, 1:TC + 1], in_=xl_in[fi * 128:(fi + 1) * 128, :]))
                ser.dma(lambda e, fi=fi: e.dma_start(
                    out=ext[:, 0:1], in_=halo_in[fi * 128:(fi + 1) * 128, :]))
                mcol = tmv[:, fi * 4 + pj: fi * 4 + pj + 1]
                m1col = tm1[:, fi * 4 + pj: fi * 4 + pj + 1]
                ser.op("vector", lambda e, mcol=mcol: e.tensor_scalar(
                    mf32[:], ext[:, 1:TC + 1], mcol, None, OP.mult))
                ser.op("vector", lambda e, fi=fi, m1col=m1col: e.scalar_tensor_tensor(
                    mixf[:, fi * TC:(fi + 1) * TC], ext[:, 0:TC], m1col, mf32[:], OP.mult, OP.add))
            marks = {}
            ser.dma(lambda e, pj=pj: e.dma_start(
                out=wtf[:].rearrange("p (f c) -> p f c", f=32), in_=w_ins[pj][0]))
            marks[0] = ser.mark()
            for fo in range(32):
                mk = marks[fo]
                ser.op("vector", lambda e: e.tensor_copy(wtb[:], wtf[:]), dwait=mk)
                if fo + 1 < 32:
                    ser.dma(lambda e, pj=pj, fo=fo: e.dma_start(
                        out=wtf[:].rearrange("p (f c) -> p f c", f=32), in_=w_ins[pj][fo + 1]))
                    marks[fo + 1] = ser.mark()
                for fi in range(32):
                    for nh in range(2):
                        pq = pf0 if nh == 0 else pf1
                        ser.op("tensor", lambda e, fi=fi, nh=nh, pq=pq: e.matmul(
                            pq[:], wtb[:, fi * 128:(fi + 1) * 128],
                            mixf[:, fi * TC + nh * 512: fi * TC + (nh + 1) * 512],
                            start=(fi == 0), stop=(fi == 31), skip_group_check=True), dwait=mk)
                for nh in range(2):
                    pq = pf0 if nh == 0 else pf1
                    if pj == 3:
                        ser.op("scalar", lambda e, pq=pq: e.activation(evb[:], pq[:], AF.Silu))
                    else:
                        ser.op("vector", lambda e, pq=pq: e.tensor_copy(evb[:], pq[:]))
                    ser.dma(lambda e, pj=pj, fo=fo, nh=nh: e.dma_start(
                        out=outs[pj][fo * 128:(fo + 1) * 128, nh * 512:(nh + 1) * 512], in_=evb[:]))
        ser.finish(block, chain, dchain)
    return nc


def run_p4(xl2, w_key, w_value, w_receptance, w_gate, tmk, tmv_, tmr, tmg):
    nc = build_p4()
    tm = np.ascontiguousarray(np.stack([tmk, tmv_, tmr, tmg], axis=1))
    ws = [np.ascontiguousarray(w.T.reshape(32, 128, 32, 128).transpose(2, 1, 0, 3))
          for w in (w_key, w_value, w_receptance, w_gate)]
    in_maps = []
    for c in range(NCORES):
        t0 = c * TC
        halo = np.zeros((D, 1), dtype=np.float32) if t0 % S == 0 else \
            np.ascontiguousarray(xl2[:, t0 - 1: t0])
        in_maps.append({"xl": np.ascontiguousarray(xl2[:, t0:t0 + TC]), "halo": halo,
                        "w0": ws[0], "w1": ws[1], "w2": ws[2], "w3": ws[3], "tm": tm})
    res = run_bass_kernel_spmd(nc, in_maps, core_ids=list(range(NCORES)))
    mats = []
    for nm in ("kfm", "vfm", "rfm", "gfm"):
        mats.append(np.concatenate([res.results[c][nm] for c in range(NCORES)], axis=1))
    return mats


def build_p5():
    nc = bass.Bass("TRN2", target_bir_lowering=False)
    kf_in = nc.dram_tensor("kf", [FPC, T], BF16, kind="ExternalInput").ap()
    rf_in = nc.dram_tensor("rf", [FPC, T], BF16, kind="ExternalInput").ap()
    kt_in = nc.dram_tensor("kt", [T, FPC], BF16, kind="ExternalInput").ap()
    vt_in = nc.dram_tensor("vt", [T, FPC], BF16, kind="ExternalInput").ap()
    dft_in = nc.dram_tensor("dft", [4, 128], F32, kind="ExternalInput").ap()
    fcol_in = nc.dram_tensor("fcol", [128, 4], F32, kind="ExternalInput").ap()
    d8f_in = nc.dram_tensor("d8f", [HPC, 128], F32, kind="ExternalInput").ap()
    mkf_in = nc.dram_tensor("mkf", [HPC, 128], F32, kind="ExternalInput").ap()
    gnw_in = nc.dram_tensor("gnw", [HPC, HD1], F32, kind="ExternalInput").ap()
    gnb_in = nc.dram_tensor("gnb", [HPC, HD1], F32, kind="ExternalInput").ap()
    ind8p_in = nc.dram_tensor("ind8p", [HPC, 16 * HD1], F32, kind="ExternalInput").ap()
    ind8b_in = nc.dram_tensor("ind8b", [HPC, NU * CH], F32, kind="ExternalInput").ap()
    taucmp_in = nc.dram_tensor("taucmp", [1, 128], F32, kind="ExternalInput").ap()
    taud_in = nc.dram_tensor("taud", [1, 128], F32, kind="ExternalInput").ap()
    taukd_in = nc.dram_tensor("taukd", [1, 128], F32, kind="ExternalInput").ap()
    taux_in = nc.dram_tensor("taux", [1, 64], F32, kind="ExternalInput").ap()
    taukx_in = nc.dram_tensor("taukx", [1, 64], F32, kind="ExternalInput").ap()
    tauir_in = nc.dram_tensor("tauir", [1, 128], F32, kind="ExternalInput").ap()
    tau31_in = nc.dram_tensor("tau31", [1, 32], F32, kind="ExternalInput").ap()
    msu_in = nc.dram_tensor("msu", [CH, NU * CH], F32, kind="ExternalInput").ap()
    mid_in = nc.dram_tensor("mid", [CH, NU * CH], F32, kind="ExternalInput").ap()
    xn_out = nc.dram_tensor("xno", [FPC, T], BF16, kind="ExternalOutput").ap()

    kf_r = kf_in.rearrange("(ft p) (b cc tau) -> ft cc p b tau", ft=4, b=4, tau=CH)
    rf_r = rf_in.rearrange("(ft p) (b cc tau) -> ft cc p b tau", ft=4, b=4, tau=CH)
    kt_r = kt_in.rearrange("(b cc tau) f -> cc tau b f", b=4, tau=CH)
    vt_r = vt_in.rearrange("(b cc tau) f -> cc tau b f", b=4, tau=CH)
    xo_r = xn_out.rearrange("(h e) (b cc tau) -> h cc e b tau", h=HPC, b=4, tau=CH)

    with ExitStack() as st:
        sb = lambda name, shp, dt: st.enter_context(nc.sbuf_tensor("t_" + name, shp, dt))
        ps = lambda name, shp, dt: st.enter_context(nc.psum_tensor("p_" + name, shp, dt))
        s_ft = [sb(f"sft{i}", [1, 128], F32) for i in range(4)]
        fcol = sb("fcol", [128, 4], F32)
        s8f = sb("s8f", [HPC, 128], F32)
        mskf = sb("mskf", [HPC, 128], F32)
        srep = sb("srep", [1, 4 * FPC], F32)
        tard = sb("tard", [128, 512], BF16)
        takd = sb("takd", [128, 512], BF16)
        tarx = sb("tarx", [128, 256], BF16)
        takx = sb("takx", [128, 256], BF16)
        tir = sb("tir", [128, 512], BF16)
        tfb = sb("tfb", [128, 512], BF16)
        tkh = sb("tkh", [CH, 4 * FPC], BF16)
        tdec = sb("tdec", [128, 16 * HD1], F32)
        gnwb = sb("gnwb", [64, NU * CH], F32)
        gnbb = sb("gnbb", [64, NU * CH], F32)
        msu = sb("msu", [CH, NU * CH], F32)
        midm = sb("midm", [CH, NU * CH], F32)
        taus = {}
        for nm, w in (("taud", 128), ("taukd", 128), ("taux", 64), ("taukx", 64),
                      ("tauir", 128), ("tau31", 32), ("taucmp", 128)):
            taus[nm] = sb("s_" + nm, [1, w], F32)
        clampf = sb("clampf", [128, 128], F32)
        ind8p = sb("ind8p", [HPC, 16 * HD1], F32)
        ind8b = sb("ind8b", [HPC, NU * CH], F32)
        gnw8 = sb("gnw8", [HPC, HD1], F32)
        gnb8 = sb("gnb8", [HPC, HD1], F32)
        ones64 = sb("ones64", [64, 1], F32)
        ones1r = sb("ones1r", [1, 64], F32)
        kc = sb("kc", [128, 512], BF16)
        rc = sb("rc", [128, 512], BF16)
        ktc = sb("ktc", [CH, 4 * FPC], BF16)
        vtc = sb("vtc", [CH, 4 * FPC], BF16)
        kdtf = sb("kdtf", [128, 1024], BF16)
        rtd = sb("rtd", [128, 512], BF16)
        rx = sb("rx", [128, 256], BF16)
        kx = sb("kx", [128, 256], BF16)
        ri = sb("ri", [128, 512], BF16)
        kh = sb("kh", [CH, 4 * FPC], BF16)
        msb = sb("msb", [CH, NU * CH], BF16)
        clA = sb("clA", [CH, NU * CH], F32)
        msc = sb("msc", [CH, NU * CH], BF16)
        sf32 = sb("sf32", [128, 16 * HD1], F32)
        sbf = sb("sbf", [128, 16 * HD1], BF16)
        osb = sb("osb", [64, NU * CH], F32)
        osq = sb("osq", [64, NU * CH], F32)
        t1 = sb("t1", [64, NU * CH], F32)
        xnb = sb("xnb", [64, NU * CH], BF16)
        mrow = sb("mrow", [1, NU * CH], F32)
        mrow2 = sb("mrow2", [1, NU * CH], F32)
        mrow3 = sb("mrow3", [1, NU * CH], F32)
        cgn = sb("cgn", [128, 1], F32)
        PA = ps("PA", [64, 1024], F32)
        PX = ps("PX", [16, 512], F32)
        PO = ps("PO", [64, 1024], F32)
        PS = ps("PS", [128, 512], F32)
        PT = ps("PT", [128, 512], F32)
        chain = st.enter_context(nc.semaphore("chain"))
        dchain = st.enter_context(nc.semaphore("dchain"))
        block = st.enter_context(nc.Block())

        ser = Ser()
        for nm, inp in (("taud", taud_in), ("taukd", taukd_in), ("taux", taux_in),
                        ("taukx", taukx_in), ("tauir", tauir_in), ("tau31", tau31_in),
                        ("taucmp", taucmp_in)):
            ser.dma(lambda e, nm=nm, inp=inp: e.dma_start(out=taus[nm][:], in_=inp))
        ser.dma(lambda e: e.dma_start(out=msu[:], in_=msu_in))
        ser.dma(lambda e: e.dma_start(out=midm[:], in_=mid_in))
        ser.dma(lambda e: e.dma_start(out=ind8p[:], in_=ind8p_in))
        ser.dma(lambda e: e.dma_start(out=ind8b[:], in_=ind8b_in))
        ser.dma(lambda e: e.dma_start(out=gnw8[:], in_=gnw_in))
        ser.dma(lambda e: e.dma_start(out=gnb8[:], in_=gnb_in))
        ser.dma(lambda e: e.dma_start(out=s8f[:], in_=d8f_in))
        ser.dma(lambda e: e.dma_start(out=mskf[:], in_=mkf_in))
        ser.dma(lambda e: e.dma_start(out=fcol[:], in_=fcol_in))
        ser.op("vector", lambda e: e.memset(ones64[:], 1.0))
        ser.op("vector", lambda e: e.memset(ones1r[:], 1.0))
        ser.op("vector", lambda e: e.memset(sf32[:], 0.0))
        ser.op("vector", lambda e: e.memset(sbf[:], 0.0))
        ser.op("vector", lambda e: e.memset(cgn[:], 1e-5))
        for ft in range(4):
            ser.dma(lambda e, ft=ft: e.dma_start(out=s_ft[ft][:], in_=dft_in[ft:ft + 1, :]))
            ser.op("scalar", lambda e, ft=ft: e.activation(s_ft[ft][:], s_ft[ft][:], AF.Exp))
            ser.op("vector", lambda e, ft=ft: e.tensor_scalar(s_ft[ft][:], s_ft[ft][:], -1.0, None, OP.mult))
            ser.op("vector", lambda e, ft=ft: e.tensor_copy(srep[0:1, ft * 128:(ft + 1) * 128], s_ft[ft][:]))
        ser.op("scalar", lambda e: e.activation(s8f[:], s8f[:], AF.Exp))
        ser.op("vector", lambda e: e.tensor_scalar(s8f[:], s8f[:], -1.0, None, OP.mult))
        ser.op("vector", lambda e: e.tensor_mul(s8f[:], s8f[:], mskf[:]))
        for b_ in range(1, 4):
            ser.op("vector", lambda e, b_=b_: e.tensor_copy(
                srep[0:1, b_ * FPC:(b_ + 1) * FPC], srep[0:1, 0:FPC]))
        def mk_tab(dst, tau_nm, width, clamp, post_tf=False):
            for ft in range(4):
                ser.op("tensor", lambda e, ft=ft, tau_nm=tau_nm, width=width: e.matmul(
                    PT[:, 0:width], s_ft[ft][:], taus[tau_nm][:],
                    start=True, stop=True, skip_group_check=True))
                if clamp:
                    ser.op("vector", lambda e, width=width: e.tensor_scalar(
                        clampf[:, 0:width], PT[:, 0:width], 75.0, -75.0, OP.min, OP.max))
                else:
                    ser.op("vector", lambda e, width=width: e.tensor_copy(
                        clampf[:, 0:width], PT[:, 0:width]))
                if post_tf:
                    ser.op("scalar", lambda e, width=width: e.activation(
                        clampf[:, 0:width], clampf[:, 0:width], AF.Exp))
                    ser.op("vector", lambda e, ft=ft, width=width: e.tensor_scalar(
                        dst[:, ft * width:(ft + 1) * width], clampf[:, 0:width],
                        fcol[:, ft:ft + 1], None, OP.mult))
                else:
                    ser.op("scalar", lambda e, ft=ft, width=width: e.activation(
                        dst[:, ft * width:(ft + 1) * width], clampf[:, 0:width], AF.Exp))
        mk_tab(tard, "taud", 128, True)
        mk_tab(takd, "taukd", 128, True)
        mk_tab(tarx, "taux", 64, False)
        mk_tab(takx, "taukx", 64, False)
        mk_tab(tir, "tauir", 128, False)
        mk_tab(tfb, "taucmp", 128, True, post_tf=True)
        for q in range(4):
            ser.op("tensor", lambda e, q=q: e.matmul(
                PT[0:CH, 0:512], taus["tau31"][:], srep[0:1, q * 512:(q + 1) * 512],
                start=True, stop=True, skip_group_check=True))
            ser.op("scalar", lambda e, q=q: e.activation(
                tkh[:, q * 512:(q + 1) * 512], PT[0:CH, 0:512], AF.Exp))
        for q in range(2):
            ser.op("tensor", lambda e, q=q: e.matmul(
                PT[:, 0:512], s8f[:], ind8p[:, q * 512:(q + 1) * 512],
                start=True, stop=True, skip_group_check=True))
            ser.op("scalar", lambda e, q=q: e.activation(
                tdec[:, q * 512:(q + 1) * 512], PT[:, 0:512], AF.Exp, scale=float(CH)))
        for q in range(2):
            ser.op("tensor", lambda e, q=q: e.matmul(
                PT[0:64, 0:512], gnw8[:], ind8b[:, q * 512:(q + 1) * 512],
                start=True, stop=True, skip_group_check=True))
            ser.op("vector", lambda e, q=q: e.tensor_copy(gnwb[:, q * 512:(q + 1) * 512], PT[0:64, 0:512]))
            ser.op("tensor", lambda e, q=q: e.matmul(
                PT[0:64, 0:512], gnb8[:], ind8b[:, q * 512:(q + 1) * 512],
                start=True, stop=True, skip_group_check=True))
            ser.op("vector", lambda e, q=q: e.tensor_copy(gnbb[:, q * 512:(q + 1) * 512], PT[0:64, 0:512]))

        kc_v = kc[:].rearrange("p (g t) -> p g t", t=CH)
        rc_v = rc[:].rearrange("p (g t) -> p g t", t=CH)
        kdtf_v = kdtf[:].rearrange("p (g x) -> p g x", x=64)
        tard_v = tard[:].rearrange("p (g t) -> p g t", t=CH)
        takd_v = takd[:].rearrange("p (g t) -> p g t", t=CH)
        tfb_v = tfb[:].rearrange("p (g t) -> p g t", t=CH)
        tarx_v = tarx[:].rearrange("p (g t) -> p g t", t=16)
        takx_v = takx[:].rearrange("p (g t) -> p g t", t=16)
        rx_v = rx[:].rearrange("p (g t) -> p g t", t=16)
        kx_v = kx[:].rearrange("p (g t) -> p g t", t=16)

        for c in range(NCH):
            for ft_ in range(4):
                ser.dma(lambda e, c=c, ft_=ft_: e.dma_start(
                    out=kc[:, ft_ * 128:(ft_ + 1) * 128].rearrange("p (b tau) -> p b tau", b=4),
                    in_=kf_r[ft_, c]))
                ser.dma(lambda e, c=c, ft_=ft_: e.dma_start(
                    out=rc[:, ft_ * 128:(ft_ + 1) * 128].rearrange("p (b tau) -> p b tau", b=4),
                    in_=rf_r[ft_, c]))
            ser.dma(lambda e, c=c: e.dma_start(
                out=ktc[:].rearrange("t (b f) -> t b f", b=4), in_=kt_r[c]))
            ser.dma(lambda e, c=c: e.dma_start(
                out=vtc[:].rearrange("t (b f) -> t b f", b=4), in_=vt_r[c]))
            ser.op("vector", lambda e: e.tensor_mul(rtd[:], rc[:], tard[:]))
            ser.op("vector", lambda e: e.tensor_tensor(kdtf_v[:, :, 0:CH], kc_v, takd_v, OP.mult))
            ser.op("vector", lambda e: e.tensor_tensor(kdtf_v[:, :, CH:64], kc_v, tfb_v, OP.mult))
            ser.op("vector", lambda e: e.tensor_tensor(rx_v, rc_v[:, :, 16:CH], tarx_v, OP.mult))
            ser.op("vector", lambda e: e.tensor_tensor(kx_v, kc_v[:, :, 0:16], takx_v, OP.mult))
            ser.op("vector", lambda e: e.tensor_mul(ri[:], rc[:], tir[:]))
            ser.op("vector", lambda e: e.tensor_mul(kh[:], ktc[:], tkh[:]))
            for u in range(NU):
                h, b_ = u // 4, u % 4
                ft, pr = h // 2, (h % 2) * 64
                g = ft * 4 + b_
                ser.op("tensor", lambda e, u=u, pr=pr, g=g: e.matmul(
                    PA[:, u * CH:(u + 1) * CH],
                    kdtf[pr:pr + 64, g * 64:(g + 1) * 64],
                    rtd[pr:pr + 64, g * CH:(g + 1) * CH],
                    start=True, stop=True, skip_group_check=True))
                ser.op("tensor", lambda e, u=u, pr=pr, g=g: e.matmul(
                    PX[:, u * 16:(u + 1) * 16],
                    kx[pr:pr + 64, g * 16:(g + 1) * 16],
                    rx[pr:pr + 64, g * 16:(g + 1) * 16],
                    start=True, stop=True, skip_group_check=True))
            ser.op("vector", lambda e: e.tensor_scalar(clA[:], PA[0:CH, :], 1e30, -1e30, OP.min, OP.max))
            ser.op("vector", lambda e: e.tensor_mul(msb[:], clA[:], msu[:]))
            ser.op("vector", lambda e: e.tensor_scalar(clA[:], PA[CH:64, :], 1e30, -1e30, OP.min, OP.max))
            ser.op("vector", lambda e: e.tensor_mul(msc[:], clA[:], midm[:]))
            ser.op("vector", lambda e: e.tensor_add(msb[:], msb[:], msc[:]))
            msb_x = msb[0:16, :].rearrange("p (u t) -> p u t", t=CH)[:, :, 16:CH]
            px_v = PX[:].rearrange("p (u t) -> p u t", t=16)
            ser.op("vector", lambda e: e.tensor_add(msb_x, msb_x, px_v))
            for u in range(NU):
                h, b_ = u // 4, u % 4
                ft, pr = h // 2, (h % 2) * 64
                g = ft * 4 + b_
                fo = b_ * FPC + h * 64
                ser.op("tensor", lambda e, u=u, fo=fo: e.matmul(
                    PO[:, u * CH:(u + 1) * CH],
                    vtc[:, fo:fo + 64], msb[:, u * CH:(u + 1) * CH],
                    start=True, stop=False, skip_group_check=True))
                ser.op("tensor", lambda e, u=u, pr=pr, g=g: e.matmul(
                    PO[:, u * CH:(u + 1) * CH],
                    sbf[pr:pr + 64, g * 64:(g + 1) * 64], ri[pr:pr + 64, g * CH:(g + 1) * CH],
                    start=False, stop=True, skip_group_check=True))
            for half in range(2):
                for ui in range(16):
                    u = half * 16 + ui
                    h, b_ = u // 4, u % 4
                    hh = h % 2
                    g = (h // 2) * 4 + b_
                    fo = b_ * FPC + h * 64
                    ser.op("tensor", lambda e, hh=hh, g=g, half=half, fo=fo: e.matmul(
                        PS[hh * 64:(hh + 1) * 64, (g - half * 8) * 64:(g - half * 8 + 1) * 64],
                        kh[:, fo:fo + 64], vtc[:, fo:fo + 64],
                        start=True, stop=True, skip_group_check=True))
                sl = slice(half * 512, (half + 1) * 512)
                ser.op("vector", lambda e, sl=sl: e.tensor_mul(sf32[:, sl], sf32[:, sl], tdec[:, sl]))
                ser.op("vector", lambda e, sl=sl: e.tensor_add(sf32[:, sl], sf32[:, sl], PS[:]))
            ser.op("vector", lambda e: e.tensor_copy(osb[:], PO[:]))
            ser.op("scalar", lambda e: e.activation(osq[:], osb[:], AF.Square))
            for q in range(2):
                ser.op("tensor", lambda e, q=q: e.matmul(
                    PA[0:1, q * 512:(q + 1) * 512], ones64[:], osb[:, q * 512:(q + 1) * 512],
                    start=True, stop=True, skip_group_check=True))
                ser.op("tensor", lambda e, q=q: e.matmul(
                    PO[0:1, q * 512:(q + 1) * 512], ones64[:], osq[:, q * 512:(q + 1) * 512],
                    start=True, stop=True, skip_group_check=True))
            ser.op("vector", lambda e: e.tensor_scalar(mrow[:], PA[0:1, :], 1.0 / 64, None, OP.mult))
            ser.op("vector", lambda e: e.tensor_scalar(mrow2[:], PO[0:1, :], 1.0 / 64, None, OP.mult))
            ser.op("vector", lambda e: e.tensor_mul(mrow3[:], mrow[:], mrow[:]))
            ser.op("vector", lambda e: e.tensor_sub(mrow2[:], mrow2[:], mrow3[:]))
            ser.op("scalar", lambda e: e.activation(mrow2[:], mrow2[:], AF.Ln, bias=cgn[0:1, 0:1]))
            ser.op("scalar", lambda e: e.activation(mrow2[:], mrow2[:], AF.Exp, scale=-0.5))
            for q in range(2):
                qs = slice(q * 512, (q + 1) * 512)
                ser.op("tensor", lambda e, q=q, qs=qs: e.matmul(
                    PA[0:64, qs], ones1r[:], mrow[0:1, qs],
                    start=True, stop=True, skip_group_check=True))
                ser.op("tensor", lambda e, q=q, qs=qs: e.matmul(
                    PT[0:64, 0:512], ones1r[:], mrow2[0:1, qs],
                    start=True, stop=True, skip_group_check=True))
                ser.op("vector", lambda e, qs=qs: e.tensor_sub(t1[:, qs], osb[:, qs], PA[0:64, qs]))
                ser.op("vector", lambda e, qs=qs: e.tensor_mul(t1[:, qs], t1[:, qs], PT[0:64, 0:512]))
            ser.op("vector", lambda e: e.tensor_mul(t1[:], t1[:], gnwb[:]))
            ser.op("vector", lambda e: e.tensor_add(xnb[:], t1[:], gnbb[:]))
            for h_ in range(HPC):
                ser.dma(lambda e, c=c, h_=h_: e.dma_start(
                    out=xo_r[h_, c],
                    in_=xnb[:, h_ * 128:(h_ + 1) * 128].rearrange("e (b tau) -> e b tau", b=4)))
            ser.op("vector", lambda e: e.tensor_copy(sbf[:], sf32[:]))
        ser.finish(block, chain, dchain)
    return nc


def run_p5(kfm, vfm, rfm, time_decay, time_faaaa, gn_weight, gn_bias):
    nc = build_p5()
    ktm = np.ascontiguousarray(kfm.T)
    vtm = np.ascontiguousarray(vfm.T)
    tau = np.arange(128) % CH
    tau16 = tau % 16
    taud = (tau16 - 8).astype(np.float32)[None, :]
    taukd = (7 - tau16).astype(np.float32)[None, :]
    taucmp = (8 - tau16).astype(np.float32)[None, :]
    tauir = tau.astype(np.float32)[None, :]
    tx = np.arange(64) % 16
    taux = tx.astype(np.float32)[None, :]
    taukx = (15 - tx).astype(np.float32)[None, :]
    tau31 = (31 - np.arange(CH)).astype(np.float32)[None, :]
    ind8p = np.zeros((HPC, 16 * HD1), dtype=np.float32)
    for h in range(HPC):
        for g in range(16):
            if g // 4 == h // 2:
                ind8p[h, g * HD1:(g + 1) * HD1] = 1.0
    ind8b = np.zeros((HPC, NU * CH), dtype=np.float32)
    for h in range(HPC):
        for u in range(NU):
            if u // 4 == h:
                ind8b[h, u * CH:(u + 1) * CH] = 1.0
    tp = np.arange(CH)[:, None]
    tq = np.arange(CH)[None, :]
    msu1 = ((tp < tq) & (tp // 16 == tq // 16)).astype(np.float32)
    mid1 = (tp == tq).astype(np.float32)
    msu = np.tile(msu1, (1, NU))
    mid = np.tile(mid1, (1, NU))
    in_maps = []
    for c in range(NCORES):
        fs = c * FPC
        hs = c * HPC
        dsl = time_decay[hs:hs + HPC]
        dft = dsl.reshape(4, 128)
        fsl = time_faaaa[hs:hs + HPC]
        fcol = np.ascontiguousarray(fsl.reshape(4, 128).T)
        d8f = np.zeros((HPC, 128), dtype=np.float32)
        mkf = np.zeros((HPC, 128), dtype=np.float32)
        for h in range(HPC):
            off = (h % 2) * 64
            d8f[h, off:off + 64] = dsl[h]
            mkf[h, off:off + 64] = 1.0
        in_maps.append({
            "kf": np.ascontiguousarray(kfm[fs:fs + FPC]),
            "rf": np.ascontiguousarray(rfm[fs:fs + FPC]),
            "kt": np.ascontiguousarray(ktm[:, fs:fs + FPC]),
            "vt": np.ascontiguousarray(vtm[:, fs:fs + FPC]),
            "dft": np.ascontiguousarray(dft.astype(np.float32)),
            "fcol": fcol.astype(np.float32),
            "d8f": d8f, "mkf": mkf,
            "gnw": np.ascontiguousarray(gn_weight[fs:fs + FPC].reshape(HPC, HD1)),
            "gnb": np.ascontiguousarray(gn_bias[fs:fs + FPC].reshape(HPC, HD1)),
            "ind8p": ind8p, "ind8b": ind8b,
            "taud": taud, "taukd": taukd, "taux": taux, "taukx": taukx,
            "tauir": tauir, "tau31": tau31, "taucmp": taucmp,
            "msu": msu, "mid": mid,
        })
    res = run_bass_kernel_spmd(nc, in_maps, core_ids=list(range(NCORES)))
    xn = np.concatenate([res.results[c]["xno"] for c in range(NCORES)], axis=0)
    return xn


NSK = 50


def build_p6():
    nc = bass.Bass("TRN2", target_bir_lowering=False)
    xnf_in = nc.dram_tensor("xnf", [D, TC], BF16, kind="ExternalInput").ap()
    gf_in = nc.dram_tensor("gf", [D, TC], BF16, kind="ExternalInput").ap()
    x2_in = nc.dram_tensor("x2", [TC, D], F32, kind="ExternalInput").ap()
    wot_in = nc.dram_tensor("wot", [8, 128, 32, 512], F32, kind="ExternalInput").ap()
    wspt_in = nc.dram_tensor("wspt", [D, 24], F32, kind="ExternalInput").ap()
    bvec_in = nc.dram_tensor("bvec", [1, 24], F32, kind="ExternalInput").ap()
    gate_in = nc.dram_tensor("gate", [1, 1], F32, kind="ExternalInput").ap()
    wb_in = nc.dram_tensor("wb", [1024, 1024], F32, kind="ExternalInput").ap()
    idf_in = nc.dram_tensor("idf", [128, 128], F32, kind="ExternalInput").ap()
    out_out = nc.dram_tensor("out", [TC, D], F32, kind="ExternalOutput").ap()
    x3_scr = nc.dram_tensor("x3_scr", [TC, D], F32, kind="Internal").ap()

    with ExitStack() as st:
        sb = lambda name, shp, dt: st.enter_context(nc.sbuf_tensor("t_" + name, shp, dt))
        ps = lambda name, shp, dt: st.enter_context(nc.psum_tensor("p_" + name, shp, dt))
        stga = sb("stga", [128, TC], BF16)
        stgb = sb("stgb", [128, TC], BF16)
        gxnb = sb("gxnb", [128, 32 * TC], BF16)
        ldf = sb("ldf", [128, 8 * 512], F32)
        wtb = sb("wtb", [128, 32 * 512], BF16)
        xr = sb("xr", [128, 512], F32)
        ev = sb("ev", [128, 512], F32)
        sx3 = sb("sx3", [128, D], F32)
        sxn = sb("sxn", [128, D], F32)
        sq = sb("sq", [128, D], F32)
        cols = sb("cols", [128, 16], F32)
        ceps = sb("ceps", [128, 1], F32)
        crms = sb("crms", [128, 1], F32)
        xnT = sb("xnT", [128, 32 * 128], BF16)
        idf = sb("idf", [128, 128], F32)
        wsptb = sb("wsptb", [128, 32 * 24], BF16)
        bvbc = sb("bvbc", [128, 24], F32)
        onesr = sb("onesr", [1, 128], F32)
        lg24 = sb("lg24", [128, 24], F32)
        hpp = sb("hpp", [128, 8 * 8], F32)
        z16all = sb("z16all", [128, 8 * 16], F32)
        t16 = sb("t16", [128, 16], F32)
        e16 = sb("e16", [128, 16], F32)
        mx4 = sb("mx4", [128, 4], F32)
        s4 = sb("s4", [128, 4], F32)
        ln4 = sb("ln4", [128, 4], F32)
        uneg = sb("uneg", [128, 4], F32)
        vneg = sb("vneg", [128, 4], F32)
        hrt = sb("hrt", [128, 8 * 16], F32)
        gg = sb("gg", [1, 1], F32)
        gcol = sb("gcol", [128, 1], F32)
        gm1 = sb("gm1", [128, 1], F32)
        wbb = sb("wbb", [128, 8 * 1024], BF16)
        to = sb("to", [128, 1024], F32)
        xp = ldf[:, 0:1024]
        y_sb = ldf[:, 1024:2048]
        pf = [ps(f"pf{i}", [128, 512], F32) for i in range(4)]
        chain = st.enter_context(nc.semaphore("chain"))
        dchain = st.enter_context(nc.semaphore("dchain"))
        block = st.enter_context(nc.Block())
        ptrf = pf[2][:, 0:128]
        plg = pf[1][:, 0:24]
        pbv = pf[1][:, 128:152]
        pg128 = pf[1][:, 256:257]

        ser = Ser()
        ser.dma(lambda e: e.dma_start(out=idf[:], in_=idf_in))
        ser.op("vector", lambda e: e.memset(onesr[:], 1.0))
        ser.op("vector", lambda e: e.memset(ceps[:], LN_EPS))
        ser.op("vector", lambda e: e.memset(crms[:], RMS_EPS))
        ser.dma(lambda e: e.dma_start(out=gg[:], in_=gate_in))
        for fi in range(32):
            ser.dma(lambda e, fi=fi: e.dma_start(out=stga[:], in_=xnf_in[fi * 128:(fi + 1) * 128, :]))
            ser.dma(lambda e, fi=fi: e.dma_start(out=stgb[:], in_=gf_in[fi * 128:(fi + 1) * 128, :]))
            ser.op("vector", lambda e, fi=fi: e.tensor_mul(gxnb[:, fi * TC:(fi + 1) * TC], stga[:], stgb[:]))
        wmarks = {}
        for fog in range(8):
            for fq in range(4):
                ser.dma(lambda e, fog=fog, fq=fq: e.dma_start(
                    out=ldf[:].rearrange("p (f c) -> p f c", f=8),
                    in_=wot_in[fog][:, fq * 8:(fq + 1) * 8, :]))
                ser.op("vector", lambda e, fq=fq: e.tensor_copy(
                    wtb[:, fq * 4096:(fq + 1) * 4096], ldf[:]))
            mkw = ser.mark()
            for rnd in range(2):
                for fi in range(32):
                    for ti in range(4):
                        tt = rnd * 4 + ti
                        pv = pf[ti][:]
                        ser.op("tensor", lambda e, fi=fi, tt=tt, pv=pv: e.matmul(
                            pv, gxnb[:, fi * TC + tt * 128: fi * TC + (tt + 1) * 128],
                            wtb[:, fi * 512:(fi + 1) * 512],
                            start=(fi == 0), stop=(fi == 31), skip_group_check=True), dwait=mkw)
                for ti in range(4):
                    tt = rnd * 4 + ti
                    co = fog * 512
                    pv = pf[ti][:]
                    ser.dma(lambda e, tt=tt, co=co: e.dma_start(
                        out=xr[:], in_=x2_in[tt * 128:(tt + 1) * 128, co:co + 512]))
                    ser.op("vector", lambda e, pv=pv: e.tensor_add(ev[:, 0:512], pv, xr[:]))
                    ser.dma(lambda e, tt=tt, co=co: e.dma_start(
                        out=x3_scr[tt * 128:(tt + 1) * 128, co:co + 512], in_=ev[:, 0:512]))

        ser.dma(lambda e: e.dma_start(
            out=ldf[:, 0:768].rearrange("p (f e) -> p f e", f=32),
            in_=wspt_in.rearrange("(f p) e -> p f e", p=128)))
        ser.op("vector", lambda e: e.tensor_copy(wsptb[:], ldf[:, 0:768]))
        ser.dma(lambda e: e.dma_start(out=ldf[0:1, 0:24], in_=bvec_in))
        ser.op("tensor", lambda e: e.matmul(pbv, onesr[:], ldf[0:1, 0:24],
                                            start=True, stop=True, skip_group_check=True))
        ser.op("vector", lambda e: e.tensor_copy(bvbc[:], pbv))
        for ci in range(8):
            ser.dma(lambda e, ci=ci: e.dma_start(out=ldf[:, 0:1024], in_=wb_in[ci * 128:(ci + 1) * 128, :]))
            ser.op("vector", lambda e, ci=ci: e.tensor_copy(wbb[:, ci * 1024:(ci + 1) * 1024], ldf[:, 0:1024]))
        c5, c6, c7, c8, c9, c10, c11 = (cols[:, 8 + i:9 + i] for i in range(7))
        for tt in range(NT):
            ser.dma(lambda e, tt=tt: e.dma_start(out=sx3[:], in_=x3_scr[tt * 128:(tt + 1) * 128, :]))
            _ln_ops(ser, sx3[:], sxn[:], sq[:], cols, D, ceps[:, 0:1])
            ser.op("scalar", lambda e: e.activation(sq[:], sxn[:], AF.Square))
            ser.op("vector", lambda e: e.reduce_sum(c5, sq[:], AX))
            ser.op("vector", lambda e: e.tensor_scalar(c5, c5, 1.0 / D, None, OP.mult))
            ser.op("scalar", lambda e: e.activation(c6, c5, AF.Ln, bias=crms[:, 0:1]))
            ser.op("scalar", lambda e: e.activation(c7, c6, AF.Exp, scale=-0.5))
            ser.op("vector", lambda e: e.tensor_scalar(sq[:], sxn[:], c7, None, OP.mult))
            for fi in range(32):
                ser.op("tensor", lambda e, fi=fi: e.transpose(ptrf, sq[:, fi * 128:(fi + 1) * 128], idf[:]))
                ser.op("vector", lambda e, fi=fi: e.tensor_copy(xnT[:, fi * 128:(fi + 1) * 128], ptrf))
            for fi in range(32):
                ser.op("tensor", lambda e, fi=fi: e.matmul(
                    plg, xnT[:, fi * 128:(fi + 1) * 128], wsptb[:, fi * 24:(fi + 1) * 24],
                    start=(fi == 0), stop=(fi == 31), skip_group_check=True))
            ser.op("vector", lambda e: e.tensor_add(lg24[:], plg, bvbc[:]))
            for gi, off in ((0, 16), (1, 20)):
                ser.op("vector", lambda e, off=off: e.reduce_max(c8, lg24[:, off:off + 4], AX))
                ser.op("vector", lambda e: e.tensor_scalar(c9, c8, -1.0, None, OP.mult))
                hv = hpp[:, tt * 8 + gi * 4: tt * 8 + gi * 4 + 4]
                ser.op("scalar", lambda e, off=off, hv=hv: e.activation(
                    hv, lg24[:, off:off + 4], AF.Exp, bias=c9))
                ser.op("vector", lambda e, hv=hv: e.reduce_sum(c10, hv, AX))
                ser.op("vector", lambda e: e.reciprocal(c11, c10))
                ser.op("vector", lambda e, hv=hv: e.tensor_scalar(hv, hv, c11, None, OP.mult))
            ser.op("vector", lambda e, tt=tt: e.tensor_scalar(
                z16all[:, tt * 16:(tt + 1) * 16], lg24[:, 0:16], 10.0, None, OP.mult))
        ser.op("scalar", lambda e: e.activation(gg[:], gg[:], AF.Sigmoid))
        ser.op("tensor", lambda e: e.matmul(pg128, onesr[:], gg[:],
                                            start=True, stop=True, skip_group_check=True))
        ser.op("vector", lambda e: e.tensor_copy(gcol[:], pg128))
        ser.op("vector", lambda e: e.tensor_scalar(gm1[:], gcol[:], -1.0, 1.0, OP.mult, OP.add))
        for tt in range(NT):
            Z = z16all[:, tt * 16:(tt + 1) * 16]
            Zv = Z.rearrange("p (i j) -> p i j", i=4)
            t16v = t16[:].rearrange("p (i j) -> p i j", i=4)
            e16v = e16[:].rearrange("p (i j) -> p i j", i=4)
            ser.op("vector", lambda e, Z=Z: e.tensor_copy(t16[:], Z))
            ser.op("vector", lambda e: e.memset(vneg[:], 0.0))
            for it in range(NSK):
                for j in range(4):
                    ser.op("vector", lambda e, Z=Zv, j=j: e.tensor_scalar(
                        t16v[:, :, j], Z[:, :, j], vneg[:, j:j + 1], None, OP.subtract))
                ser.op("vector", lambda e: e.reduce_max(mx4[:], t16v, AX))
                for i in range(4):
                    ser.op("vector", lambda e, i=i: e.tensor_scalar(
                        t16[:, i * 4:(i + 1) * 4], t16[:, i * 4:(i + 1) * 4],
                        mx4[:, i:i + 1], None, OP.subtract))
                ser.op("scalar", lambda e: e.activation(e16[:], t16[:], AF.Exp))
                ser.op("vector", lambda e: e.reduce_sum(s4[:], e16v, AX))
                ser.op("scalar", lambda e: e.activation(ln4[:], s4[:], AF.Ln))
                ser.op("vector", lambda e: e.tensor_add(uneg[:], mx4[:], ln4[:]))
                for i in range(4):
                    ser.op("vector", lambda e, Z=Z, i=i: e.tensor_scalar(
                        t16[:, i * 4:(i + 1) * 4], Z[:, i * 4:(i + 1) * 4],
                        uneg[:, i:i + 1], None, OP.subtract))
                tjv = t16[:].rearrange("p (i j) -> p j i", i=4)
                ejv = e16[:].rearrange("p (i j) -> p j i", i=4)
                ser.op("vector", lambda e, tjv=tjv: e.reduce_max(mx4[:], tjv, AX))
                for j in range(4):
                    ser.op("vector", lambda e, j=j: e.tensor_scalar(
                        t16v[:, :, j], t16v[:, :, j], mx4[:, j:j + 1], None, OP.subtract))
                ser.op("scalar", lambda e: e.activation(e16[:], t16[:], AF.Exp))
                ser.op("vector", lambda e, ejv=ejv: e.reduce_sum(s4[:], ejv, AX))
                ser.op("scalar", lambda e: e.activation(ln4[:], s4[:], AF.Ln))
                ser.op("vector", lambda e: e.tensor_add(vneg[:], mx4[:], ln4[:]))
            for i in range(4):
                ser.op("vector", lambda e, Z=Z, i=i: e.tensor_scalar(
                    t16[:, i * 4:(i + 1) * 4], Z[:, i * 4:(i + 1) * 4],
                    uneg[:, i:i + 1], None, OP.subtract))
            for j in range(4):
                ser.op("vector", lambda e, j=j: e.tensor_scalar(
                    t16v[:, :, j], t16v[:, :, j], vneg[:, j:j + 1], None, OP.subtract))
            ser.op("scalar", lambda e: e.activation(e16[:], t16[:], AF.Exp))
            hv = hrt[:, tt * 16:(tt + 1) * 16]
            ser.op("vector", lambda e, hv=hv: e.tensor_scalar(hv, e16[:], gcol[:, 0:1], None, OP.mult))
            for ii in range(4):
                col = tt * 16 + ii * 5
                ser.op("vector", lambda e, col=col: e.tensor_scalar(
                    hrt[:, col:col + 1], hrt[:, col:col + 1], gm1[:, 0:1], None, OP.add))
        for tt in range(NT):
            ser.dma(lambda e, tt=tt: e.dma_start(out=sx3[:], in_=x3_scr[tt * 128:(tt + 1) * 128, :]))
            _ln_ops(ser, sx3[:], sxn[:], sq[:], cols, D, ceps[:, 0:1])
            ser.op("vector", lambda e, tt=tt: e.tensor_scalar(
                xp[:], sxn[:, 0:1024], hpp[:, tt * 8: tt * 8 + 1], None, OP.mult))
            for j in range(1, 4):
                ser.op("vector", lambda e, tt=tt, j=j: e.scalar_tensor_tensor(
                    xp[:], sxn[:, j * 1024:(j + 1) * 1024], hpp[:, tt * 8 + j: tt * 8 + j + 1],
                    xp[:], OP.mult, OP.add))
            for ci in range(8):
                ser.op("tensor", lambda e, ci=ci: e.transpose(ptrf, xp[:, ci * 128:(ci + 1) * 128], idf[:]))
                ser.op("vector", lambda e, ci=ci: e.tensor_copy(xnT[:, ci * 128:(ci + 1) * 128], ptrf))
            for nh in range(2):
                pv = pf[nh]
                for ci in range(8):
                    ser.op("tensor", lambda e, ci=ci, nh=nh, pv=pv: e.matmul(
                        pv[:], xnT[:, ci * 128:(ci + 1) * 128],
                        wbb[:, ci * 1024 + nh * 512: ci * 1024 + (nh + 1) * 512],
                        start=(ci == 0), stop=(ci == 7), skip_group_check=True))
                ser.op("vector", lambda e, nh=nh, pv=pv: e.tensor_copy(y_sb[:, nh * 512:(nh + 1) * 512], pv[:]))
            for i in range(4):
                ser.op("vector", lambda e, tt=tt, i=i: e.tensor_scalar(
                    to[:], sxn[:, 0:1024], hrt[:, tt * 16 + i * 4: tt * 16 + i * 4 + 1], None, OP.mult))
                for j in range(1, 4):
                    ser.op("vector", lambda e, tt=tt, i=i, j=j: e.scalar_tensor_tensor(
                        to[:], sxn[:, j * 1024:(j + 1) * 1024],
                        hrt[:, tt * 16 + i * 4 + j: tt * 16 + i * 4 + j + 1],
                        to[:], OP.mult, OP.add))
                ser.op("vector", lambda e, tt=tt, i=i: e.scalar_tensor_tensor(
                    to[:], y_sb[:], hpp[:, tt * 8 + 4 + i: tt * 8 + 4 + i + 1],
                    to[:], OP.mult, OP.add))
                ser.op("vector", lambda e, i=i: e.tensor_add(to[:], to[:], sx3[:, i * 1024:(i + 1) * 1024]))
                ser.dma(lambda e, tt=tt, i=i: e.dma_start(
                    out=out_out[tt * 128:(tt + 1) * 128, i * 1024:(i + 1) * 1024], in_=to[:]))
        ser.finish(block, chain, dchain)
    return nc


def run_p6(xnf, gfm, x2, w_output, W_res, W_pre, W_post, H_res_base, H_pre_base,
           H_post_base, gate, W_branch):
    nc = build_p6()
    idf = np.eye(128, dtype=np.float32)
    wot = np.ascontiguousarray(w_output.T.reshape(32, 128, 8, 512).transpose(2, 1, 0, 3))
    wspt = np.ascontiguousarray(np.concatenate([W_res, W_pre, W_post], axis=0).T)
    bvec = np.concatenate([H_res_base.reshape(-1), H_pre_base, H_post_base])[None, :].astype(np.float32)
    in_maps = []
    for c in range(NCORES):
        t0 = c * TC
        in_maps.append({
            "xnf": np.ascontiguousarray(xnf[:, t0:t0 + TC]),
            "gf": np.ascontiguousarray(gfm[:, t0:t0 + TC]),
            "x2": np.ascontiguousarray(x2[t0:t0 + TC]),
            "wot": wot, "wspt": wspt, "bvec": bvec,
            "gate": gate.reshape(1, 1).astype(np.float32),
            "wb": W_branch, "idf": idf,
        })
    res = run_bass_kernel_spmd(nc, in_maps, core_ids=list(range(NCORES)))
    out = np.concatenate([res.results[c]["out"] for c in range(NCORES)], axis=0)
    return out


def kernel(L0_attn_hidden_states, L0_attn_Wq, L0_attn_Wkv, L0_attn_Wo,
           L0_ffn_router_w, L0_ffn_A, L0_ffn_B,
           L1_attn_w_key, L1_attn_w_value, L1_attn_w_receptance, L1_attn_w_gate,
           L1_attn_w_output, L1_attn_time_mix_key, L1_attn_time_mix_value,
           L1_attn_time_mix_receptance, L1_attn_time_mix_gate, L1_attn_time_decay,
           L1_attn_time_faaaa, L1_attn_gn_weight, L1_attn_gn_bias,
           L1_ffn_W_res, L1_ffn_W_pre, L1_ffn_W_post, L1_ffn_H_res_base,
           L1_ffn_H_pre_base, L1_ffn_H_post_base, L1_ffn_gate, L1_ffn_W_branch):
    x = np.ascontiguousarray(np.asarray(L0_attn_hidden_states, dtype=np.float32).reshape(T, D))
    qfm, kv = run_p1(x, np.asarray(L0_attn_Wq), np.asarray(L0_attn_Wkv))
    ao = run_p2(qfm, kv)
    x2, xl2 = run_p3(ao, x, np.asarray(L0_attn_Wo), np.asarray(L0_ffn_router_w),
                     np.asarray(L0_ffn_A), np.asarray(L0_ffn_B))
    kfm, vfm, rfm, gfm = run_p4(xl2, np.asarray(L1_attn_w_key), np.asarray(L1_attn_w_value),
                                np.asarray(L1_attn_w_receptance), np.asarray(L1_attn_w_gate),
                                np.asarray(L1_attn_time_mix_key), np.asarray(L1_attn_time_mix_value),
                                np.asarray(L1_attn_time_mix_receptance), np.asarray(L1_attn_time_mix_gate))
    xnf = run_p5(kfm, vfm, rfm, np.asarray(L1_attn_time_decay), np.asarray(L1_attn_time_faaaa),
                 np.asarray(L1_attn_gn_weight), np.asarray(L1_attn_gn_bias))
    out = run_p6(xnf, gfm, x2, np.asarray(L1_attn_w_output), np.asarray(L1_ffn_W_res),
                 np.asarray(L1_ffn_W_pre), np.asarray(L1_ffn_W_post),
                 np.asarray(L1_ffn_H_res_base), np.asarray(L1_ffn_H_pre_base),
                 np.asarray(L1_ffn_H_post_base), np.asarray(L1_ffn_gate),
                 np.asarray(L1_ffn_W_branch))
    return out.reshape(B, S, D)


# version 04 (final)
# speedup vs baseline: 1.5771x; 1.0315x over previous
from contextlib import ExitStack

import numpy as np
import ml_dtypes
import concourse.bass as bass
import concourse.mybir as mybir
from concourse.bass_utils import run_bass_kernel_spmd

F32 = mybir.dt.float32
BF16 = mybir.dt.bfloat16
AX = mybir.AxisListType.X
AF = mybir.ActivationFunctionType
OP = mybir.AluOpType
BF = ml_dtypes.bfloat16

NCORES = 8
B, S, D = 4, 2048, 4096
T = B * S
TC = T // NCORES
NT = TC // 128
NH0, HD0 = 32, 128
NH1, HD1 = 64, 64
HPC = NH1 // NCORES
FPC = HPC * HD1
LN_EPS = 1e-5
RMS_EPS = 1.1920929e-07
QSCALE = float(HD0) ** -0.5
NU = 32
CH = 32
NCH = S // CH


def _alibi_slopes(n):
    start = 2.0 ** (-(2.0 ** (-(np.log2(n) - 3))))
    return np.array([start ** (i + 1) for i in range(n)], dtype=np.float64)


class Ser:

    def __init__(self):
        self.ops = []
        self.ncomp = 0
        self.ndma = 0

    def op(self, eng, fn, dwait=None):
        self.ops.append((eng, fn, "c", dwait))
        self.ncomp += 1

    def dma(self, fn):
        self.ops.append(("sync", fn, "d", None))
        self.ndma += 1

    def mark(self):
        return self.ndma

    def finish(self, block, chain, dchain):
        comp_before = []
        dma_before = []
        nc_ = nd_ = 0
        for e, fn, k, dw in self.ops:
            comp_before.append(nc_)
            dma_before.append(nd_)
            if k == "c":
                nc_ += 1
            else:
                nd_ += 1
        total_c, total_d = nc_, nd_

        def runner_for(name):
            def runner(eng):
                for i, (e, fn, k, dw) in enumerate(self.ops):
                    if e != name:
                        continue
                    if k == "d":
                        eng.wait_ge(chain, comp_before[i])
                        fn(eng).then_inc(dchain, 16)
                    else:
                        nd = dma_before[i] if dw is None else dw
                        eng.wait_ge(dchain, 16 * nd)
                        eng.wait_ge(chain, comp_before[i])
                        fn(eng).then_inc(chain, 1)
                if name == "gpsimd":
                    eng.wait_ge(chain, total_c)
                    eng.wait_ge(dchain, 16 * total_d)
            return runner

        block.sync(runner_for("sync"))
        block.vector(runner_for("vector"))
        block.scalar(runner_for("scalar"))
        block.tensor(runner_for("tensor"))
        block.gpsimd(runner_for("gpsimd"))


def _ln_ops(ser, x, xn, sq, cols, n, eps_col):
    mean, msq, negv, lnv, rstd = (cols[:, i:i + 1] for i in range(5))
    ser.op("vector", lambda e: e.reduce_sum(mean, x, AX))
    ser.op("vector", lambda e: e.tensor_scalar(mean, mean, 1.0 / n, None, OP.mult))
    ser.op("scalar", lambda e: e.activation(sq, x, AF.Square))
    ser.op("vector", lambda e: e.reduce_sum(msq, sq, AX))
    ser.op("vector", lambda e: e.tensor_scalar(msq, msq, 1.0 / n, None, OP.mult))
    ser.op("vector", lambda e: e.scalar_tensor_tensor(negv, mean, mean, msq, OP.mult, OP.subtract))
    ser.op("scalar", lambda e: e.activation(lnv, negv, AF.Ln, bias=eps_col, scale=-1.0))
    ser.op("scalar", lambda e: e.activation(rstd, lnv, AF.Exp, scale=-0.5))
    ser.op("vector", lambda e: e.tensor_scalar(xn, x, mean, rstd, OP.subtract, OP.mult))


def build_p1():
    nc = bass.Bass("TRN2", target_bir_lowering=False)
    x_in = nc.dram_tensor("x", [TC, D], F32, kind="ExternalInput").ap()
    wq_in = nc.dram_tensor("wq", [32, 128, 32, 128], F32, kind="ExternalInput").ap()
    wkv_in = nc.dram_tensor("wkv", [D, 256], F32, kind="ExternalInput").ap()
    idf_in = nc.dram_tensor("idf", [128, 128], F32, kind="ExternalInput").ap()
    qfm_out = nc.dram_tensor("qfm", [D, TC], F32, kind="ExternalOutput").ap()
    kv_out = nc.dram_tensor("kv", [TC, 256], F32, kind="ExternalOutput").ap()

    with ExitStack() as st:
        sb = lambda name, shp, dt: st.enter_context(nc.sbuf_tensor("t_" + name, shp, dt))
        ps = lambda name, shp, dt: st.enter_context(nc.psum_tensor("p_" + name, shp, dt))
        sx = sb("sx", [128, D], F32)
        sq = sb("sq", [128, D], F32)
        cols = sb("cols", [128, 8], F32)
        ceps = sb("ceps", [128, 1], F32)
        xn = sb("xn", [128, D], F32)
        xnT = sb("xnT", [128, 32 * TC], BF16)
        idf = sb("idf", [128, 128], F32)
        wtf = sb("wtf", [128, 32 * 128], F32)
        wtb = sb("wtb", [128, 32 * 128], BF16)
        wkvf = sb("wkvf", [128, 256], F32)
        wkvb = sb("wkvb", [128, 32 * 256], BF16)
        evac = sb("evac", [128, 512], F32)
        pf0 = ps("pf0", [128, 512], F32)
        pf1 = ps("pf1", [128, 512], F32)
        pf2 = ps("pf2", [128, 512], F32)
        ptrf = pf2[:, 0:128]
        pkv = ps("pkv", [128, 256], F32)
        chain = st.enter_context(nc.semaphore("chain"))
        dchain = st.enter_context(nc.semaphore("dchain"))
        block = st.enter_context(nc.Block())

        ser = Ser()
        ser.dma(lambda e: e.dma_start(out=idf[:], in_=idf_in))
        ser.op("vector", lambda e: e.memset(ceps[:], LN_EPS))
        for tt in range(NT):
            ser.dma(lambda e, tt=tt: e.dma_start(out=sx[:], in_=x_in[tt * 128:(tt + 1) * 128, :]))
            _ln_ops(ser, sx[:], xn[:], sq[:], cols, D, ceps[:, 0:1])
            for fi in range(32):
                ser.op("tensor", lambda e, fi=fi: e.transpose(ptrf, xn[:, fi * 128:(fi + 1) * 128], idf[:]))
                ser.op("vector", lambda e, fi=fi, tt=tt:
                       e.tensor_copy(xnT[:, fi * TC + tt * 128: fi * TC + (tt + 1) * 128], ptrf))
        marks = {}
        ser.dma(lambda e: e.dma_start(
            out=wtf[:].rearrange("p (f c) -> p f c", f=32), in_=wq_in[0]))
        marks[0] = ser.mark()
        for fo in range(32):
            mk = marks[fo]
            ser.op("vector", lambda e: e.tensor_copy(wtb[:], wtf[:]), dwait=mk)
            if fo + 1 < 32:
                ser.dma(lambda e, fo=fo: e.dma_start(
                    out=wtf[:].rearrange("p (f c) -> p f c", f=32), in_=wq_in[fo + 1]))
                marks[fo + 1] = ser.mark()
            for fi in range(32):
                for th in range(2):
                    pq = pf0 if th == 0 else pf1
                    ser.op("tensor", lambda e, fi=fi, th=th, pq=pq: e.matmul(
                        pq[:], wtb[:, fi * 128:(fi + 1) * 128],
                        xnT[:, fi * TC + th * 512: fi * TC + (th + 1) * 512],
                        start=(fi == 0), stop=(fi == 31), skip_group_check=True), dwait=mk)
            for th in range(2):
                pq = pf0 if th == 0 else pf1
                ser.op("vector", lambda e, pq=pq: e.tensor_copy(evac[:], pq[:]))
                ser.dma(lambda e, fo=fo, th=th: e.dma_start(
                    out=qfm_out[fo * 128:(fo + 1) * 128, th * 512:(th + 1) * 512], in_=evac[:]))
        for fi in range(32):
            ser.dma(lambda e, fi=fi: e.dma_start(out=wkvf[:], in_=wkv_in[fi * 128:(fi + 1) * 128, :]))
            ser.op("vector", lambda e, fi=fi: e.tensor_copy(wkvb[:, fi * 256:(fi + 1) * 256], wkvf[:]))
        for tt in range(NT):
            for fi in range(32):
                ser.op("tensor", lambda e, tt=tt, fi=fi: e.matmul(
                    pkv[:], xnT[:, fi * TC + tt * 128: fi * TC + (tt + 1) * 128],
                    wkvb[:, fi * 256:(fi + 1) * 256],
                    start=(fi == 0), stop=(fi == 31), skip_group_check=True))
            ser.op("vector", lambda e: e.tensor_copy(evac[:, 0:256], pkv[:]))
            ser.dma(lambda e, tt=tt: e.dma_start(
                out=kv_out[tt * 128:(tt + 1) * 128, :], in_=evac[:, 0:256]))
        ser.finish(block, chain, dchain)
    return nc


def run_p1(x_full, wq, wkv):
    nc = build_p1()
    idf = np.eye(128, dtype=np.float32)
    wq_p = np.ascontiguousarray(wq.reshape(32, 128, 32, 128).transpose(2, 1, 0, 3))
    in_maps = [{"x": np.ascontiguousarray(x_full[c * TC:(c + 1) * TC]),
                "wq": wq_p, "wkv": wkv, "idf": idf} for c in range(NCORES)]
    res = run_bass_kernel_spmd(nc, in_maps, core_ids=list(range(NCORES)))
    qfm = np.concatenate([res.results[c]["qfm"] for c in range(NCORES)], axis=1)
    kv = np.concatenate([res.results[c]["kv"] for c in range(NCORES)], axis=0)
    return qfm, kv


NSLOT = 4
KT_OF_SLOT = [4 * (s + 1) for s in range(NSLOT)]
NKT = sum(KT_OF_SLOT)


def build_p2():
    nc = bass.Bass("TRN2", target_bir_lowering=False)
    qb_in = nc.dram_tensor("qb", [D, NSLOT * 256], F32, kind="ExternalInput").ap()
    kf_in = nc.dram_tensor("kf", [HD0, S], F32, kind="ExternalInput").ap()
    vt_in = nc.dram_tensor("vt", [S, HD0], F32, kind="ExternalInput").ap()
    pmjd_in = nc.dram_tensor("pmjd", [128, NKT * 256], F32, kind="ExternalInput").ap()
    ao_out = nc.dram_tensor("ao", [D, NSLOT * 256], F32, kind="ExternalOutput").ap()

    slopes = _alibi_slopes(NH0)
    with ExitStack() as st:
        sb = lambda name, shp, dt: st.enter_context(nc.sbuf_tensor("t_" + name, shp, dt))
        ps = lambda name, shp, dt: st.enter_context(nc.psum_tensor("p_" + name, shp, dt))
        kf_f = sb("kf_f", [128, S], F32)
        kf_b = sb("kf_b", [128, S], BF16)
        v_f = sb("v_f", [128, 16 * HD0], F32)
        v_b = sb("v_b", [128, 16 * HD0], BF16)
        pmjd = sb("pmjd", [128, NKT * 256], F32)
        q_f = sb("q_f", [128, NSLOT * 256], F32)
        q_b = sb("q_b", [128, NSLOT * 256], BF16)
        earg = sb("earg", [128, 256], F32)
        expT = sb("expT", [128, 16 * 256], BF16)
        ones1 = sb("ones1", [128, 1], BF16)
        onesr = sb("onesr", [1, 128], F32)
        recip = sb("recip", [1, 256], F32)
        cm16 = sb("cm16", [128, 1], F32)
        recb = sb("recb", [128, 256], F32)
        aout = sb("aout", [128, 256], F32)
        psc = ps("psc", [128, 256], F32)
        pav = ps("pav", [128, 256], F32)
        psum1 = ps("psum1", [1, 256], F32)
        prb = ps("prb", [128, 256], F32)
        chain = st.enter_context(nc.semaphore("chain"))
        dchain = st.enter_context(nc.semaphore("dchain"))
        block = st.enter_context(nc.Block())

        ser = Ser()
        ser.dma(lambda e: e.dma_start(out=kf_f[:], in_=kf_in))
        ser.op("vector", lambda e: e.tensor_copy(kf_b[:], kf_f[:]))
        ser.dma(lambda e: e.dma_start(
            out=v_f[:].rearrange("p (kt e) -> p kt e", kt=16),
            in_=vt_in.rearrange("(kt p) e -> p kt e", p=128)))
        ser.op("vector", lambda e: e.tensor_copy(v_b[:], v_f[:]))
        ser.dma(lambda e: e.dma_start(out=pmjd[:], in_=pmjd_in))
        ser.op("vector", lambda e: e.memset(ones1[:], 1.0))
        ser.op("vector", lambda e: e.memset(onesr[:], 1.0))
        ser.op("vector", lambda e: e.memset(cm16[:], -16.0))

        marks = {}
        ser.dma(lambda e: e.dma_start(out=q_f[:], in_=qb_in[0:128, :]))
        marks[0] = ser.mark()
        for h in range(NH0):
            sl = float(slopes[h])
            mk = marks[h]
            ser.op("vector", lambda e: e.tensor_copy(q_b[:], q_f[:]), dwait=mk)
            if h + 1 < NH0:
                ser.dma(lambda e, h=h: e.dma_start(out=q_f[:], in_=qb_in[(h + 1) * 128:(h + 2) * 128, :]))
                marks[h + 1] = ser.mark()
            kt_base = 0
            for s in range(NSLOT):
                nkt = KT_OF_SLOT[s]
                for kt in range(nkt):
                    ser.op("tensor", lambda e, kt=kt, s=s: e.matmul(
                        psc[:], kf_b[:, kt * 128:(kt + 1) * 128], q_b[:, s * 256:(s + 1) * 256],
                        start=True, stop=True, skip_group_check=True), dwait=mk)
                    col = (kt_base + kt) * 256
                    ser.op("vector", lambda e, col=col, sl=sl: e.scalar_tensor_tensor(
                        earg[:], psc[:], QSCALE / sl, pmjd[:, col:col + 256], OP.mult, OP.add), dwait=mk)
                    ser.op("scalar", lambda e, kt=kt, sl=sl: e.activation(
                        expT[:, kt * 256:(kt + 1) * 256], earg[:], AF.Exp, bias=cm16[:, 0:1], scale=sl), dwait=mk)
                for kt in range(nkt):
                    ser.op("tensor", lambda e, kt=kt, nkt=nkt: e.matmul(
                        pav[:], v_b[:, kt * 128:(kt + 1) * 128], expT[:, kt * 256:(kt + 1) * 256],
                        start=(kt == 0), stop=(kt == nkt - 1), skip_group_check=True), dwait=mk)
                for kt in range(nkt):
                    ser.op("tensor", lambda e, kt=kt, nkt=nkt: e.matmul(
                        psum1[:], ones1[:], expT[:, kt * 256:(kt + 1) * 256],
                        start=(kt == 0), stop=(kt == nkt - 1), skip_group_check=True), dwait=mk)
                ser.op("vector", lambda e: e.reciprocal(recip[:], psum1[:]), dwait=mk)
                ser.op("tensor", lambda e: e.matmul(
                    prb[:], onesr[:], recip[:], start=True, stop=True, skip_group_check=True), dwait=mk)
                ser.op("vector", lambda e: e.tensor_copy(recb[:], prb[:]), dwait=mk)
                ser.op("vector", lambda e: e.tensor_mul(aout[:], pav[:], recb[:]))
                ser.dma(lambda e, h=h, s=s: e.dma_start(
                    out=ao_out[h * 128:(h + 1) * 128, s * 256:(s + 1) * 256], in_=aout[:]))
                kt_base += nkt
        ser.finish(block, chain, dchain)
    return nc


def make_pmjd(core):
    out = np.zeros((128, NKT * 256), dtype=np.float32)
    p = np.arange(128)[:, None]
    j = np.arange(256)[None, :]
    base = 0
    for s in range(NSLOT):
        i_c = 2 * s + (core % 2)
        for kt in range(KT_OF_SLOT[s]):
            v = (p - j + 128 * (kt - 2 * i_c)).astype(np.float32)
            v[v > 0] = -1e33
            out[:, (base + kt) * 256:(base + kt + 1) * 256] = v
        base += KT_OF_SLOT[s]
    return out


def run_p2(qfm, kv):
    nc = build_p2()
    in_maps = []
    for c in range(NCORES):
        b = c // 2
        blocks = [2 * s + (c % 2) for s in range(NSLOT)]
        qcols = np.concatenate(
            [qfm[:, b * S + i * 256: b * S + (i + 1) * 256] for i in blocks], axis=1)
        kvb = kv[b * S:(b + 1) * S]
        in_maps.append({
            "qb": np.ascontiguousarray(qcols),
            "kf": np.ascontiguousarray(kvb[:, :HD0].T),
            "vt": np.ascontiguousarray(kvb[:, HD0:]),
            "pmjd": make_pmjd(c),
        })
    res = run_bass_kernel_spmd(nc, in_maps, core_ids=list(range(NCORES)))
    ao = np.zeros((D, T), dtype=np.float32)
    for c in range(NCORES):
        b = c // 2
        r = res.results[c]["ao"]
        for s in range(NSLOT):
            i = 2 * s + (c % 2)
            ao[:, b * S + i * 256: b * S + (i + 1) * 256] = r[:, s * 256:(s + 1) * 256]
    return ao


def build_p3():
    nc = bass.Bass("TRN2", target_bir_lowering=False)
    ao_in = nc.dram_tensor("ao", [D, TC], F32, kind="ExternalInput").ap()
    x_in = nc.dram_tensor("x", [TC, D], F32, kind="ExternalInput").ap()
    wo_in = nc.dram_tensor("wo", [16, 128, 32, 256], F32, kind="ExternalInput").ap()
    rwt_in = nc.dram_tensor("rwt", [D, 4], F32, kind="ExternalInput").ap()
    at_in = nc.dram_tensor("at", [4, D, 16], F32, kind="ExternalInput").ap()
    bt_in = nc.dram_tensor("bt", [4, 16, D], F32, kind="ExternalInput").ap()
    idf_in = nc.dram_tensor("idf", [128, 128], F32, kind="ExternalInput").ap()
    x2_out = nc.dram_tensor("x2", [TC, D], F32, kind="ExternalOutput").ap()
    xl2_out = nc.dram_tensor("xl2", [D, TC], F32, kind="ExternalOutput").ap()
    x1_scr = nc.dram_tensor("x1_scr", [TC, D], F32, kind="Internal").ap()

    with ExitStack() as st:
        sb = lambda name, shp, dt: st.enter_context(nc.sbuf_tensor("t_" + name, shp, dt))
        ps = lambda name, shp, dt: st.enter_context(nc.psum_tensor("p_" + name, shp, dt))
        aob = sb("aob", [128, 32 * TC], BF16)
        ldf = sb("ldf", [128, 16 * 256], F32)
        wtb = sb("wtb", [128, 32 * 256], BF16)
        sx1 = sb("sx1", [128, D], F32)
        sx2 = sb("sx2", [128, D], F32)
        sq = sb("sq", [128, D], F32)
        cols = sb("cols", [128, 16], F32)
        ceps = sb("ceps", [128, 1], F32)
        xn = sb("xn", [128, D], F32)
        xnT = sb("xnT", [128, 32 * 128], BF16)
        idf = sb("idf", [128, 128], F32)
        rwtb = sb("rwtb", [128, 32 * 4], BF16)
        atb = [sb(f"atb{i}", [128, 32 * 16], BF16) for i in range(4)]
        btb = [sb(f"btb{i}", [16, D], BF16) for i in range(4)]
        lg = sb("lg", [128, 4], F32)
        gsc = sb("gsc", [128, 16], F32)
        gates = sb("gates", [128, 4], F32)
        gbc = sb("gbc", [16, 128], F32)
        wmid4 = sb("wmid4", [16, 4 * 128], BF16)
        ones16 = sb("ones16", [1, 16], F32)
        pcc = [ps(f"pcc{i}", [128, 256], F32) for i in range(8)]
        chain = st.enter_context(nc.semaphore("chain"))
        dchain = st.enter_context(nc.semaphore("dchain"))
        block = st.enter_context(nc.Block())
        ptrf = pcc[7][:, 0:128]
        plg = pcc[0][:, 0:4]
        pg1 = pcc[6][0:1, 0:128]
        pgb = pcc[5][0:16, 0:128]
        pmid = pcc[4][0:16, 0:128]
        pdl = pcc[1]

        gT = ldf[0:1, 3584:4096]
        ser = Ser()
        ser.dma(lambda e: e.dma_start(out=idf[:], in_=idf_in))
        ser.op("vector", lambda e: e.memset(ones16[:], 1.0))
        ser.op("vector", lambda e: e.memset(ceps[:], LN_EPS))
        for fi in range(32):
            ser.dma(lambda e, fi=fi: e.dma_start(out=ldf[:, 0:TC], in_=ao_in[fi * 128:(fi + 1) * 128, :]))
            ser.op("vector", lambda e, fi=fi: e.tensor_copy(aob[:, fi * TC:(fi + 1) * TC], ldf[:, 0:TC]))
        for fog in range(8):
            for half in range(2):
                cb = fog * 2 + half
                for hq in range(2):
                    ser.dma(lambda e, cb=cb, hq=hq: e.dma_start(
                        out=ldf[:].rearrange("p (f c) -> p f c", f=16),
                        in_=wo_in[cb][:, hq * 16:(hq + 1) * 16, :]))
                    ser.op("vector", lambda e, hq=hq: e.tensor_copy(
                        wtb[:, hq * 4096:(hq + 1) * 4096], ldf[:]))
                for fi in range(32):
                    for tt in range(NT):
                        pv = pcc[tt][:]
                        ser.op("tensor", lambda e, fi=fi, tt=tt, pv=pv: e.matmul(
                            pv, aob[:, fi * TC + tt * 128: fi * TC + (tt + 1) * 128],
                            wtb[:, fi * 256:(fi + 1) * 256],
                            start=(fi == 0), stop=(fi == 31), skip_group_check=True))
                xr_v = ldf[:, 3072:3328]
                ev_v = ldf[:, 3328:3584]
                for tt in range(NT):
                    co = fog * 512 + half * 256
                    pv = pcc[tt][:]
                    ser.dma(lambda e, tt=tt, co=co, xr_v=xr_v: e.dma_start(
                        out=xr_v, in_=x_in[tt * 128:(tt + 1) * 128, co:co + 256]))
                    ser.op("vector", lambda e, pv=pv, xr_v=xr_v, ev_v=ev_v: e.tensor_add(ev_v, pv, xr_v))
                    ser.dma(lambda e, tt=tt, co=co, ev_v=ev_v: e.dma_start(
                        out=x1_scr[tt * 128:(tt + 1) * 128, co:co + 256], in_=ev_v))
        for fi in range(32):
            ser.dma(lambda e, fi=fi: e.dma_start(out=ldf[:, 0:4], in_=rwt_in[fi * 128:(fi + 1) * 128, :]))
            ser.op("vector", lambda e, fi=fi: e.tensor_copy(rwtb[:, fi * 4:(fi + 1) * 4], ldf[:, 0:4]))
        for ei in range(4):
            ser.dma(lambda e, ei=ei: e.dma_start(
                out=sq[:, 0:512].rearrange("p (f r) -> p f r", f=32),
                in_=at_in[ei].rearrange("(f p) r -> p f r", p=128)))
            ser.op("vector", lambda e, ei=ei: e.tensor_copy(atb[ei][:], sq[:, 0:512]))
            ser.dma(lambda e, ei=ei: e.dma_start(out=ldf[0:16, 0:D], in_=bt_in[ei]))
            ser.op("vector", lambda e, ei=ei: e.tensor_copy(btb[ei][:], ldf[0:16, 0:D]))
        for tt in range(NT):
            ser.dma(lambda e, tt=tt: e.dma_start(out=sx1[:], in_=x1_scr[tt * 128:(tt + 1) * 128, :]))
            _ln_ops(ser, sx1[:], xn[:], sq[:], cols, D, ceps[:, 0:1])
            for fi in range(32):
                ser.op("tensor", lambda e, fi=fi: e.transpose(ptrf, xn[:, fi * 128:(fi + 1) * 128], idf[:]))
                ser.op("vector", lambda e, fi=fi: e.tensor_copy(xnT[:, fi * 128:(fi + 1) * 128], ptrf))
            for fi in range(32):
                ser.op("tensor", lambda e, fi=fi: e.matmul(
                    plg, xnT[:, fi * 128:(fi + 1) * 128], rwtb[:, fi * 4:(fi + 1) * 4],
                    start=(fi == 0), stop=(fi == 31), skip_group_check=True))
            ser.op("vector", lambda e: e.tensor_copy(lg[:], plg))
            c0, c1, c2, c3, c4 = (cols[:, 8 + i:9 + i] for i in range(5))
            ser.op("vector", lambda e: e.reduce_max(c0, lg[:], AX))
            ser.op("vector", lambda e: e.tensor_scalar(gsc[:, 0:4], lg[:], c0, None, OP.is_equal))
            ser.op("vector", lambda e: e.scalar_tensor_tensor(gsc[:, 4:8], gsc[:, 0:4], -1e30, lg[:], OP.mult, OP.add))
            ser.op("vector", lambda e: e.reduce_max(c1, gsc[:, 4:8], AX))
            ser.op("vector", lambda e: e.tensor_scalar(gsc[:, 8:12], lg[:], c1, None, OP.is_ge))
            ser.op("vector", lambda e: e.tensor_scalar(c2, c0, -1.0, None, OP.mult))
            ser.op("scalar", lambda e: e.activation(gsc[:, 12:16], lg[:], AF.Exp, bias=c2))
            ser.op("vector", lambda e: e.tensor_mul(gsc[:, 12:16], gsc[:, 12:16], gsc[:, 8:12]))
            ser.op("vector", lambda e: e.reduce_sum(c3, gsc[:, 12:16], AX))
            ser.op("vector", lambda e: e.reciprocal(c4, c3))
            ser.op("vector", lambda e: e.tensor_scalar(gates[:], gsc[:, 12:16], c4, None, OP.mult))
            for ei in range(4):
                ser.op("tensor", lambda e, ei=ei: e.transpose(pg1, gates[:, ei:ei + 1], idf[:]))
                ser.op("vector", lambda e, ei=ei: e.tensor_copy(gT[0:1, ei * 128:(ei + 1) * 128], pg1))
            for ei in range(4):
                for fi in range(32):
                    ser.op("tensor", lambda e, ei=ei, fi=fi: e.matmul(
                        pmid, atb[ei][:, fi * 16:(fi + 1) * 16], xnT[:, fi * 128:(fi + 1) * 128],
                        start=(fi == 0), stop=(fi == 31), skip_group_check=True))
                ser.op("tensor", lambda e, ei=ei: e.matmul(
                    pgb, ones16[:], gT[0:1, ei * 128:(ei + 1) * 128],
                    start=True, stop=True, skip_group_check=True))
                ser.op("vector", lambda e: e.tensor_copy(gbc[:], pgb))
                ser.op("vector", lambda e, ei=ei: e.tensor_mul(wmid4[:, ei * 128:(ei + 1) * 128], pmid, gbc[:]))
            for og in range(16):
                for ei in range(4):
                    ser.op("tensor", lambda e, og=og, ei=ei: e.matmul(
                        pdl[:], wmid4[:, ei * 128:(ei + 1) * 128], btb[ei][:, og * 256:(og + 1) * 256],
                        start=(ei == 0), stop=(ei == 3), skip_group_check=True))
                ser.op("vector", lambda e, og=og: e.scalar_tensor_tensor(
                    sx2[:, og * 256:(og + 1) * 256], pdl[:], 4.0, sx1[:, og * 256:(og + 1) * 256],
                    OP.mult, OP.add))
                ser.dma(lambda e, tt=tt, og=og: e.dma_start(
                    out=x2_out[tt * 128:(tt + 1) * 128, og * 256:(og + 1) * 256],
                    in_=sx2[:, og * 256:(og + 1) * 256]))
            _ln_ops(ser, sx2[:], xn[:], sq[:], cols, D, ceps[:, 0:1])
            for fi in range(32):
                ser.op("tensor", lambda e, fi=fi: e.transpose(ptrf, xn[:, fi * 128:(fi + 1) * 128], idf[:]))
                ser.op("vector", lambda e: e.tensor_copy(ldf[:, 3072:3200], ptrf))
                ser.dma(lambda e, tt=tt, fi=fi: e.dma_start(
                    out=xl2_out[fi * 128:(fi + 1) * 128, tt * 128:(tt + 1) * 128], in_=ldf[:, 3072:3200]))
        ser.finish(block, chain, dchain)
    return nc


def run_p3(ao, x_full, wo, router_w, A, Bm):
    nc = build_p3()
    idf = np.eye(128, dtype=np.float32)
    wo_p = np.ascontiguousarray(wo.reshape(32, 128, 16, 256).transpose(2, 1, 0, 3))
    rwt = np.ascontiguousarray(router_w.T)
    at = np.ascontiguousarray(A.transpose(0, 2, 1))
    bt = np.ascontiguousarray(Bm.transpose(0, 2, 1))
    in_maps = [{"ao": np.ascontiguousarray(ao[:, c * TC:(c + 1) * TC]),
                "x": np.ascontiguousarray(x_full[c * TC:(c + 1) * TC]),
                "wo": wo_p, "rwt": rwt, "at": at, "bt": bt, "idf": idf}
               for c in range(NCORES)]
    res = run_bass_kernel_spmd(nc, in_maps, core_ids=list(range(NCORES)))
    x2 = np.concatenate([res.results[c]["x2"] for c in range(NCORES)], axis=0)
    xl2 = np.concatenate([res.results[c]["xl2"] for c in range(NCORES)], axis=1)
    return x2, xl2


def build_p4():
    nc = bass.Bass("TRN2", target_bir_lowering=False)
    xl_in = nc.dram_tensor("xl", [D, TC], F32, kind="ExternalInput").ap()
    halo_in = nc.dram_tensor("halo", [D, 1], F32, kind="ExternalInput").ap()
    w_ins = [nc.dram_tensor(f"w{i}", [32, 128, 32, 128], F32, kind="ExternalInput").ap() for i in range(4)]
    tm_in = nc.dram_tensor("tm", [D, 4], F32, kind="ExternalInput").ap()
    outs = [nc.dram_tensor(nm, [D, TC], BF16, kind="ExternalOutput").ap()
            for nm in ("kfm", "vfm", "rfm", "gfm")]

    with ExitStack() as st:
        sb = lambda name, shp, dt: st.enter_context(nc.sbuf_tensor("t_" + name, shp, dt))
        ps = lambda name, shp, dt: st.enter_context(nc.psum_tensor("p_" + name, shp, dt))
        ext = sb("ext", [128, TC + 1], F32)
        mf32 = sb("mf32", [128, TC], F32)
        mixf = sb("mixf", [128, 32 * TC], BF16)
        tmv = sb("tmv", [128, 4 * 32], F32)
        tm1 = sb("tm1", [128, 4 * 32], F32)
        wtf = sb("wtf", [128, 32 * 128], F32)
        wtb = sb("wtb", [128, 32 * 128], BF16)
        evb = sb("evb", [128, 512], BF16)
        pf0 = ps("pf0", [128, 512], F32)
        pf1 = ps("pf1", [128, 512], F32)
        chain = st.enter_context(nc.semaphore("chain"))
        dchain = st.enter_context(nc.semaphore("dchain"))
        block = st.enter_context(nc.Block())

        ser = Ser()
        ser.dma(lambda e: e.dma_start(
            out=tmv[:].rearrange("p (f m) -> p f m", f=32),
            in_=tm_in.rearrange("(f p) m -> p f m", p=128)))
        ser.op("vector", lambda e: e.tensor_scalar(tm1[:], tmv[:], -1.0, 1.0, OP.mult, OP.add))
        for pj in range(4):
            for fi in range(32):
                ser.dma(lambda e, fi=fi: e.dma_start(
                    out=ext[:, 1:TC + 1], in_=xl_in[fi * 128:(fi + 1) * 128, :]))
                ser.dma(lambda e, fi=fi: e.dma_start(
                    out=ext[:, 0:1], in_=halo_in[fi * 128:(fi + 1) * 128, :]))
                mcol = tmv[:, fi * 4 + pj: fi * 4 + pj + 1]
                m1col = tm1[:, fi * 4 + pj: fi * 4 + pj + 1]
                ser.op("vector", lambda e, mcol=mcol: e.tensor_scalar(
                    mf32[:], ext[:, 1:TC + 1], mcol, None, OP.mult))
                ser.op("vector", lambda e, fi=fi, m1col=m1col: e.scalar_tensor_tensor(
                    mixf[:, fi * TC:(fi + 1) * TC], ext[:, 0:TC], m1col, mf32[:], OP.mult, OP.add))
            marks = {}
            ser.dma(lambda e, pj=pj: e.dma_start(
                out=wtf[:].rearrange("p (f c) -> p f c", f=32), in_=w_ins[pj][0]))
            marks[0] = ser.mark()
            for fo in range(32):
                mk = marks[fo]
                ser.op("vector", lambda e: e.tensor_copy(wtb[:], wtf[:]), dwait=mk)
                if fo + 1 < 32:
                    ser.dma(lambda e, pj=pj, fo=fo: e.dma_start(
                        out=wtf[:].rearrange("p (f c) -> p f c", f=32), in_=w_ins[pj][fo + 1]))
                    marks[fo + 1] = ser.mark()
                for fi in range(32):
                    for nh in range(2):
                        pq = pf0 if nh == 0 else pf1
                        ser.op("tensor", lambda e, fi=fi, nh=nh, pq=pq: e.matmul(
                            pq[:], wtb[:, fi * 128:(fi + 1) * 128],
                            mixf[:, fi * TC + nh * 512: fi * TC + (nh + 1) * 512],
                            start=(fi == 0), stop=(fi == 31), skip_group_check=True), dwait=mk)
                for nh in range(2):
                    pq = pf0 if nh == 0 else pf1
                    if pj == 3:
                        ser.op("scalar", lambda e, pq=pq: e.activation(evb[:], pq[:], AF.Silu))
                    else:
                        ser.op("vector", lambda e, pq=pq: e.tensor_copy(evb[:], pq[:]))
                    ser.dma(lambda e, pj=pj, fo=fo, nh=nh: e.dma_start(
                        out=outs[pj][fo * 128:(fo + 1) * 128, nh * 512:(nh + 1) * 512], in_=evb[:]))
        ser.finish(block, chain, dchain)
    return nc


def run_p4(xl2, w_key, w_value, w_receptance, w_gate, tmk, tmv_, tmr, tmg):
    nc = build_p4()
    tm = np.ascontiguousarray(np.stack([tmk, tmv_, tmr, tmg], axis=1))
    ws = [np.ascontiguousarray(w.T.reshape(32, 128, 32, 128).transpose(2, 1, 0, 3))
          for w in (w_key, w_value, w_receptance, w_gate)]
    in_maps = []
    for c in range(NCORES):
        t0 = c * TC
        halo = np.zeros((D, 1), dtype=np.float32) if t0 % S == 0 else \
            np.ascontiguousarray(xl2[:, t0 - 1: t0])
        in_maps.append({"xl": np.ascontiguousarray(xl2[:, t0:t0 + TC]), "halo": halo,
                        "w0": ws[0], "w1": ws[1], "w2": ws[2], "w3": ws[3], "tm": tm})
    res = run_bass_kernel_spmd(nc, in_maps, core_ids=list(range(NCORES)))
    mats = []
    for nm in ("kfm", "vfm", "rfm", "gfm"):
        mats.append(np.concatenate([res.results[c][nm] for c in range(NCORES)], axis=1))
    return mats


def build_p5():
    nc = bass.Bass("TRN2", target_bir_lowering=False)
    kf_in = nc.dram_tensor("kf", [FPC, T], BF16, kind="ExternalInput").ap()
    rf_in = nc.dram_tensor("rf", [FPC, T], BF16, kind="ExternalInput").ap()
    kt_in = nc.dram_tensor("kt", [T, FPC], BF16, kind="ExternalInput").ap()
    vt_in = nc.dram_tensor("vt", [T, FPC], BF16, kind="ExternalInput").ap()
    dft_in = nc.dram_tensor("dft", [4, 128], F32, kind="ExternalInput").ap()
    fcol_in = nc.dram_tensor("fcol", [128, 4], F32, kind="ExternalInput").ap()
    d8f_in = nc.dram_tensor("d8f", [HPC, 128], F32, kind="ExternalInput").ap()
    mkf_in = nc.dram_tensor("mkf", [HPC, 128], F32, kind="ExternalInput").ap()
    gnw_in = nc.dram_tensor("gnw", [HPC, HD1], F32, kind="ExternalInput").ap()
    gnb_in = nc.dram_tensor("gnb", [HPC, HD1], F32, kind="ExternalInput").ap()
    ind8p_in = nc.dram_tensor("ind8p", [HPC, 16 * HD1], F32, kind="ExternalInput").ap()
    ind8b_in = nc.dram_tensor("ind8b", [HPC, NU * CH], F32, kind="ExternalInput").ap()
    taucmp_in = nc.dram_tensor("taucmp", [1, 128], F32, kind="ExternalInput").ap()
    taud_in = nc.dram_tensor("taud", [1, 128], F32, kind="ExternalInput").ap()
    taukd_in = nc.dram_tensor("taukd", [1, 128], F32, kind="ExternalInput").ap()
    taux_in = nc.dram_tensor("taux", [1, 64], F32, kind="ExternalInput").ap()
    taukx_in = nc.dram_tensor("taukx", [1, 64], F32, kind="ExternalInput").ap()
    tauir_in = nc.dram_tensor("tauir", [1, 128], F32, kind="ExternalInput").ap()
    tau31_in = nc.dram_tensor("tau31", [1, 32], F32, kind="ExternalInput").ap()
    msu_in = nc.dram_tensor("msu", [CH, NU * CH], F32, kind="ExternalInput").ap()
    mid_in = nc.dram_tensor("mid", [CH, NU * CH], F32, kind="ExternalInput").ap()
    xn_out = nc.dram_tensor("xno", [FPC, T], BF16, kind="ExternalOutput").ap()

    kf_r = kf_in.rearrange("(ft p) (b cc tau) -> ft cc p b tau", ft=4, b=4, tau=CH)
    rf_r = rf_in.rearrange("(ft p) (b cc tau) -> ft cc p b tau", ft=4, b=4, tau=CH)
    kt_r = kt_in.rearrange("(b cc tau) f -> cc tau b f", b=4, tau=CH)
    vt_r = vt_in.rearrange("(b cc tau) f -> cc tau b f", b=4, tau=CH)
    xo_r = xn_out.rearrange("(h e) (b cc tau) -> h cc e b tau", h=HPC, b=4, tau=CH)

    with ExitStack() as st:
        sb = lambda name, shp, dt: st.enter_context(nc.sbuf_tensor("t_" + name, shp, dt))
        ps = lambda name, shp, dt: st.enter_context(nc.psum_tensor("p_" + name, shp, dt))
        s_ft = [sb(f"sft{i}", [1, 128], F32) for i in range(4)]
        fcol = sb("fcol", [128, 4], F32)
        s8f = sb("s8f", [HPC, 128], F32)
        mskf = sb("mskf", [HPC, 128], F32)
        srep = sb("srep", [1, 4 * FPC], F32)
        tard = sb("tard", [128, 512], BF16)
        takd = sb("takd", [128, 512], BF16)
        tarx = sb("tarx", [128, 256], BF16)
        takx = sb("takx", [128, 256], BF16)
        tir = sb("tir", [128, 512], BF16)
        tfb = sb("tfb", [128, 512], BF16)
        tkh = sb("tkh", [CH, 4 * FPC], BF16)
        tdec = sb("tdec", [128, 16 * HD1], F32)
        gnwb = sb("gnwb", [64, NU * CH], F32)
        gnbb = sb("gnbb", [64, NU * CH], F32)
        msu = sb("msu", [CH, NU * CH], F32)
        midm = sb("midm", [CH, NU * CH], F32)
        taus = {}
        for nm, w in (("taud", 128), ("taukd", 128), ("taux", 64), ("taukx", 64),
                      ("tauir", 128), ("tau31", 32), ("taucmp", 128)):
            taus[nm] = sb("s_" + nm, [1, w], F32)
        clampf = sb("clampf", [128, 128], F32)
        ind8p = sb("ind8p", [HPC, 16 * HD1], F32)
        ind8b = sb("ind8b", [HPC, NU * CH], F32)
        gnw8 = sb("gnw8", [HPC, HD1], F32)
        gnb8 = sb("gnb8", [HPC, HD1], F32)
        ones64 = sb("ones64", [64, 1], F32)
        ones1r = sb("ones1r", [1, 64], F32)
        kc = sb("kc", [128, 512], BF16)
        rc = sb("rc", [128, 512], BF16)
        ktc = sb("ktc", [CH, 4 * FPC], BF16)
        vtc = sb("vtc", [CH, 4 * FPC], BF16)
        kdtf = sb("kdtf", [128, 1024], BF16)
        rtd = sb("rtd", [128, 512], BF16)
        rx = sb("rx", [128, 256], BF16)
        kx = sb("kx", [128, 256], BF16)
        ri = sb("ri", [128, 512], BF16)
        kh = sb("kh", [CH, 4 * FPC], BF16)
        msb = sb("msb", [CH, NU * CH], BF16)
        clA = sb("clA", [CH, NU * CH], F32)
        msc = sb("msc", [CH, NU * CH], BF16)
        sf32 = sb("sf32", [128, 16 * HD1], F32)
        sbf = sb("sbf", [128, 16 * HD1], BF16)
        osb = sb("osb", [64, NU * CH], F32)
        osq = sb("osq", [64, NU * CH], F32)
        t1 = sb("t1", [64, NU * CH], F32)
        xnb = sb("xnb", [64, NU * CH], BF16)
        mrow = sb("mrow", [1, NU * CH], F32)
        mrow2 = sb("mrow2", [1, NU * CH], F32)
        mrow3 = sb("mrow3", [1, NU * CH], F32)
        cgn = sb("cgn", [128, 1], F32)
        PA = ps("PA", [64, 1024], F32)
        PX = ps("PX", [16, 512], F32)
        PO = ps("PO", [64, 1024], F32)
        PS = ps("PS", [128, 512], F32)
        PT = ps("PT", [128, 512], F32)
        chain = st.enter_context(nc.semaphore("chain"))
        dchain = st.enter_context(nc.semaphore("dchain"))
        block = st.enter_context(nc.Block())

        ser = Ser()
        for nm, inp in (("taud", taud_in), ("taukd", taukd_in), ("taux", taux_in),
                        ("taukx", taukx_in), ("tauir", tauir_in), ("tau31", tau31_in),
                        ("taucmp", taucmp_in)):
            ser.dma(lambda e, nm=nm, inp=inp: e.dma_start(out=taus[nm][:], in_=inp))
        ser.dma(lambda e: e.dma_start(out=msu[:], in_=msu_in))
        ser.dma(lambda e: e.dma_start(out=midm[:], in_=mid_in))
        ser.dma(lambda e: e.dma_start(out=ind8p[:], in_=ind8p_in))
        ser.dma(lambda e: e.dma_start(out=ind8b[:], in_=ind8b_in))
        ser.dma(lambda e: e.dma_start(out=gnw8[:], in_=gnw_in))
        ser.dma(lambda e: e.dma_start(out=gnb8[:], in_=gnb_in))
        ser.dma(lambda e: e.dma_start(out=s8f[:], in_=d8f_in))
        ser.dma(lambda e: e.dma_start(out=mskf[:], in_=mkf_in))
        ser.dma(lambda e: e.dma_start(out=fcol[:], in_=fcol_in))
        ser.op("vector", lambda e: e.memset(ones64[:], 1.0))
        ser.op("vector", lambda e: e.memset(ones1r[:], 1.0))
        ser.op("vector", lambda e: e.memset(sf32[:], 0.0))
        ser.op("vector", lambda e: e.memset(sbf[:], 0.0))
        ser.op("vector", lambda e: e.memset(cgn[:], 1e-5))
        for ft in range(4):
            ser.dma(lambda e, ft=ft: e.dma_start(out=s_ft[ft][:], in_=dft_in[ft:ft + 1, :]))
            ser.op("scalar", lambda e, ft=ft: e.activation(s_ft[ft][:], s_ft[ft][:], AF.Exp))
            ser.op("vector", lambda e, ft=ft: e.tensor_scalar(s_ft[ft][:], s_ft[ft][:], -1.0, None, OP.mult))
            ser.op("vector", lambda e, ft=ft: e.tensor_copy(srep[0:1, ft * 128:(ft + 1) * 128], s_ft[ft][:]))
        ser.op("scalar", lambda e: e.activation(s8f[:], s8f[:], AF.Exp))
        ser.op("vector", lambda e: e.tensor_scalar(s8f[:], s8f[:], -1.0, None, OP.mult))
        ser.op("vector", lambda e: e.tensor_mul(s8f[:], s8f[:], mskf[:]))
        for b_ in range(1, 4):
            ser.op("vector", lambda e, b_=b_: e.tensor_copy(
                srep[0:1, b_ * FPC:(b_ + 1) * FPC], srep[0:1, 0:FPC]))
        def mk_tab(dst, tau_nm, width, clamp, post_tf=False):
            for ft in range(4):
                ser.op("tensor", lambda e, ft=ft, tau_nm=tau_nm, width=width: e.matmul(
                    PT[:, 0:width], s_ft[ft][:], taus[tau_nm][:],
                    start=True, stop=True, skip_group_check=True))
                if clamp:
                    ser.op("vector", lambda e, width=width: e.tensor_scalar(
                        clampf[:, 0:width], PT[:, 0:width], 75.0, -75.0, OP.min, OP.max))
                else:
                    ser.op("vector", lambda e, width=width: e.tensor_copy(
                        clampf[:, 0:width], PT[:, 0:width]))
                if post_tf:
                    ser.op("scalar", lambda e, width=width: e.activation(
                        clampf[:, 0:width], clampf[:, 0:width], AF.Exp))
                    ser.op("vector", lambda e, ft=ft, width=width: e.tensor_scalar(
                        dst[:, ft * width:(ft + 1) * width], clampf[:, 0:width],
                        fcol[:, ft:ft + 1], None, OP.mult))
                else:
                    ser.op("scalar", lambda e, ft=ft, width=width: e.activation(
                        dst[:, ft * width:(ft + 1) * width], clampf[:, 0:width], AF.Exp))
        mk_tab(tard, "taud", 128, True)
        mk_tab(takd, "taukd", 128, True)
        mk_tab(tarx, "taux", 64, False)
        mk_tab(takx, "taukx", 64, False)
        mk_tab(tir, "tauir", 128, False)
        mk_tab(tfb, "taucmp", 128, True, post_tf=True)
        for q in range(4):
            ser.op("tensor", lambda e, q=q: e.matmul(
                PT[0:CH, 0:512], taus["tau31"][:], srep[0:1, q * 512:(q + 1) * 512],
                start=True, stop=True, skip_group_check=True))
            ser.op("scalar", lambda e, q=q: e.activation(
                tkh[:, q * 512:(q + 1) * 512], PT[0:CH, 0:512], AF.Exp))
        for q in range(2):
            ser.op("tensor", lambda e, q=q: e.matmul(
                PT[:, 0:512], s8f[:], ind8p[:, q * 512:(q + 1) * 512],
                start=True, stop=True, skip_group_check=True))
            ser.op("scalar", lambda e, q=q: e.activation(
                tdec[:, q * 512:(q + 1) * 512], PT[:, 0:512], AF.Exp, scale=float(CH)))
        for q in range(2):
            ser.op("tensor", lambda e, q=q: e.matmul(
                PT[0:64, 0:512], gnw8[:], ind8b[:, q * 512:(q + 1) * 512],
                start=True, stop=True, skip_group_check=True))
            ser.op("vector", lambda e, q=q: e.tensor_copy(gnwb[:, q * 512:(q + 1) * 512], PT[0:64, 0:512]))
            ser.op("tensor", lambda e, q=q: e.matmul(
                PT[0:64, 0:512], gnb8[:], ind8b[:, q * 512:(q + 1) * 512],
                start=True, stop=True, skip_group_check=True))
            ser.op("vector", lambda e, q=q: e.tensor_copy(gnbb[:, q * 512:(q + 1) * 512], PT[0:64, 0:512]))

        kc_v = kc[:].rearrange("p (g t) -> p g t", t=CH)
        rc_v = rc[:].rearrange("p (g t) -> p g t", t=CH)
        kdtf_v = kdtf[:].rearrange("p (g x) -> p g x", x=64)
        tard_v = tard[:].rearrange("p (g t) -> p g t", t=CH)
        takd_v = takd[:].rearrange("p (g t) -> p g t", t=CH)
        tfb_v = tfb[:].rearrange("p (g t) -> p g t", t=CH)
        tarx_v = tarx[:].rearrange("p (g t) -> p g t", t=16)
        takx_v = takx[:].rearrange("p (g t) -> p g t", t=16)
        rx_v = rx[:].rearrange("p (g t) -> p g t", t=16)
        kx_v = kx[:].rearrange("p (g t) -> p g t", t=16)

        def emit_chunk_loads(c):
            for ft_ in range(4):
                ser.dma(lambda e, c=c, ft_=ft_: e.dma_start(
                    out=kc[:, ft_ * 128:(ft_ + 1) * 128].rearrange("p (b tau) -> p b tau", b=4),
                    in_=kf_r[ft_, c]))
                ser.dma(lambda e, c=c, ft_=ft_: e.dma_start(
                    out=rc[:, ft_ * 128:(ft_ + 1) * 128].rearrange("p (b tau) -> p b tau", b=4),
                    in_=rf_r[ft_, c]))
            ser.dma(lambda e, c=c: e.dma_start(
                out=ktc[:].rearrange("t (b f) -> t b f", b=4), in_=kt_r[c]))
            ser.dma(lambda e, c=c: e.dma_start(
                out=vtc[:].rearrange("t (b f) -> t b f", b=4), in_=vt_r[c]))

        cmarks = {}
        emit_chunk_loads(0)
        cmarks[0] = ser.mark()
        for c in range(NCH):
            mk = cmarks[c]
            ser.op("vector", lambda e: e.tensor_mul(rtd[:], rc[:], tard[:]), dwait=mk)
            ser.op("vector", lambda e: e.tensor_tensor(kdtf_v[:, :, 0:CH], kc_v, takd_v, OP.mult), dwait=mk)
            ser.op("vector", lambda e: e.tensor_tensor(kdtf_v[:, :, CH:64], kc_v, tfb_v, OP.mult), dwait=mk)
            ser.op("vector", lambda e: e.tensor_tensor(rx_v, rc_v[:, :, 16:CH], tarx_v, OP.mult), dwait=mk)
            ser.op("vector", lambda e: e.tensor_tensor(kx_v, kc_v[:, :, 0:16], takx_v, OP.mult), dwait=mk)
            ser.op("vector", lambda e: e.tensor_mul(ri[:], rc[:], tir[:]), dwait=mk)
            ser.op("vector", lambda e: e.tensor_mul(kh[:], ktc[:], tkh[:]), dwait=mk)
            for u in range(NU):
                h, b_ = u // 4, u % 4
                ft, pr = h // 2, (h % 2) * 64
                g = ft * 4 + b_
                ser.op("tensor", lambda e, u=u, pr=pr, g=g: e.matmul(
                    PA[:, u * CH:(u + 1) * CH],
                    kdtf[pr:pr + 64, g * 64:(g + 1) * 64],
                    rtd[pr:pr + 64, g * CH:(g + 1) * CH],
                    start=True, stop=True, skip_group_check=True), dwait=mk)
                ser.op("tensor", lambda e, u=u, pr=pr, g=g: e.matmul(
                    PX[:, u * 16:(u + 1) * 16],
                    kx[pr:pr + 64, g * 16:(g + 1) * 16],
                    rx[pr:pr + 64, g * 16:(g + 1) * 16],
                    start=True, stop=True, skip_group_check=True), dwait=mk)
            ser.op("vector", lambda e: e.tensor_scalar(clA[:], PA[0:CH, :], 1e30, -1e30, OP.min, OP.max), dwait=mk)
            ser.op("vector", lambda e: e.tensor_mul(msb[:], clA[:], msu[:]), dwait=mk)
            ser.op("vector", lambda e: e.tensor_scalar(clA[:], PA[CH:64, :], 1e30, -1e30, OP.min, OP.max), dwait=mk)
            ser.op("vector", lambda e: e.tensor_mul(msc[:], clA[:], midm[:]), dwait=mk)
            ser.op("vector", lambda e: e.tensor_add(msb[:], msb[:], msc[:]), dwait=mk)
            msb_x = msb[0:16, :].rearrange("p (u t) -> p u t", t=CH)[:, :, 16:CH]
            px_v = PX[:].rearrange("p (u t) -> p u t", t=16)
            ser.op("vector", lambda e: e.tensor_add(msb_x, msb_x, px_v), dwait=mk)
            for u in range(NU):
                h, b_ = u // 4, u % 4
                ft, pr = h // 2, (h % 2) * 64
                g = ft * 4 + b_
                fo = b_ * FPC + h * 64
                ser.op("tensor", lambda e, u=u, fo=fo: e.matmul(
                    PO[:, u * CH:(u + 1) * CH],
                    vtc[:, fo:fo + 64], msb[:, u * CH:(u + 1) * CH],
                    start=True, stop=False, skip_group_check=True), dwait=mk)
                ser.op("tensor", lambda e, u=u, pr=pr, g=g: e.matmul(
                    PO[:, u * CH:(u + 1) * CH],
                    sbf[pr:pr + 64, g * 64:(g + 1) * 64], ri[pr:pr + 64, g * CH:(g + 1) * CH],
                    start=False, stop=True, skip_group_check=True), dwait=mk)
            for half in range(2):
                for ui in range(16):
                    u = half * 16 + ui
                    h, b_ = u // 4, u % 4
                    hh = h % 2
                    g = (h // 2) * 4 + b_
                    fo = b_ * FPC + h * 64
                    ser.op("tensor", lambda e, hh=hh, g=g, half=half, fo=fo: e.matmul(
                        PS[hh * 64:(hh + 1) * 64, (g - half * 8) * 64:(g - half * 8 + 1) * 64],
                        kh[:, fo:fo + 64], vtc[:, fo:fo + 64],
                        start=True, stop=True, skip_group_check=True), dwait=mk)
                sl = slice(half * 512, (half + 1) * 512)
                ser.op("vector", lambda e, sl=sl: e.tensor_mul(sf32[:, sl], sf32[:, sl], tdec[:, sl]), dwait=mk)
                ser.op("vector", lambda e, sl=sl: e.tensor_add(sf32[:, sl], sf32[:, sl], PS[:]), dwait=mk)
            if c + 1 < NCH:
                emit_chunk_loads(c + 1)
                cmarks[c + 1] = ser.mark()
            ser.op("vector", lambda e: e.tensor_copy(osb[:], PO[:]), dwait=mk)
            ser.op("scalar", lambda e: e.activation(osq[:], osb[:], AF.Square), dwait=mk)
            for q in range(2):
                ser.op("tensor", lambda e, q=q: e.matmul(
                    PA[0:1, q * 512:(q + 1) * 512], ones64[:], osb[:, q * 512:(q + 1) * 512],
                    start=True, stop=True, skip_group_check=True), dwait=mk)
                ser.op("tensor", lambda e, q=q: e.matmul(
                    PO[0:1, q * 512:(q + 1) * 512], ones64[:], osq[:, q * 512:(q + 1) * 512],
                    start=True, stop=True, skip_group_check=True), dwait=mk)
            ser.op("vector", lambda e: e.tensor_scalar(mrow[:], PA[0:1, :], 1.0 / 64, None, OP.mult), dwait=mk)
            ser.op("vector", lambda e: e.tensor_scalar(mrow2[:], PO[0:1, :], 1.0 / 64, None, OP.mult), dwait=mk)
            ser.op("vector", lambda e: e.tensor_mul(mrow3[:], mrow[:], mrow[:]), dwait=mk)
            ser.op("vector", lambda e: e.tensor_sub(mrow2[:], mrow2[:], mrow3[:]), dwait=mk)
            ser.op("scalar", lambda e: e.activation(mrow2[:], mrow2[:], AF.Ln, bias=cgn[0:1, 0:1]), dwait=mk)
            ser.op("scalar", lambda e: e.activation(mrow2[:], mrow2[:], AF.Exp, scale=-0.5), dwait=mk)
            for q in range(2):
                qs = slice(q * 512, (q + 1) * 512)
                ser.op("tensor", lambda e, q=q, qs=qs: e.matmul(
                    PA[0:64, qs], ones1r[:], mrow[0:1, qs],
                    start=True, stop=True, skip_group_check=True), dwait=mk)
                ser.op("tensor", lambda e, q=q, qs=qs: e.matmul(
                    PT[0:64, 0:512], ones1r[:], mrow2[0:1, qs],
                    start=True, stop=True, skip_group_check=True), dwait=mk)
                ser.op("vector", lambda e, qs=qs: e.tensor_sub(t1[:, qs], osb[:, qs], PA[0:64, qs]), dwait=mk)
                ser.op("vector", lambda e, qs=qs: e.tensor_mul(t1[:, qs], t1[:, qs], PT[0:64, 0:512]), dwait=mk)
            ser.op("vector", lambda e: e.tensor_mul(t1[:], t1[:], gnwb[:]), dwait=mk)
            ser.op("vector", lambda e: e.tensor_add(xnb[:], t1[:], gnbb[:]))
            for h_ in range(HPC):
                ser.dma(lambda e, c=c, h_=h_: e.dma_start(
                    out=xo_r[h_, c],
                    in_=xnb[:, h_ * 128:(h_ + 1) * 128].rearrange("e (b tau) -> e b tau", b=4)))
            ser.op("vector", lambda e: e.tensor_copy(sbf[:], sf32[:]), dwait=mk)
        ser.finish(block, chain, dchain)
    return nc


def run_p5(kfm, vfm, rfm, time_decay, time_faaaa, gn_weight, gn_bias):
    nc = build_p5()
    ktm = np.ascontiguousarray(kfm.T)
    vtm = np.ascontiguousarray(vfm.T)
    tau = np.arange(128) % CH
    tau16 = tau % 16
    taud = (tau16 - 8).astype(np.float32)[None, :]
    taukd = (7 - tau16).astype(np.float32)[None, :]
    taucmp = (8 - tau16).astype(np.float32)[None, :]
    tauir = tau.astype(np.float32)[None, :]
    tx = np.arange(64) % 16
    taux = tx.astype(np.float32)[None, :]
    taukx = (15 - tx).astype(np.float32)[None, :]
    tau31 = (31 - np.arange(CH)).astype(np.float32)[None, :]
    ind8p = np.zeros((HPC, 16 * HD1), dtype=np.float32)
    for h in range(HPC):
        for g in range(16):
            if g // 4 == h // 2:
                ind8p[h, g * HD1:(g + 1) * HD1] = 1.0
    ind8b = np.zeros((HPC, NU * CH), dtype=np.float32)
    for h in range(HPC):
        for u in range(NU):
            if u // 4 == h:
                ind8b[h, u * CH:(u + 1) * CH] = 1.0
    tp = np.arange(CH)[:, None]
    tq = np.arange(CH)[None, :]
    msu1 = ((tp < tq) & (tp // 16 == tq // 16)).astype(np.float32)
    mid1 = (tp == tq).astype(np.float32)
    msu = np.tile(msu1, (1, NU))
    mid = np.tile(mid1, (1, NU))
    in_maps = []
    for c in range(NCORES):
        fs = c * FPC
        hs = c * HPC
        dsl = time_decay[hs:hs + HPC]
        dft = dsl.reshape(4, 128)
        fsl = time_faaaa[hs:hs + HPC]
        fcol = np.ascontiguousarray(fsl.reshape(4, 128).T)
        d8f = np.zeros((HPC, 128), dtype=np.float32)
        mkf = np.zeros((HPC, 128), dtype=np.float32)
        for h in range(HPC):
            off = (h % 2) * 64
            d8f[h, off:off + 64] = dsl[h]
            mkf[h, off:off + 64] = 1.0
        in_maps.append({
            "kf": np.ascontiguousarray(kfm[fs:fs + FPC]),
            "rf": np.ascontiguousarray(rfm[fs:fs + FPC]),
            "kt": np.ascontiguousarray(ktm[:, fs:fs + FPC]),
            "vt": np.ascontiguousarray(vtm[:, fs:fs + FPC]),
            "dft": np.ascontiguousarray(dft.astype(np.float32)),
            "fcol": fcol.astype(np.float32),
            "d8f": d8f, "mkf": mkf,
            "gnw": np.ascontiguousarray(gn_weight[fs:fs + FPC].reshape(HPC, HD1)),
            "gnb": np.ascontiguousarray(gn_bias[fs:fs + FPC].reshape(HPC, HD1)),
            "ind8p": ind8p, "ind8b": ind8b,
            "taud": taud, "taukd": taukd, "taux": taux, "taukx": taukx,
            "tauir": tauir, "tau31": tau31, "taucmp": taucmp,
            "msu": msu, "mid": mid,
        })
    res = run_bass_kernel_spmd(nc, in_maps, core_ids=list(range(NCORES)))
    xn = np.concatenate([res.results[c]["xno"] for c in range(NCORES)], axis=0)
    return xn


NSK = 50


def build_p6():
    nc = bass.Bass("TRN2", target_bir_lowering=False)
    xnf_in = nc.dram_tensor("xnf", [D, TC], BF16, kind="ExternalInput").ap()
    gf_in = nc.dram_tensor("gf", [D, TC], BF16, kind="ExternalInput").ap()
    x2_in = nc.dram_tensor("x2", [TC, D], F32, kind="ExternalInput").ap()
    wot_in = nc.dram_tensor("wot", [8, 128, 32, 512], F32, kind="ExternalInput").ap()
    wspt_in = nc.dram_tensor("wspt", [D, 24], F32, kind="ExternalInput").ap()
    bvec_in = nc.dram_tensor("bvec", [1, 24], F32, kind="ExternalInput").ap()
    gate_in = nc.dram_tensor("gate", [1, 1], F32, kind="ExternalInput").ap()
    wb_in = nc.dram_tensor("wb", [1024, 1024], F32, kind="ExternalInput").ap()
    idf_in = nc.dram_tensor("idf", [128, 128], F32, kind="ExternalInput").ap()
    out_out = nc.dram_tensor("out", [TC, D], F32, kind="ExternalOutput").ap()
    x3_scr = nc.dram_tensor("x3_scr", [TC, D], F32, kind="Internal").ap()

    with ExitStack() as st:
        sb = lambda name, shp, dt: st.enter_context(nc.sbuf_tensor("t_" + name, shp, dt))
        ps = lambda name, shp, dt: st.enter_context(nc.psum_tensor("p_" + name, shp, dt))
        stga = sb("stga", [128, TC], BF16)
        stgb = sb("stgb", [128, TC], BF16)
        gxnb = sb("gxnb", [128, 32 * TC], BF16)
        ldf = sb("ldf", [128, 8 * 512], F32)
        wtb = sb("wtb", [128, 32 * 512], BF16)
        xr = sb("xr", [128, 512], F32)
        ev = sb("ev", [128, 512], F32)
        sx3 = sb("sx3", [128, D], F32)
        sxn = sb("sxn", [128, D], F32)
        sq = sb("sq", [128, D], F32)
        cols = sb("cols", [128, 16], F32)
        ceps = sb("ceps", [128, 1], F32)
        crms = sb("crms", [128, 1], F32)
        xnT = sb("xnT", [128, 32 * 128], BF16)
        idf = sb("idf", [128, 128], F32)
        wsptb = sb("wsptb", [128, 32 * 24], BF16)
        bvbc = sb("bvbc", [128, 24], F32)
        onesr = sb("onesr", [1, 128], F32)
        lg24 = sb("lg24", [128, 24], F32)
        hpp = sb("hpp", [128, 8 * 8], F32)
        z16all = sb("z16all", [128, 8 * 16], F32)
        t16 = sb("t16", [128, 16], F32)
        e16 = sb("e16", [128, 16], F32)
        mx4 = sb("mx4", [128, 4], F32)
        s4 = sb("s4", [128, 4], F32)
        ln4 = sb("ln4", [128, 4], F32)
        uneg = sb("uneg", [128, 4], F32)
        vneg = sb("vneg", [128, 4], F32)
        hrt = sb("hrt", [128, 8 * 16], F32)
        gg = sb("gg", [1, 1], F32)
        gcol = sb("gcol", [128, 1], F32)
        gm1 = sb("gm1", [128, 1], F32)
        wbb = sb("wbb", [128, 8 * 1024], BF16)
        to = sb("to", [128, 1024], F32)
        xp = ldf[:, 0:1024]
        y_sb = ldf[:, 1024:2048]
        pf = [ps(f"pf{i}", [128, 512], F32) for i in range(4)]
        chain = st.enter_context(nc.semaphore("chain"))
        dchain = st.enter_context(nc.semaphore("dchain"))
        block = st.enter_context(nc.Block())
        ptrf = pf[2][:, 0:128]
        plg = pf[1][:, 0:24]
        pbv = pf[1][:, 128:152]
        pg128 = pf[1][:, 256:257]

        ser = Ser()
        ser.dma(lambda e: e.dma_start(out=idf[:], in_=idf_in))
        ser.op("vector", lambda e: e.memset(onesr[:], 1.0))
        ser.op("vector", lambda e: e.memset(ceps[:], LN_EPS))
        ser.op("vector", lambda e: e.memset(crms[:], RMS_EPS))
        ser.dma(lambda e: e.dma_start(out=gg[:], in_=gate_in))
        for fi in range(32):
            ser.dma(lambda e, fi=fi: e.dma_start(out=stga[:], in_=xnf_in[fi * 128:(fi + 1) * 128, :]))
            ser.dma(lambda e, fi=fi: e.dma_start(out=stgb[:], in_=gf_in[fi * 128:(fi + 1) * 128, :]))
            ser.op("vector", lambda e, fi=fi: e.tensor_mul(gxnb[:, fi * TC:(fi + 1) * TC], stga[:], stgb[:]))
        wmarks = {}
        for fog in range(8):
            for fq in range(4):
                ser.dma(lambda e, fog=fog, fq=fq: e.dma_start(
                    out=ldf[:].rearrange("p (f c) -> p f c", f=8),
                    in_=wot_in[fog][:, fq * 8:(fq + 1) * 8, :]))
                ser.op("vector", lambda e, fq=fq: e.tensor_copy(
                    wtb[:, fq * 4096:(fq + 1) * 4096], ldf[:]))
            mkw = ser.mark()
            for rnd in range(2):
                for fi in range(32):
                    for ti in range(4):
                        tt = rnd * 4 + ti
                        pv = pf[ti][:]
                        ser.op("tensor", lambda e, fi=fi, tt=tt, pv=pv: e.matmul(
                            pv, gxnb[:, fi * TC + tt * 128: fi * TC + (tt + 1) * 128],
                            wtb[:, fi * 512:(fi + 1) * 512],
                            start=(fi == 0), stop=(fi == 31), skip_group_check=True), dwait=mkw)
                for ti in range(4):
                    tt = rnd * 4 + ti
                    co = fog * 512
                    pv = pf[ti][:]
                    ser.dma(lambda e, tt=tt, co=co: e.dma_start(
                        out=xr[:], in_=x2_in[tt * 128:(tt + 1) * 128, co:co + 512]))
                    ser.op("vector", lambda e, pv=pv: e.tensor_add(ev[:, 0:512], pv, xr[:]))
                    ser.dma(lambda e, tt=tt, co=co: e.dma_start(
                        out=x3_scr[tt * 128:(tt + 1) * 128, co:co + 512], in_=ev[:, 0:512]))

        ser.dma(lambda e: e.dma_start(
            out=ldf[:, 0:768].rearrange("p (f e) -> p f e", f=32),
            in_=wspt_in.rearrange("(f p) e -> p f e", p=128)))
        ser.op("vector", lambda e: e.tensor_copy(wsptb[:], ldf[:, 0:768]))
        ser.dma(lambda e: e.dma_start(out=ldf[0:1, 0:24], in_=bvec_in))
        ser.op("tensor", lambda e: e.matmul(pbv, onesr[:], ldf[0:1, 0:24],
                                            start=True, stop=True, skip_group_check=True))
        ser.op("vector", lambda e: e.tensor_copy(bvbc[:], pbv))
        for ci in range(8):
            ser.dma(lambda e, ci=ci: e.dma_start(out=ldf[:, 0:1024], in_=wb_in[ci * 128:(ci + 1) * 128, :]))
            ser.op("vector", lambda e, ci=ci: e.tensor_copy(wbb[:, ci * 1024:(ci + 1) * 1024], ldf[:, 0:1024]))
        c5, c6, c7, c8, c9, c10, c11 = (cols[:, 8 + i:9 + i] for i in range(7))
        for tt in range(NT):
            ser.dma(lambda e, tt=tt: e.dma_start(out=sx3[:], in_=x3_scr[tt * 128:(tt + 1) * 128, :]))
            _ln_ops(ser, sx3[:], sxn[:], sq[:], cols, D, ceps[:, 0:1])
            ser.op("scalar", lambda e: e.activation(sq[:], sxn[:], AF.Square))
            ser.op("vector", lambda e: e.reduce_sum(c5, sq[:], AX))
            ser.op("vector", lambda e: e.tensor_scalar(c5, c5, 1.0 / D, None, OP.mult))
            ser.op("scalar", lambda e: e.activation(c6, c5, AF.Ln, bias=crms[:, 0:1]))
            ser.op("scalar", lambda e: e.activation(c7, c6, AF.Exp, scale=-0.5))
            ser.op("vector", lambda e: e.tensor_scalar(sq[:], sxn[:], c7, None, OP.mult))
            for fi in range(32):
                ser.op("tensor", lambda e, fi=fi: e.transpose(ptrf, sq[:, fi * 128:(fi + 1) * 128], idf[:]))
                ser.op("vector", lambda e, fi=fi: e.tensor_copy(xnT[:, fi * 128:(fi + 1) * 128], ptrf))
            for fi in range(32):
                ser.op("tensor", lambda e, fi=fi: e.matmul(
                    plg, xnT[:, fi * 128:(fi + 1) * 128], wsptb[:, fi * 24:(fi + 1) * 24],
                    start=(fi == 0), stop=(fi == 31), skip_group_check=True))
            ser.op("vector", lambda e: e.tensor_add(lg24[:], plg, bvbc[:]))
            for gi, off in ((0, 16), (1, 20)):
                ser.op("vector", lambda e, off=off: e.reduce_max(c8, lg24[:, off:off + 4], AX))
                ser.op("vector", lambda e: e.tensor_scalar(c9, c8, -1.0, None, OP.mult))
                hv = hpp[:, tt * 8 + gi * 4: tt * 8 + gi * 4 + 4]
                ser.op("scalar", lambda e, off=off, hv=hv: e.activation(
                    hv, lg24[:, off:off + 4], AF.Exp, bias=c9))
                ser.op("vector", lambda e, hv=hv: e.reduce_sum(c10, hv, AX))
                ser.op("vector", lambda e: e.reciprocal(c11, c10))
                ser.op("vector", lambda e, hv=hv: e.tensor_scalar(hv, hv, c11, None, OP.mult))
            ser.op("vector", lambda e, tt=tt: e.tensor_scalar(
                z16all[:, tt * 16:(tt + 1) * 16], lg24[:, 0:16], 10.0, None, OP.mult))
        ser.op("scalar", lambda e: e.activation(gg[:], gg[:], AF.Sigmoid))
        ser.op("tensor", lambda e: e.matmul(pg128, onesr[:], gg[:],
                                            start=True, stop=True, skip_group_check=True))
        ser.op("vector", lambda e: e.tensor_copy(gcol[:], pg128))
        ser.op("vector", lambda e: e.tensor_scalar(gm1[:], gcol[:], -1.0, 1.0, OP.mult, OP.add))
        for tt in range(NT):
            Z = z16all[:, tt * 16:(tt + 1) * 16]
            Zv = Z.rearrange("p (i j) -> p i j", i=4)
            t16v = t16[:].rearrange("p (i j) -> p i j", i=4)
            e16v = e16[:].rearrange("p (i j) -> p i j", i=4)
            ser.op("vector", lambda e, Z=Z: e.tensor_copy(t16[:], Z))
            ser.op("vector", lambda e: e.memset(vneg[:], 0.0))
            for it in range(NSK):
                for j in range(4):
                    ser.op("vector", lambda e, Z=Zv, j=j: e.tensor_scalar(
                        t16v[:, :, j], Z[:, :, j], vneg[:, j:j + 1], None, OP.subtract))
                ser.op("vector", lambda e: e.reduce_max(mx4[:], t16v, AX))
                for i in range(4):
                    ser.op("vector", lambda e, i=i: e.tensor_scalar(
                        t16[:, i * 4:(i + 1) * 4], t16[:, i * 4:(i + 1) * 4],
                        mx4[:, i:i + 1], None, OP.subtract))
                ser.op("scalar", lambda e: e.activation(e16[:], t16[:], AF.Exp))
                ser.op("vector", lambda e: e.reduce_sum(s4[:], e16v, AX))
                ser.op("scalar", lambda e: e.activation(ln4[:], s4[:], AF.Ln))
                ser.op("vector", lambda e: e.tensor_add(uneg[:], mx4[:], ln4[:]))
                for i in range(4):
                    ser.op("vector", lambda e, Z=Z, i=i: e.tensor_scalar(
                        t16[:, i * 4:(i + 1) * 4], Z[:, i * 4:(i + 1) * 4],
                        uneg[:, i:i + 1], None, OP.subtract))
                tjv = t16[:].rearrange("p (i j) -> p j i", i=4)
                ejv = e16[:].rearrange("p (i j) -> p j i", i=4)
                ser.op("vector", lambda e, tjv=tjv: e.reduce_max(mx4[:], tjv, AX))
                for j in range(4):
                    ser.op("vector", lambda e, j=j: e.tensor_scalar(
                        t16v[:, :, j], t16v[:, :, j], mx4[:, j:j + 1], None, OP.subtract))
                ser.op("scalar", lambda e: e.activation(e16[:], t16[:], AF.Exp))
                ser.op("vector", lambda e, ejv=ejv: e.reduce_sum(s4[:], ejv, AX))
                ser.op("scalar", lambda e: e.activation(ln4[:], s4[:], AF.Ln))
                ser.op("vector", lambda e: e.tensor_add(vneg[:], mx4[:], ln4[:]))
            for i in range(4):
                ser.op("vector", lambda e, Z=Z, i=i: e.tensor_scalar(
                    t16[:, i * 4:(i + 1) * 4], Z[:, i * 4:(i + 1) * 4],
                    uneg[:, i:i + 1], None, OP.subtract))
            for j in range(4):
                ser.op("vector", lambda e, j=j: e.tensor_scalar(
                    t16v[:, :, j], t16v[:, :, j], vneg[:, j:j + 1], None, OP.subtract))
            ser.op("scalar", lambda e: e.activation(e16[:], t16[:], AF.Exp))
            hv = hrt[:, tt * 16:(tt + 1) * 16]
            ser.op("vector", lambda e, hv=hv: e.tensor_scalar(hv, e16[:], gcol[:, 0:1], None, OP.mult))
            for ii in range(4):
                col = tt * 16 + ii * 5
                ser.op("vector", lambda e, col=col: e.tensor_scalar(
                    hrt[:, col:col + 1], hrt[:, col:col + 1], gm1[:, 0:1], None, OP.add))
        for tt in range(NT):
            ser.dma(lambda e, tt=tt: e.dma_start(out=sx3[:], in_=x3_scr[tt * 128:(tt + 1) * 128, :]))
            _ln_ops(ser, sx3[:], sxn[:], sq[:], cols, D, ceps[:, 0:1])
            ser.op("vector", lambda e, tt=tt: e.tensor_scalar(
                xp[:], sxn[:, 0:1024], hpp[:, tt * 8: tt * 8 + 1], None, OP.mult))
            for j in range(1, 4):
                ser.op("vector", lambda e, tt=tt, j=j: e.scalar_tensor_tensor(
                    xp[:], sxn[:, j * 1024:(j + 1) * 1024], hpp[:, tt * 8 + j: tt * 8 + j + 1],
                    xp[:], OP.mult, OP.add))
            for ci in range(8):
                ser.op("tensor", lambda e, ci=ci: e.transpose(ptrf, xp[:, ci * 128:(ci + 1) * 128], idf[:]))
                ser.op("vector", lambda e, ci=ci: e.tensor_copy(xnT[:, ci * 128:(ci + 1) * 128], ptrf))
            for nh in range(2):
                pv = pf[nh]
                for ci in range(8):
                    ser.op("tensor", lambda e, ci=ci, nh=nh, pv=pv: e.matmul(
                        pv[:], xnT[:, ci * 128:(ci + 1) * 128],
                        wbb[:, ci * 1024 + nh * 512: ci * 1024 + (nh + 1) * 512],
                        start=(ci == 0), stop=(ci == 7), skip_group_check=True))
                ser.op("vector", lambda e, nh=nh, pv=pv: e.tensor_copy(y_sb[:, nh * 512:(nh + 1) * 512], pv[:]))
            for i in range(4):
                ser.op("vector", lambda e, tt=tt, i=i: e.tensor_scalar(
                    to[:], sxn[:, 0:1024], hrt[:, tt * 16 + i * 4: tt * 16 + i * 4 + 1], None, OP.mult))
                for j in range(1, 4):
                    ser.op("vector", lambda e, tt=tt, i=i, j=j: e.scalar_tensor_tensor(
                        to[:], sxn[:, j * 1024:(j + 1) * 1024],
                        hrt[:, tt * 16 + i * 4 + j: tt * 16 + i * 4 + j + 1],
                        to[:], OP.mult, OP.add))
                ser.op("vector", lambda e, tt=tt, i=i: e.scalar_tensor_tensor(
                    to[:], y_sb[:], hpp[:, tt * 8 + 4 + i: tt * 8 + 4 + i + 1],
                    to[:], OP.mult, OP.add))
                ser.op("vector", lambda e, i=i: e.tensor_add(to[:], to[:], sx3[:, i * 1024:(i + 1) * 1024]))
                ser.dma(lambda e, tt=tt, i=i: e.dma_start(
                    out=out_out[tt * 128:(tt + 1) * 128, i * 1024:(i + 1) * 1024], in_=to[:]))
        ser.finish(block, chain, dchain)
    return nc


def run_p6(xnf, gfm, x2, w_output, W_res, W_pre, W_post, H_res_base, H_pre_base,
           H_post_base, gate, W_branch):
    nc = build_p6()
    idf = np.eye(128, dtype=np.float32)
    wot = np.ascontiguousarray(w_output.T.reshape(32, 128, 8, 512).transpose(2, 1, 0, 3))
    wspt = np.ascontiguousarray(np.concatenate([W_res, W_pre, W_post], axis=0).T)
    bvec = np.concatenate([H_res_base.reshape(-1), H_pre_base, H_post_base])[None, :].astype(np.float32)
    in_maps = []
    for c in range(NCORES):
        t0 = c * TC
        in_maps.append({
            "xnf": np.ascontiguousarray(xnf[:, t0:t0 + TC]),
            "gf": np.ascontiguousarray(gfm[:, t0:t0 + TC]),
            "x2": np.ascontiguousarray(x2[t0:t0 + TC]),
            "wot": wot, "wspt": wspt, "bvec": bvec,
            "gate": gate.reshape(1, 1).astype(np.float32),
            "wb": W_branch, "idf": idf,
        })
    res = run_bass_kernel_spmd(nc, in_maps, core_ids=list(range(NCORES)))
    out = np.concatenate([res.results[c]["out"] for c in range(NCORES)], axis=0)
    return out


def kernel(L0_attn_hidden_states, L0_attn_Wq, L0_attn_Wkv, L0_attn_Wo,
           L0_ffn_router_w, L0_ffn_A, L0_ffn_B,
           L1_attn_w_key, L1_attn_w_value, L1_attn_w_receptance, L1_attn_w_gate,
           L1_attn_w_output, L1_attn_time_mix_key, L1_attn_time_mix_value,
           L1_attn_time_mix_receptance, L1_attn_time_mix_gate, L1_attn_time_decay,
           L1_attn_time_faaaa, L1_attn_gn_weight, L1_attn_gn_bias,
           L1_ffn_W_res, L1_ffn_W_pre, L1_ffn_W_post, L1_ffn_H_res_base,
           L1_ffn_H_pre_base, L1_ffn_H_post_base, L1_ffn_gate, L1_ffn_W_branch):
    x = np.ascontiguousarray(np.asarray(L0_attn_hidden_states, dtype=np.float32).reshape(T, D))
    qfm, kv = run_p1(x, np.asarray(L0_attn_Wq), np.asarray(L0_attn_Wkv))
    ao = run_p2(qfm, kv)
    x2, xl2 = run_p3(ao, x, np.asarray(L0_attn_Wo), np.asarray(L0_ffn_router_w),
                     np.asarray(L0_ffn_A), np.asarray(L0_ffn_B))
    kfm, vfm, rfm, gfm = run_p4(xl2, np.asarray(L1_attn_w_key), np.asarray(L1_attn_w_value),
                                np.asarray(L1_attn_w_receptance), np.asarray(L1_attn_w_gate),
                                np.asarray(L1_attn_time_mix_key), np.asarray(L1_attn_time_mix_value),
                                np.asarray(L1_attn_time_mix_receptance), np.asarray(L1_attn_time_mix_gate))
    xnf = run_p5(kfm, vfm, rfm, np.asarray(L1_attn_time_decay), np.asarray(L1_attn_time_faaaa),
                 np.asarray(L1_attn_gn_weight), np.asarray(L1_attn_gn_bias))
    out = run_p6(xnf, gfm, x2, np.asarray(L1_attn_w_output), np.asarray(L1_ffn_W_res),
                 np.asarray(L1_ffn_W_pre), np.asarray(L1_ffn_W_post),
                 np.asarray(L1_ffn_H_res_base), np.asarray(L1_ffn_H_pre_base),
                 np.asarray(L1_ffn_H_post_base), np.asarray(L1_ffn_gate),
                 np.asarray(L1_ffn_W_branch))
    return out.reshape(B, S, D)
